# version 01
# baseline (speedup 1.0000x reference)
import math
from contextlib import ExitStack

import numpy as np
import ml_dtypes

import concourse.bass as bass
import concourse.bacc as bacc
import concourse.mybir as mybir

F32 = mybir.dt.float32
BF16 = mybir.dt.bfloat16
I32 = mybir.dt.int32
AF = mybir.ActivationFunctionType
ALU = mybir.AluOpType
AX = mybir.AxisListType
DS = bass.ds

N, D, NH, DH, M = 8192, 1024, 8, 64, 256
MP_ITERS, CK = 6, 13
SCL = DH ** -0.5
NT = 4096
FD, NEXP, EPS = 2048, 8, 1e-5
BF = ml_dtypes.bfloat16


class Sems:
    def __init__(self, nc, st, names):
        self.h = {nm: st.enter_context(nc.semaphore(nm)) for nm in names}
        self.n = {nm: 0 for nm in names}

    def inc(self, instr, nm, k=1):
        instr.then_inc(self.h[nm], k)
        self.n[nm] += k

    def w(self, eng, nm, v):
        eng.wait_ge(self.h[nm], v)


SEM_NAMES = ["sx", "cc",
             "cvw", "cvx", "cvl", "cvq", "mmq",
             "mm1", "ca1", "cve", "sgn", "sgo",
             "mm2", "ca2", "mm3", "ca3", "cv3",
             "mm4", "cv4", "mm5", "cv5", "mm6", "cv6",
             "mm7", "cv7",
             "mm8", "cv8", "ca8", "sx8",
             "cv9", "mm9", "ca9", "sw9",
             "swq0", "swq1", "swl0", "swl1", "sxx0", "sxx1", "scv",
             "sxv", "sl3", "sst0", "sst1", "sci",
             "sfc0", "sfc1", "sr1", "shf",
             "rwl0", "rwl1", "scw0", "scw1", "sxw", "shh0", "shh1", "sgr",
             "gda", "gdb", "sac0", "sac1", "sy0", "sy1"]


def build(n_pass=4, do_ffn=True, dumps=(), dumpreq=()):
    nc = bacc.Bacc("TRN2", target_bir_lowering=False)
    st = ExitStack()

    S = {"nc": nc, "st": st}
    S["xT"] = nc.dram_tensor("xT", [8, 128, N], F32, kind="ExternalInput").ap()
    S["x_own"] = nc.dram_tensor("x_own", [NT // 128, 128, D], F32, kind="ExternalInput").ap()
    S["wqkv"] = nc.dram_tensor("wqkv", [8, 128, 3 * NH * DH], F32, kind="ExternalInput").ap()
    S["wfc"] = nc.dram_tensor("wfc", [8, 128, D], F32, kind="ExternalInput").ap()
    S["bfc_rep"] = nc.dram_tensor("bfc_rep", [128, D], F32, kind="ExternalInput").ap()
    S["lng_rep"] = nc.dram_tensor("lng_rep", [128, D], F32, kind="ExternalInput").ap()
    S["lnb_rep"] = nc.dram_tensor("lnb_rep", [128, D], F32, kind="ExternalInput").ap()
    S["convd"] = nc.dram_tensor("convd", [4, CK, 128, 128], F32, kind="ExternalInput").ap()
    S["identc"] = nc.dram_tensor("identc", [128, 6 * M], BF16, kind="ExternalInput").ap()
    S["idbf"] = nc.dram_tensor("idbf", [128, 128], BF16, kind="ExternalInput").ap()
    S["idf32"] = nc.dram_tensor("idf32", [128, 128], F32, kind="ExternalInput").ap()
    S["onesbf"] = nc.dram_tensor("onesbf", [128, 128], BF16, kind="ExternalInput").ap()
    S["onesf"] = nc.dram_tensor("onesf", [128, 128], F32, kind="ExternalInput").ap()
    S["meta"] = nc.dram_tensor("meta", [1, 8], I32, kind="ExternalInput").ap()
    S["rwT"] = nc.dram_tensor("rwT", [8, 128, NEXP], F32, kind="ExternalInput").ap()
    S["sw1"] = nc.dram_tensor("sw1", [FD // 128, 128, D], F32, kind="ExternalInput").ap()
    S["sw2"] = nc.dram_tensor("sw2", [D // 128, 128, FD], F32, kind="ExternalInput").ap()
    S["ew1"] = nc.dram_tensor("ew1", [NEXP, FD // 128, 128, D], F32, kind="ExternalInput").ap()
    S["ew2"] = nc.dram_tensor("ew2", [NEXP, D // 128, 128, FD], F32, kind="ExternalInput").ap()
    S["y"] = nc.dram_tensor("y", [NT // 128, 128, D], F32, kind="ExternalOutput").ap()

    S["outall"] = nc.dram_tensor("outall", [4, 128, N], BF16, kind="Internal").ap()
    S["cin"] = nc.dram_tensor("cin", [4, 128, NT], BF16, kind="Internal").ap()
    S["cout"] = nc.dram_tensor("cout", [2, 4, 128, NT], BF16, kind="Internal").ap()
    S["gin"] = nc.dram_tensor("gin", [1, 2], F32, kind="Internal").ap()
    S["gout"] = nc.dram_tensor("gout", [1, 2], F32, kind="Internal").ap()
    S["r1n_hbm"] = nc.dram_tensor("r1n_hbm", [NT // 128, 128, D], F32, kind="Internal").ap()
    S["h_hbm"] = nc.dram_tensor("h_hbm", [8, 128, NT], BF16, kind="Internal").ap()
    S["w1scr"] = nc.dram_tensor("w1scr", [2, 8, FD, 128], BF16, kind="Internal").ap()
    S["w2scr"] = nc.dram_tensor("w2scr", [2, 16, D, 128], BF16, kind="Internal").ap()
    S["acc_hbm"] = nc.dram_tensor("acc_hbm", [8, 128, NT], F32, kind="Internal").ap()
    S["grows_hbm"] = nc.dram_tensor("grows_hbm", [8, NT], F32, kind="Internal").ap()

    S["dump"] = {}
    S["dumpreq"] = dumpreq

    sb = lambda *a, **k: st.enter_context(nc.sbuf_tensor(*a, **k))
    S["mt"] = sb("mt", [1, 8], I32)
    S["id_bf"] = sb("id_bf", [128, 128], BF16)
    S["id_f32"] = sb("id_f32", [128, 128], F32)
    S["ones_bf"] = sb("ones_bf", [128, 128], BF16)
    S["ones_f32"] = sb("ones_f32", [128, 128], F32)
    S["identc_sb"] = sb("identc_sb", [128, 6 * M], BF16)
    ast = ExitStack()
    S["ql_all"] = ast.enter_context(nc.sbuf_tensor("ql_all", [128, NH * M], BF16, side="right"))
    S["kl_all"] = ast.enter_context(nc.sbuf_tensor("kl_all", [128, NH * M], BF16, side="right"))
    S["a2T_all"] = ast.enter_context(nc.sbuf_tensor("a2T_all", [128, NH * 2 * M], BF16, side="right"))
    S["junk"] = sb("junk", [1, 8], BF16)
    S["statv"] = sb("statv", [128, 8], F32)
    S["gsc"] = sb("gsc", [1, 8], F32)
    S["rden_rep"] = sb("rden_rep", [128, 1], F32)

    sems = Sems(nc, st, SEM_NAMES)
    S["sems"] = sems

    with nc.Block() as b:
        @b.sync
        def _(e):
            for dst, src in ((S["mt"], S["meta"]), (S["id_bf"], S["idbf"]),
                             (S["id_f32"], S["idf32"]), (S["ones_bf"], S["onesbf"]),
                             (S["ones_f32"], S["onesf"]),
                             (S["identc_sb"], S["identc"])):
                e.dma_start(out=dst[:], in_=src[:, :]).then_inc(sems.h["sx"], 16)
            sems.n["sx"] += 96

        @b.vector
        def _(e):
            sems.w(e, "sx", sems.n["sx"])
            e.memset(S["junk"][0:1, 0:4], 1.0)
            e.memset(S["junk"][0:1, 4:8], 0.0)
            e.memset(S["statv"][:], 0.0)
            e.memset(S["gsc"][0:1, 0:4], 1.0)
            e.memset(S["gsc"][0:1, 4:8], 0.0)

    S["v_oth"] = nc.values_load(S["mt"][0:1, 0:1], engines=[mybir.EngineType.SP, mybir.EngineType.Activation])
    S["v_slot"] = nc.values_load(S["mt"][0:1, 1:2], engines=[mybir.EngineType.SP])
    S["v_own"] = nc.values_load(S["mt"][0:1, 2:3], engines=[mybir.EngineType.SP])

    for p in range(n_pass):
        emit_pass(S, p)
    ast.close()
    emit_exchange(S)
    emit_fc_ln(S)
    if do_ffn:
        emit_ffn(S)
        emit_tail(S)
    else:
        emit_dummy_tail(S)

    if dumps:
        dt_ = {}
        for key in dumps:
            t = S[key].tensor
            dt_[key] = nc.dram_tensor("d_" + key, list(t.shape), t.dtype,
                                      kind="ExternalOutput").ap()
        with nc.Block() as b:
            @b.scalar
            def _(e):
                for key in dumps:
                    e.dma_start(out=dt_[key][...], in_=S[key][...]).then_inc(sems.h["scv"], 16)
                    sems.n["scv"] += 16
                sems.w(e, "scv", sems.n["scv"])

    st.close()
    return nc


def emit_pass(S, p):
    nc, sems = S["nc"], S["sems"]
    with ExitStack() as pst:
        sb = lambda *a, **k: pst.enter_context(nc.sbuf_tensor(*a, **k))
        ps_ = lambda *a, **k: pst.enter_context(nc.psum_tensor(*a, **k))
        qkv = [sb(f"qkv{p}_{i}", [128, N], BF16) for i in range(3)]
        cvb = sb(f"cvb{p}", [128, CK * 128], BF16)
        outp = sb(f"outp{p}", [128, N + 2 * (CK // 2)], BF16)
        ostg = [sb(f"ostg{p}_{i}", [128, 2048], BF16) for i in range(2)]
        pA = [ps_(f"pA{p}_{i}", [128, 512], F32) for i in range(8)]
        if p == 0:
            xlb = [sb(f"xlb_{i}", [128, M], BF16) for i in range(8)]
        inner = ExitStack()
        sbi = lambda *a, **k: inner.enter_context(nc.sbuf_tensor(*a, **k))
        xw = [sbi(f"xw{p}_{i}", [128, 2048], F32) for i in range(2)]
        xbg = [sbi(f"xbg{p}_{i}", [128, 2048], BF16) for i in range(8)]
        wqf = [sbi(f"wqf{p}_{i}", [128, 384], F32) for i in range(2)]
        wqb = [sbi(f"wqb{p}_{i}", [128, 384], BF16) for i in range(8)]
        cvf = sbi(f"cvf{p}", [128, CK * 128], F32)
        if p == 0:
            xl = [sbi(f"xl_{i}", [128, M], F32) for i in range(8)]
            wlf = [sbi(f"wlf_{i}", [128, 1024], F32) for i in range(2)]
            wlb = [sbi(f"wlb_{i}", [128, 1024], BF16) for i in range(8)]

        qc, kc, vc = 2 * p * DH, NH * DH + 2 * p * DH, 2 * NH * DH + 2 * p * DH
        cw0, cx0, cl0 = sems.n["cvw"], sems.n["cvx"], sems.n["cvl"]
        mq0, cq0 = sems.n["mmq"], sems.n["cvq"]
        q0 = [sems.n["swq0"], sems.n["swq1"]]
        x0 = [sems.n["sxx0"], sems.n["sxx1"]]
        v0s = sems.n["scv"]
        nwl = 8 if p == 0 else 0
        nwq_cast = 8 + nwl + 1

        with nc.Block() as b:
            @b.sync
            def _(e):
                for dc in range(8):
                    if dc >= 2:
                        sems.w(e, "cvw", cw0 + dc - 1)
                    for ci, c0 in enumerate((qc, kc, vc)):
                        e.dma_start(out=wqf[dc % 2][:, ci * 128:(ci + 1) * 128],
                                    in_=S["wqkv"][dc][:, c0:c0 + 128]).then_inc(sems.h[f"swq{dc % 2}"], 16)
                if p == 0:
                    for dc in range(8):
                        if dc >= 2:
                            sems.w(e, "cvw", cw0 + 8 + dc - 1)
                        e.dma_start(out=wlf[dc % 2][:],
                                    in_=S["wqkv"][dc][:, 0:1024]).then_inc(sems.h[f"swl{dc % 2}"], 16)
                for jj in range(CK):
                    e.dma_start(out=cvf[:, jj * 128:(jj + 1) * 128],
                                in_=S["convd"][p, jj]).then_inc(sems.h["scv"], 16)
                for g in range(4):
                    for dc in range(8):
                        i = g * 8 + dc
                        if i >= 2:
                            sems.w(e, "cvx", cx0 + i - 1)
                            if p == 0:
                                sems.w(e, "cvl", cl0 + i - 1)
                        e.dma_start(out=xw[i % 2][:],
                                    in_=S["xT"][dc][:, g * 2048:(g + 1) * 2048]).then_inc(sems.h[f"sxx{i % 2}"], 16)
                sems.n["swq0"] = q0[0] + 16 * 12
                sems.n["swq1"] = q0[1] + 16 * 12
                if p == 0:
                    sems.n["swl0"] += 16 * 4
                    sems.n["swl1"] += 16 * 4
                sems.n["scv"] = v0s + 16 * CK
                sems.n["sxx0"] = x0[0] + 16 * 16
                sems.n["sxx1"] = x0[1] + 16 * 16

            @b.vector
            def _(e):
                for dc in range(8):
                    sems.w(e, f"swq{dc % 2}", q0[dc % 2] + 16 * 3 * (dc // 2 + 1))
                    e.tensor_copy(wqb[dc][:], wqf[dc % 2][:]).then_inc(sems.h["cvw"], 1)
                if p == 0:
                    for dc in range(8):
                        sems.w(e, f"swl{dc % 2}", sems.n[f"swl{dc % 2}"] - 16 * 4 + 16 * (dc // 2 + 1))
                        e.tensor_copy(wlb[dc][:], wlf[dc % 2][:]).then_inc(sems.h["cvw"], 1)
                sems.w(e, "scv", v0s + 16 * CK)
                e.tensor_copy(cvb[:], cvf[:]).then_inc(sems.h["cvw"], 1)
                sems.n["cvw"] = cw0 + nwq_cast
                for g in range(4):
                    for dc in range(8):
                        i = g * 8 + dc
                        sems.w(e, f"sxx{i % 2}", x0[i % 2] + 16 * (i // 2 + 1))
                        if g > 0:
                            sems.w(e, "mmq", mq0 + 3 * g)
                        e.tensor_copy(xbg[dc][:], xw[i % 2][:]).then_inc(sems.h["cvx"], 1)
                        if p == 0:
                            e.reduce_sum(xl[dc][:, g * 64:(g + 1) * 64],
                                         xw[i % 2][:].rearrange("q (a w) -> q a w", w=32),
                                         axis=AX.X).then_inc(sems.h["cvl"], 1)
                    for cc in range(3):
                        k = g * 3 + cc
                        sems.w(e, "mmq", mq0 + k + 1)
                        for wi in range(4):
                            bank = (k % 2) * 4 + wi
                            e.tensor_copy(qkv[cc][:, g * 2048 + wi * 512:g * 2048 + (wi + 1) * 512],
                                          pA[bank][:])
                        e.drain().then_inc(sems.h["cvq"], 1)
                sems.n["cvx"] = cx0 + 32
                sems.n["cvq"] = cq0 + 12
                if p == 0:
                    sems.n["cvl"] = cl0 + 32

            @b.tensor
            def _(e):
                sems.w(e, "cvw", cw0 + 8)
                for g in range(4):
                    sems.w(e, "cvx", cx0 + (g + 1) * 8)
                    for cc in range(3):
                        k = g * 3 + cc
                        if k >= 2:
                            sems.w(e, "cvq", cq0 + k - 1)
                        for dc in range(8):
                            e.ldweights(wqb[dc][:, cc * 128:(cc + 1) * 128])
                            for wi in range(4):
                                bank = (k % 2) * 4 + wi
                                e.matmul(pA[bank][:], wqb[dc][:, cc * 128:(cc + 1) * 128],
                                         xbg[dc][:, wi * 512:(wi + 1) * 512],
                                         start=(dc == 0), stop=(dc == 7))
                        e.drain().then_inc(sems.h["mmq"], 1)
                sems.n["mmq"] = mq0 + 12

        if p == 0:
            for ci in range(3):
                dump_sb(S, f"qkv{p}_{ci}", qkv[ci][:])
            emit_attn2prep(S, xl, xlb, wlb, pA)
            dump_sb(S, "ql", S["ql_all"][:])
            dump_sb(S, "kl", S["kl_all"][:])
            dump_sb(S, "a2T", S["a2T_all"][:])
        inner.close()

        for hl in range(2):
            emit_head(S, p, hl, qkv, outp, pA)
        dump_sb(S, f"outp{p}", outp[:])
        emit_conv_store(S, p, outp, cvb, ostg, pA)


def emit_attn2prep(S, xl, xlb, wlb, pA):
    nc, sems = S["nc"], S["sems"]
    ql, kl, a2T = S["ql_all"], S["kl_all"], S["a2T_all"]
    ones = S["ones_bf"]
    with ExitStack() as pst:
        sb = lambda *a, **k: pst.enter_context(nc.sbuf_tensor(*a, **k))
        e2b = sb("e2b", [128, 2 * M], BF16)
        rl2 = sb("rl2", [128, M], F32)
        rsum = sb("rsum", [128, 1], F32)
        m0, a0, v0, x0 = sems.n["mm1"], sems.n["ca1"], sems.n["cve"], sems.n["cvx"]

        with nc.Block() as b:
            @b.vector
            def _(e):
                e.memset(ql[:], 0.0)
                e.memset(kl[:], 0.0)
                for dc in range(8):
                    e.tensor_scalar_mul(xlb[dc][:], xl[dc][:], 1.0 / 32.0).then_inc(sems.h["cvx"], 1)
                for h in range(NH):
                    ob = (h % 2) * 64
                    sems.w(e, "mm1", m0 + 6 * h + 2)
                    e.tensor_copy(ql[ob:ob + 64, h * M:(h + 1) * M], pA[0][ob:ob + 64, 0:M])
                    e.tensor_copy(kl[ob:ob + 64, h * M:(h + 1) * M],
                                  pA[1][ob:ob + 64, 0:M]).then_inc(sems.h["cve"], 1)
                    sems.w(e, "mm1", m0 + 6 * h + 5)
                    e.reciprocal(rl2[:], pA[4][:, 0:M])
                    e.drain()
                    e.tensor_mul(a2T[:, h * 2 * M:h * 2 * M + M], e2b[:, 0:M], rl2[:])
                    e.tensor_mul(a2T[:, h * 2 * M + M:(h + 1) * 2 * M], e2b[:, M:2 * M], rl2[:]).then_inc(sems.h["cve"], 1)
                    e.drain()
                    e.reduce_sum(rsum[:], a2T[:, h * 2 * M:h * 2 * M + M], axis=AX.X)
                    e.reduce_sum(S["statv"][:, 1:2], a2T[:, h * 2 * M + M:(h + 1) * 2 * M], axis=AX.X)
                    e.drain()
                    e.tensor_max(rsum[:], rsum[:], S["statv"][:, 1:2])
                    e.drain()
                    if h == 0:
                        e.tensor_copy(S["statv"][:, 0:1], rsum[:])
                    else:
                        e.tensor_max(S["statv"][:, 0:1], S["statv"][:, 0:1], rsum[:])
                    sems.w(e, "mm1", m0 + 6 * h + 6)
                    e.reduce_max(S["gsc"][0:1, 0:1] if h == 0 else S["gsc"][0:1, 2:3],
                                 pA[5][0:1, 0:M], axis=AX.X)
                    e.drain()
                    if h > 0:
                        e.tensor_max(S["gsc"][0:1, 0:1], S["gsc"][0:1, 0:1], S["gsc"][0:1, 2:3])
                    e.drain().then_inc(sems.h["cve"], 1)
                sems.w(e, "mm1", m0 + 6 * NH + 1)
                e.reduce_max(S["gsc"][0:1, 1:2], pA[6][0:1, 0:128], axis=AX.X).then_inc(sems.h["cve"], 1)
                sems.n["cvx"] = x0 + 8
                sems.n["cve"] = v0 + 3 * NH + 1

            @b.tensor
            def _(e):
                sems.w(e, "cvx", x0 + 8)
                for h in range(NH):
                    ob = (h % 2) * 64
                    if h > 0:
                        sems.w(e, "cve", v0 + 3 * (h - 1) + 3)
                    for side, dst in ((0, 0), (1, 1)):
                        for dc in range(8):
                            wslice = wlb[dc][:, side * 512 + h * DH:side * 512 + (h + 1) * DH]
                            e.ldweights(wslice)
                            e.matmul(pA[dst][ob:ob + 64, 0:M], wslice, xlb[dc][:],
                                     start=(dc == 0), stop=(dc == 7))
                        e.drain().then_inc(sems.h["mm1"], 1)
                    sems.w(e, "cve", v0 + 3 * h + 1)
                    for mc in range(2):
                        lw = kl[:, h * M + mc * 128:h * M + (mc + 1) * 128]
                        e.ldweights(lw)
                        e.matmul(pA[2 + mc][:, 0:M], lw, ql[:, h * M:(h + 1) * M],
                                 start=True, stop=True).then_inc(sems.h["mm1"], 1)
                    sems.w(e, "ca1", a0 + 2 * (h + 1))
                    e.ldweights(ones[:, :])
                    e.matmul(pA[4][:, 0:M], ones[:, :], e2b[:, 0:M], start=True, stop=False)
                    e.matmul(pA[4][:, 0:M], ones[:, :], e2b[:, M:2 * M],
                             start=False, stop=True).then_inc(sems.h["mm1"], 1)
                    sems.w(e, "cve", v0 + 3 * h + 2)
                    e.ldweights(ones[:, 0:64])
                    e.matmul(pA[5][0:64, 0:M], ones[:, 0:64], a2T[:, h * 2 * M:h * 2 * M + M],
                             start=True, stop=False)
                    e.matmul(pA[5][0:64, 0:M], ones[:, 0:64],
                             a2T[:, h * 2 * M + M:(h + 1) * 2 * M],
                             start=False, stop=True).then_inc(sems.h["mm1"], 1)
                sems.w(e, "cve", v0 + 3 * NH)
                e.transpose(pA[6][0:1, 0:128], S["statv"][:, 0:1],
                            S["id_f32"][:, :]).then_inc(sems.h["mm1"], 1)
                sems.n["mm1"] = m0 + 6 * NH + 1

            @b.scalar
            def _(e):
                for h in range(NH):
                    for mc in range(2):
                        sems.w(e, "mm1", m0 + 6 * h + 3 + mc)
                        if h > 0 and mc == 0:
                            sems.w(e, "cve", v0 + 3 * (h - 1) + 2)
                        e.activation(e2b[:, mc * M:(mc + 1) * M], pA[2 + mc][:, 0:M],
                                     AF.Exp, scale=SCL).then_inc(sems.h["ca1"], 1)
                sems.w(e, "cve", v0 + 3 * NH + 1)
                e.dma_start(out=S["gin"][:, :], in_=S["gsc"][0:1, 0:2]).then_inc(sems.h["sgn"], 16)
                sems.n["ca1"] = a0 + 2 * NH
                sems.n["sgn"] += 16

        with nc.Block() as b:
            @b.gpsimd
            def _(e):
                sems.w(e, "sgn", sems.n["sgn"])
                e.collective_compute("AllReduce", ALU.max,
                                     ins=[S["gin"][:, :]], outs=[S["gout"][:, :]],
                                     replica_groups=[[0, 1, 2, 3, 4, 5, 6, 7]]).then_inc(sems.h["cc"], 1)
                sems.n["cc"] += 1
                sems.w(e, "cc", sems.n["cc"])
                e.dma_start(out=S["gsc"][0:1, 4:6], in_=S["gout"][:, :]).then_inc(sems.h["sgo"], 16)
                sems.n["sgo"] += 16

        with nc.Block() as b:
            m1 = sems.n["mm1"]

            @b.vector
            def _(e):
                sems.w(e, "sgo", sems.n["sgo"])
                e.tensor_mul(S["gsc"][0:1, 6:7], S["gsc"][0:1, 4:5], S["gsc"][0:1, 5:6])
                e.drain()
                e.reciprocal(S["gsc"][0:1, 7:8], S["gsc"][0:1, 6:7]).then_inc(sems.h["cve"], 1)
                sems.w(e, "mm1", m1 + 1)
                e.tensor_copy(S["rden_rep"][:], pA[7][:, 0:1])
                sems.n["cve"] += 1

            @b.tensor
            def _(e):
                sems.w(e, "cve", sems.n["cve"])
                e.matmul(pA[7][:, 0:1], S["ones_f32"][0:1, :], S["gsc"][0:1, 7:8],
                         start=True, stop=True).then_inc(sems.h["mm1"], 1)
                sems.n["mm1"] = m1 + 1


def emit_head(S, p, hl, qkv, outp, pA):
    nc, sems = S["nc"], S["sems"]
    h = 2 * p + hl
    r0, r1 = hl * 64, hl * 64 + 64
    ql, kl, a2T = S["ql_all"], S["kl_all"], S["a2T_all"]
    ones, idc = S["ones_bf"], S["identc_sb"]

    with ExitStack() as hst:
        sb = lambda *a, **k: hst.enter_context(nc.sbuf_tensor(*a, **k))
        vaug = sb(f"vaug{h}", [128, 128 * 64], BF16, is_transpose_dest=True)
        exp1 = [sb(f"exp1_{h}_{i}", [128, N], BF16) for i in range(2)]
        e3b = [sb(f"e3b{h}_{i}", [128, M], BF16) for i in range(2)]
        t1f = sb(f"t1f{h}", [65, M], F32)
        t1id = [sb(f"t1id{h}_{i}", [128, 65], BF16) for i in range(2)]
        l3r = sb(f"l3r{h}", [1, M], F32)
        rl3r = sb(f"rl3r{h}", [1, M], F32)
        rl3pp = sb(f"rl3pp{h}", [128, 2], F32)
        a2f = sb(f"a2f{h}", [128, 512], F32)
        a2b = sb(f"a2b{h}", [128, 512], BF16)
        zb = sb(f"zb{h}", [128, 512], BF16)
        ztb = sb(f"ztb{h}", [128, 512], BF16)
        aztb = sb(f"aztb{h}", [128, 512], BF16)
        mpb = sb(f"mpb{h}", [128, 512], BF16)
        zstb = sb(f"zstb{h}", [128, 512], BF16)
        t2b = [sb(f"t2b{h}_{i}", [128, 64], BF16) for i in range(2)]
        rl1 = sb(f"rl1{h}", [128, 512], F32)

        a2s = a2T[:, h * 2 * M:(h + 1) * 2 * M]

        sx0, m20, a20 = sems.n["sxv"], sems.n["mm2"], sems.n["ca2"]
        with nc.Block() as b:
            @b.vector
            def _(e):
                e.memset(vaug[:], 1.0)
                if hl == 0:
                    e.memset(outp[:, 0:CK // 2], 0.0)
                    e.memset(outp[:, CK // 2 + N:CK // 2 + N + CK // 2], 0.0)
                e.drain().then_inc(sems.h["cv3"], 1)

            @b.sync
            def _(e):
                sems.w(e, "cv3", sems.n["cv3"] + 1)
                for c in range(64):
                    e.dma_start_transpose(out=vaug[:, c * 128:c * 128 + 64],
                                          in_=qkv[2][r0:r1, c * 128:(c + 1) * 128]).then_inc(sems.h["sxv"], 16)
                sems.n["sxv"] = sx0 + 64 * 16
            sems.n["cv3"] += 1

            @b.tensor
            def _(e):
                for wg in range(4):
                    for mc in range(2):
                        j = 2 * wg + mc
                        if j >= 2:
                            sems.w(e, "ca2", a20 + j - 1)
                        lw = kl[:, h * M + mc * 128:h * M + (mc + 1) * 128]
                        e.ldweights(lw)
                        for wi in range(4):
                            w = wg * 4 + wi
                            e.matmul(pA[mc * 4 + wi][:], lw,
                                     qkv[0][:, w * 512:(w + 1) * 512],
                                     start=True, stop=True)
                        e.drain().then_inc(sems.h["mm2"], 1)
                sems.n["mm2"] = m20 + 8

            @b.scalar
            def _(e):
                for wg in range(4):
                    for mc in range(2):
                        j = 2 * wg + mc
                        sems.w(e, "mm2", m20 + j + 1)
                        for wi in range(4):
                            w = wg * 4 + wi
                            e.activation(exp1[mc][:, w * 512:(w + 1) * 512],
                                         pA[mc * 4 + wi][:], AF.Exp, scale=SCL)
                        e.drain().then_inc(sems.h["ca2"], 1)
                sems.n["ca2"] = a20 + 8

        dump_sb(S, f"vaug{h}", vaug[:])
        dump_sb(S, f"exp1a{h}", exp1[0][:])
        m30, a30 = sems.n["mm3"], sems.n["ca3"]
        with nc.Block() as b:
            @b.tensor
            def _(e):
                sems.w(e, "sxv", sems.n["sxv"])
                for c in range(64):
                    if c >= 2:
                        sems.w(e, "ca3", a30 + c - 1)
                    lw = qkv[1][:, c * 128:(c + 1) * 128]
                    e.ldweights(lw)
                    e.matmul(pA[c % 2][:, 0:M], lw, ql[:, h * M:(h + 1) * M],
                             start=True, stop=True).then_inc(sems.h["mm3"], 1)
                    sems.w(e, "ca3", a30 + c + 1)
                    e.ldweights(vaug[:, c * 128:c * 128 + 65])
                    e.matmul(pA[2][0:65, 0:M], vaug[:, c * 128:c * 128 + 65], e3b[c % 2][:],
                             start=(c == 0), stop=(c == 63)).then_inc(sems.h["mm3"], 1)
                sems.n["mm3"] = m30 + 128

            @b.scalar
            def _(e):
                for c in range(64):
                    sems.w(e, "mm3", m30 + 2 * c + 1)
                    e.activation(e3b[c % 2][:], pA[c % 2][:, 0:M],
                                 AF.Exp, scale=SCL).then_inc(sems.h["ca3"], 1)
                sems.n["ca3"] = a30 + 64

        m40, v40, w40 = sems.n["mm4"], sems.n["cv4"], sems.n["sl3"]
        with nc.Block() as b:
            @b.vector
            def _(e):
                e.tensor_copy(t1f[:], pA[2][0:65, 0:M]).then_inc(sems.h["cv4"], 1)
                e.tensor_copy(a2f[:], a2s).then_inc(sems.h["cv4"], 1)
                sems.w(e, "sl3", w40 + 16)
                e.reciprocal(rl3r[:], l3r[:]).then_inc(sems.h["cv4"], 1)
                sems.w(e, "mm4", m40 + 2)
                e.tensor_copy(rl3pp[:, 0:1], pA[3][:, 0:1])
                e.tensor_copy(rl3pp[:, 1:2], pA[3][:, 1:2]).then_inc(sems.h["cv4"], 1)
                sems.w(e, "mm4", m40 + 4)
                e.tensor_copy(t1id[0][:], pA[4][:, 0:65])
                e.tensor_copy(t1id[1][:], pA[4][:, 65:130]).then_inc(sems.h["cv4"], 1)
                sems.w(e, "mm4", m40 + 8)
                for pc in range(2):
                    for rc in range(2):
                        e.tensor_copy(a2b[64 * 0:128, 0:1] if False else
                                      a2b[:, pc * 256 + rc * 128:pc * 256 + (rc + 1) * 128],
                                      pA[5 + pc][:, rc * 128:(rc + 1) * 128])
                e.drain().then_inc(sems.h["cv4"], 1)
                sems.n["cv4"] = v40 + 6

            @b.scalar
            def _(e):
                sems.w(e, "cv4", v40 + 1)
                e.dma_start(out=l3r[0:1, :], in_=t1f[64:65, :]).then_inc(sems.h["sl3"], 16)
                sems.n["sl3"] = w40 + 16

            @b.tensor
            def _(e):
                sems.w(e, "cv4", v40 + 3)
                e.transpose(pA[3][:, 0:1], rl3r[0:1, 0:128], S["id_f32"][0:1, 0:1]).then_inc(sems.h["mm4"], 1)
                e.transpose(pA[3][:, 1:2], rl3r[0:1, 128:256], S["id_f32"][0:1, 0:1]).then_inc(sems.h["mm4"], 1)
                e.transpose(pA[4][:, 0:65], t1f[0:65, 0:128], S["id_f32"][0:65, 0:65]).then_inc(sems.h["mm4"], 1)
                e.transpose(pA[4][:, 65:130], t1f[0:65, 128:256], S["id_f32"][0:65, 0:65]).then_inc(sems.h["mm4"], 1)
                sems.w(e, "cv4", v40 + 2)
                for pc in range(2):
                    for rc in range(2):
                        e.transpose(pA[5 + pc][:, rc * 128:(rc + 1) * 128],
                                    a2f[:, rc * 256 + pc * 128:rc * 256 + (pc + 1) * 128],
                                    S["id_f32"][:, :]).then_inc(sems.h["mm4"], 1)
                sems.n["mm4"] = m40 + 8

        dump_sb(S, f"t1f{h}", t1f[:])
        dump_sb(S, f"rl3pp{h}", rl3pp[:])
        dump_sb(S, f"a2b{h}", a2b[:])
        m50, v50 = sems.n["mm5"], sems.n["cv5"]
        with nc.Block() as b:
            @b.vector
            def _(e):
                e.tensor_scalar_mul(zb[:], a2s, S["rden_rep"][:, 0:1])
                e.tensor_scalar_mul(ztb[:], a2b[:], S["rden_rep"][:, 0:1]).then_inc(sems.h["cv5"], 1)
                for i in range(MP_ITERS):
                    last = i == MP_ITERS - 1
                    base = m50 + 12 * i
                    vb = v50 + 1 + 6 * i
                    sems.w(e, "mm5", base + 2)
                    for pc in range(2):
                        e.scalar_tensor_tensor(mpb[:, pc * 256:(pc + 1) * 256],
                                               pA[pc][:, 0:M], -1.0,
                                               idc[:, 0 * 512 + pc * 256:0 * 512 + (pc + 1) * 256],
                                               ALU.mult, ALU.add)
                    e.drain().then_inc(sems.h["cv5"], 1)
                    sems.w(e, "mm5", base + 4)
                    for pc in range(2):
                        e.tensor_copy(aztb[:, pc * 256:(pc + 1) * 256], pA[2 + pc][:, 0:M])
                    e.drain().then_inc(sems.h["cv5"], 1)
                    sems.w(e, "mm5", base + 6)
                    for pc in range(2):
                        e.scalar_tensor_tensor(mpb[:, pc * 256:(pc + 1) * 256],
                                               pA[4 + pc][:, 0:M], -1.0,
                                               idc[:, 1 * 512 + pc * 256:1 * 512 + (pc + 1) * 256],
                                               ALU.mult, ALU.add)
                    e.drain().then_inc(sems.h["cv5"], 1)
                    sems.w(e, "mm5", base + 8)
                    for pc in range(2):
                        e.scalar_tensor_tensor(mpb[:, pc * 256:(pc + 1) * 256],
                                               pA[4 + pc][:, 0:M], -0.25,
                                               idc[:, 2 * 512 + pc * 256:2 * 512 + (pc + 1) * 256],
                                               ALU.mult, ALU.add)
                    e.drain().then_inc(sems.h["cv5"], 1)
                    sems.w(e, "mm5", base + 10)
                    if not last:
                        for pc in range(2):
                            e.tensor_copy(zb[:, pc * 256:(pc + 1) * 256], pA[6 + pc][:, 0:M])
                    e.drain().then_inc(sems.h["cv5"], 1)
                    sems.w(e, "mm5", base + 12)
                    for pc in range(2):
                        if last:
                            e.tensor_scalar_mul(zstb[:, pc * 256:(pc + 1) * 256],
                                                pA[0 + pc][:, 0:M], rl3pp[:, pc:pc + 1])
                        else:
                            e.tensor_copy(ztb[:, pc * 256:(pc + 1) * 256], pA[0 + pc][:, 0:M])
                    e.drain().then_inc(sems.h["cv5"], 1)
                sems.w(e, "mm5", m50 + 12 * MP_ITERS + 2)
                for mc in range(2):
                    e.tensor_copy(t2b[mc][:], pA[2 + mc][:, 0:64])
                e.drain().then_inc(sems.h["cv5"], 1)
                sems.n["cv5"] = v50 + 2 + 6 * MP_ITERS

            @b.tensor
            def _(e):
                sems.w(e, "cv5", v50 + 1)
                for i in range(MP_ITERS):
                    base = m50 + 12 * i
                    vb = v50 + 1 + 6 * i
                    if i > 0:
                        sems.w(e, "cv5", vb)
                    for rc in range(2):
                        for mc in range(2):
                            lw = a2s[:, mc * 256 + rc * 128:mc * 256 + rc * 128 + 128]
                            e.ldweights(lw)
                            e.matmul(pA[rc][:, 0:M], lw, zb[:, mc * 256:(mc + 1) * 256],
                                     start=(mc == 0), stop=(mc == 1))
                        e.drain().then_inc(sems.h["mm5"], 1)
                    for ic in range(2):
                        for mc in range(2):
                            lw = zb[:, mc * 256 + ic * 128:mc * 256 + ic * 128 + 128]
                            e.ldweights(lw)
                            e.matmul(pA[2 + ic][:, 0:M], lw, a2s[:, mc * 256:(mc + 1) * 256],
                                     start=(mc == 0), stop=(mc == 1))
                        e.drain().then_inc(sems.h["mm5"], 1)
                    sems.w(e, "cv5", vb + 2)
                    for rc in range(2):
                        for ic in range(2):
                            lw = aztb[:, ic * 256 + rc * 128:ic * 256 + rc * 128 + 128]
                            e.ldweights(lw)
                            e.matmul(pA[4 + rc][:, 0:M], lw, mpb[:, ic * 256:(ic + 1) * 256],
                                     start=(ic == 0), stop=(ic == 1))
                        e.drain().then_inc(sems.h["mm5"], 1)
                    sems.w(e, "cv5", vb + 3)
                    for rc in range(2):
                        for ic in range(2):
                            lw = aztb[:, ic * 256 + rc * 128:ic * 256 + rc * 128 + 128]
                            e.ldweights(lw)
                            e.matmul(pA[4 + rc][:, 0:M], lw, mpb[:, ic * 256:(ic + 1) * 256],
                                     start=(ic == 0), stop=(ic == 1))
                        e.drain().then_inc(sems.h["mm5"], 1)
                    sems.w(e, "cv5", vb + 4)
                    for rc in range(2):
                        for ic in range(2):
                            lw = ztb[:, ic * 256 + rc * 128:ic * 256 + rc * 128 + 128]
                            e.ldweights(lw)
                            e.matmul(pA[6 + rc][:, 0:M], lw, mpb[:, ic * 256:(ic + 1) * 256],
                                     start=(ic == 0), stop=(ic == 1))
                        e.drain().then_inc(sems.h["mm5"], 1)
                    for jc in range(2):
                        for ic in range(2):
                            lw = mpb[:, ic * 256 + jc * 128:ic * 256 + jc * 128 + 128]
                            e.ldweights(lw)
                            e.matmul(pA[0 + jc][:, 0:M], lw, ztb[:, ic * 256:(ic + 1) * 256],
                                     start=(ic == 0), stop=(ic == 1))
                        e.drain().then_inc(sems.h["mm5"], 1)
                sems.w(e, "cv5", v50 + 1 + 6 * MP_ITERS)
                for mc in range(2):
                    for ic in range(2):
                        lw = zstb[:, ic * 256 + mc * 128:ic * 256 + mc * 128 + 128]
                        e.ldweights(lw)
                        e.matmul(pA[2 + mc][:, 0:64], lw, t1id[ic][:, 0:64],
                                 start=(ic == 0), stop=(ic == 1))
                    e.drain().then_inc(sems.h["mm5"], 1)
                sems.n["mm5"] = m50 + 12 * MP_ITERS + 2

        dump_sb(S, f"zstb{h}", zstb[:])
        dump_sb(S, f"t2b{h}", t2b[0][:])
        m60, v60 = sems.n["mm6"], sems.n["cv6"]
        ob = hl * 64
        with nc.Block() as b:
            @b.tensor
            def _(e):
                for wg in range(4):
                    if wg >= 1:
                        sems.w(e, "cv6", v60 + wg)
                    for mc in range(2):
                        e.ldweights(t2b[mc][:])
                        for wi in range(4):
                            w = wg * 4 + wi
                            e.matmul(pA[wi][ob:ob + 64, :], t2b[mc][:],
                                     exp1[mc][:, w * 512:(w + 1) * 512],
                                     start=(mc == 0), stop=(mc == 1))
                    e.ldweights(ones[:, 0:64])
                    for wi in range(4):
                        w = wg * 4 + wi
                        e.matmul(pA[4 + wi][ob:ob + 64, :], ones[:, 0:64],
                                 exp1[0][:, w * 512:(w + 1) * 512], start=True, stop=False)
                        e.matmul(pA[4 + wi][ob:ob + 64, :], ones[:, 0:64],
                                 exp1[1][:, w * 512:(w + 1) * 512], start=False, stop=True)
                    e.drain().then_inc(sems.h["mm6"], 1)
                sems.n["mm6"] = m60 + 4

            @b.vector
            def _(e):
                for wg in range(4):
                    sems.w(e, "mm6", m60 + wg + 1)
                    for wi in range(4):
                        w = wg * 4 + wi
                        e.reciprocal(rl1[ob:ob + 64, :], pA[4 + wi][ob:ob + 64, :])
                        e.drain()
                        e.tensor_mul(outp[ob:ob + 64, CK // 2 + w * 512:CK // 2 + (w + 1) * 512],
                                     pA[wi][ob:ob + 64, :], rl1[ob:ob + 64, :])
                        e.drain()
                    e.drain().then_inc(sems.h["cv6"], 1)
                sems.n["cv6"] = v60 + 4


def dump_sb(S, name, ap):
    if name not in S.get("dumpreq", ()):
        return
    nc, sems = S["nc"], S["sems"]
    t = nc.dram_tensor("d_" + name, list(ap.shape), ap.dtype, kind="ExternalOutput").ap()
    with nc.Block() as b:
        @b.scalar
        def _(e):
            e.dma_start(out=t[...], in_=ap).then_inc(sems.h["scv"], 16)
            sems.n["scv"] += 16
            sems.w(e, "scv", sems.n["scv"])


def maybe_dump(S, name, src_ap):
    if name not in S["dump"]:
        return
    nc, sems = S["nc"], S["sems"]
    with nc.Block() as b:
        @b.scalar
        def _(e):
            e.dma_start(out=S["dump"][name][...], in_=src_ap).then_inc(sems.h["scv"], 16)
            sems.n["scv"] += 16
            sems.w(e, "scv", sems.n["scv"])


def emit_conv_store(S, p, outp, cvb, ostg, pA):
    nc, sems = S["nc"], S["sems"]
    m70, v70 = sems.n["mm7"], sems.n["cv7"]
    s70 = [sems.n["sst0"], sems.n["sst1"]]
    with nc.Block() as b:
        @b.tensor
        def _(e):
            for wg in range(4):
                if wg >= 2:
                    sems.w(e, "cv7", v70 + wg - 1)
                for j in range(CK):
                    lw = cvb[:, j * 128:(j + 1) * 128]
                    e.ldweights(lw)
                    for wi in range(4):
                        w = wg * 4 + wi
                        e.matmul(pA[(wg % 2) * 4 + wi][:],
                                 lw, outp[:, w * 512 + j:w * 512 + j + 512],
                                 start=(j == 0), stop=(j == CK - 1))
                e.drain().then_inc(sems.h["mm7"], 1)
            sems.n["mm7"] = m70 + 4

        @b.vector
        def _(e):
            for wg in range(4):
                sems.w(e, "mm7", m70 + wg + 1)
                if wg >= 2:
                    sems.w(e, f"sst{wg % 2}", s70[wg % 2] + 16)
                for wi in range(4):
                    w = wg * 4 + wi
                    e.scalar_tensor_tensor(ostg[wg % 2][:, wi * 512:(wi + 1) * 512],
                                           pA[(wg % 2) * 4 + wi][:], 0.0,
                                           outp[:, CK // 2 + w * 512:CK // 2 + (w + 1) * 512],
                                           ALU.add, ALU.add)
                e.drain().then_inc(sems.h["cv7"], 1)
            sems.n["cv7"] = v70 + 4

        @b.scalar
        def _(e):
            for wg in range(4):
                sems.w(e, "cv7", v70 + wg + 1)
                e.dma_start(out=S["outall"][p][:, wg * 2048:(wg + 1) * 2048],
                            in_=ostg[wg % 2][:]).then_inc(sems.h[f"sst{wg % 2}"], 16)
            sems.w(e, "sst0", s70[0] + 32)
            sems.w(e, "sst1", s70[1] + 32)
            e.dma_start(out=S["cin"][p][:, :],
                        in_=S["outall"][p][:, DS(S["v_oth"], NT)]).then_inc(sems.h["sci"], 16)
            sems.n["sst0"] = s70[0] + 32
            sems.n["sst1"] = s70[1] + 32
            sems.n["sci"] += 16


def emit_exchange(S):
    nc, sems = S["nc"], S["sems"]
    with nc.Block() as b:
        @b.gpsimd
        def _(e):
            sems.w(e, "sci", sems.n["sci"])
            e.collective_compute("AllGather", ALU.bypass,
                                 ins=[S["cin"][:, :, :]], outs=[S["cout"][:, :, :, :]],
                                 replica_groups=[[0, 1], [2, 3], [4, 5], [6, 7]]).then_inc(sems.h["cc"], 1)
            sems.n["cc"] += 1


def emit_fc_ln(S):
    nc, sems = S["nc"], S["sems"]
    NTT = NT // 128
    RS = AF.Abs_reciprocal_sqrt
    with ExitStack() as pst:
        sb = lambda *a, **k: pst.enter_context(nc.sbuf_tensor(*a, **k))
        ps_ = lambda *a, **k: pst.enter_context(nc.psum_tensor(*a, **k))
        wff = [sb(f"wff{i}", [128, D], F32) for i in range(2)]
        wfb = [sb(f"wfb{i}", [128, D], BF16) for i in range(8)]
        rwf = sb("rwf", [128, NEXP], F32)
        rwb = [sb(f"rwb{i}", [128, NEXP], BF16) for i in range(8)]
        bfc_sb = sb("bfc_sb", [128, D], F32)
        lng_sb = sb("lng_sb", [128, D], F32)
        lnb_sb = sb("lnb_sb", [128, D], F32)
        lt = [sb(f"lt{i}", [128, 128], BF16) for i in range(16)]
        xo = [sb(f"xo{i}", [128, D], F32) for i in range(2)]
        res_t = sb("res_t", [128, D], F32)
        sq_t = sb("sq_t", [128, D], F32)
        r1t = sb("r1t", [128, D], F32)
        hh = sb("hh", [128, D], F32)
        hb = sb("hb", [128, D], BF16)
        hstg = sb("hstg", [128, 8192], BF16, is_transpose_dest=True)
        lsb = sb("lsb", [8, 128], F32)
        logall = sb("logall", [128, NEXP * NTT], F32)
        el = sb("el", [128, NEXP], F32)
        stt_ = sb("stt_", [128, 8], F32)
        gt = sb("gt", [128, NEXP], F32)
        gates8 = sb("gates8", [8, NT], F32)
        pF = [ps_(f"pF{i}", [128, 512], F32) for i in range(8)]

        v0a = sems.n["cv8"]
        q0f = [sems.n["swq0"], sems.n["swq1"]]
        g0f, c0f = sems.n["sgr"], sems.n["scv"]
        with nc.Block() as b:
            @b.sync
            def _(e):
                for dc in range(8):
                    if dc >= 2:
                        sems.w(e, "cv8", v0a + dc - 1)
                    e.dma_start(out=wff[dc % 2][:], in_=S["wfc"][dc]).then_inc(sems.h[f"swq{dc % 2}"], 16)
                e.dma_start(out=bfc_sb[:], in_=S["bfc_rep"][:, :]).then_inc(sems.h["scv"], 16)
                e.dma_start(out=lng_sb[:], in_=S["lng_rep"][:, :]).then_inc(sems.h["scv"], 16)
                e.dma_start(out=lnb_sb[:], in_=S["lnb_rep"][:, :]).then_inc(sems.h["scv"], 16)
                for dc in range(8):
                    sems.w(e, "cv8", v0a + 8 + dc)
                    e.dma_start(out=rwf[:], in_=S["rwT"][dc]).then_inc(sems.h["sgr"], 16)
                sems.n["swq0"] = q0f[0] + 16 * 4
                sems.n["swq1"] = q0f[1] + 16 * 4
                sems.n["scv"] = c0f + 16 * 3
                sems.n["sgr"] = g0f + 16 * 8
                sems.w(e, "scv", sems.n["scv"])

            @b.vector
            def _(e):
                for dc in range(8):
                    sems.w(e, f"swq{dc % 2}", q0f[dc % 2] + 16 * (dc // 2 + 1))
                    e.tensor_copy(wfb[dc][:], wff[dc % 2][:]).then_inc(sems.h["cv8"], 1)
                for dc in range(8):
                    sems.w(e, "sgr", g0f + 16 * (dc + 1))
                    e.tensor_copy(rwb[dc][:], rwf[:]).then_inc(sems.h["cv8"], 1)
                sems.n["cv8"] = v0a + 16

        f0 = [sems.n["sfc0"], sems.n["sfc1"]]
        r10, hf0 = sems.n["sr1"], sems.n["shf"]
        m0, v0, a0, t0 = sems.n["mm8"], sems.n["cv8"], sems.n["ca8"], sems.n["sx8"]
        with nc.Block() as b:
            @b.sync
            def _(e):
                sems.w(e, "cc", sems.n["cc"])
                for tt in range(NTT):
                    if tt >= 2:
                        sems.w(e, "mm8", m0 + 3 * (tt - 1))
                        sems.w(e, "cv8", v0 + 7 * (tt - 1))
                    for kc in range(8):
                        if kc < 4:
                            src = S["outall"][kc][:, DS(S["v_own"] + tt * 128, 128)]
                        else:
                            src = S["cout"][DS(S["v_slot"], 1), kc - 4].rearrange(
                                "s p t -> p (s t)")[:, tt * 128:(tt + 1) * 128]
                        e.dma_start(out=lt[(tt % 2) * 8 + kc][:], in_=src).then_inc(sems.h[f"sfc{tt % 2}"], 16)
                    e.dma_start(out=xo[tt % 2][:], in_=S["x_own"][tt]).then_inc(sems.h[f"sfc{tt % 2}"], 16)
                sems.n["sfc0"] = f0[0] + 16 * 9 * (NTT // 2)
                sems.n["sfc1"] = f0[1] + 16 * 9 * (NTT // 2)

            @b.tensor
            def _(e):
                for tt in range(NTT):
                    sems.w(e, f"sfc{tt % 2}", f0[tt % 2] + 16 * 9 * (tt // 2 + 1))
                    if tt >= 2:
                        sems.w(e, "cv8", v0 + 7 * (tt - 1))
                    for kc in range(8):
                        e.ldweights(lt[(tt % 2) * 8 + kc][:])
                        for half in range(2):
                            e.matmul(pF[(tt % 2) * 2 + half][:], lt[(tt % 2) * 8 + kc][:],
                                     wfb[kc][:, half * 512:(half + 1) * 512],
                                     start=(kc == 0), stop=(kc == 7))
                    e.drain().then_inc(sems.h["mm8"], 1)
                    if tt >= 1:
                        sems.w(e, "cv8", v0 + 7 * (tt - 1) + 6)
                    sems.w(e, "sx8", t0 + 16 * 8 * (tt + 1))
                    for dc in range(8):
                        e.ldweights(rwb[dc][:])
                        e.matmul(pF[4][0:8, 0:128], rwb[dc][:],
                                 hstg[:, dc * 1024 + (tt % 8) * 128:dc * 1024 + (tt % 8) * 128 + 128],
                                 start=(dc == 0), stop=(dc == 7))
                    e.drain().then_inc(sems.h["mm8"], 1)
                    sems.w(e, "cv8", v0 + 7 * tt + 6)
                    e.transpose(pF[5][:, 0:8], lsb[0:8, :], S["id_f32"][0:8, 0:8]).then_inc(sems.h["mm8"], 1)
                sems.n["mm8"] = m0 + 3 * NTT

            @b.vector
            def _(e):
                for tt in range(NTT):
                    sems.w(e, "mm8", m0 + 3 * tt + 1)
                    if tt >= 1:
                        sems.w(e, "ca8", a0 + 4 * (tt - 1) + 1)
                    e.scalar_tensor_tensor(res_t[:, 0:512], pF[(tt % 2) * 2][:], 0.0,
                                           bfc_sb[:, 0:512], ALU.add, ALU.add)
                    e.scalar_tensor_tensor(res_t[:, 512:1024], pF[(tt % 2) * 2 + 1][:], 0.0,
                                           bfc_sb[:, 512:1024], ALU.add, ALU.add)
                    sems.w(e, f"sfc{tt % 2}", f0[tt % 2] + 16 * 9 * (tt // 2 + 1))
                    e.drain()
                    e.tensor_add(res_t[:], res_t[:], xo[tt % 2][:])
                    e.drain()
                    e.reduce_sum(stt_[:, 0:1], res_t[:], axis=AX.X).then_inc(sems.h["cv8"], 1)
                    sems.w(e, "ca8", a0 + 4 * tt + 1)
                    e.drain()
                    e.tensor_scalar(stt_[:, 2:3], stt_[:, 0:1], 1.0 / D, None, ALU.mult)
                    e.tensor_scalar(stt_[:, 3:4], stt_[:, 1:2], 1.0 / D, None, ALU.mult)
                    e.drain()
                    e.tensor_mul(stt_[:, 6:7], stt_[:, 2:3], stt_[:, 2:3])
                    e.drain()
                    e.tensor_sub(stt_[:, 4:5], stt_[:, 3:4], stt_[:, 6:7])
                    e.drain()
                    e.tensor_scalar(stt_[:, 4:5], stt_[:, 4:5], EPS, None, ALU.add).then_inc(sems.h["cv8"], 1)
                    sems.w(e, "ca8", a0 + 4 * tt + 2)
                    if tt >= 1:
                        sems.w(e, "sr1", r10 + 16 * tt)
                    e.reciprocal(stt_[:, 7:8], stt_[:, 5:6])
                    e.drain()
                    e.tensor_scalar(stt_[:, 6:7], stt_[:, 2:3], stt_[:, 7:8], -1.0,
                                    ALU.mult, ALU.mult)
                    e.drain()
                    e.tensor_scalar(r1t[:], res_t[:], stt_[:, 7:8], stt_[:, 6:7],
                                    ALU.mult, ALU.add).then_inc(sems.h["cv8"], 1)
                    sems.w(e, "ca8", a0 + 4 * tt + 3)
                    e.drain()
                    e.reduce_sum(stt_[:, 0:1], r1t[:], axis=AX.X)
                    e.drain()
                    e.tensor_scalar(stt_[:, 2:3], stt_[:, 0:1], 1.0 / D, None, ALU.mult)
                    e.tensor_scalar(stt_[:, 3:4], stt_[:, 1:2], 1.0 / D, None, ALU.mult)
                    e.drain()
                    e.tensor_mul(stt_[:, 6:7], stt_[:, 2:3], stt_[:, 2:3])
                    e.drain()
                    e.tensor_sub(stt_[:, 4:5], stt_[:, 3:4], stt_[:, 6:7])
                    e.drain()
                    e.tensor_scalar(stt_[:, 4:5], stt_[:, 4:5], EPS, None, ALU.add).then_inc(sems.h["cv8"], 1)
                    sems.w(e, "ca8", a0 + 4 * tt + 4)
                    if tt >= 1:
                        sems.w(e, "sx8", t0 + 16 * 8 * tt)
                    e.reciprocal(stt_[:, 7:8], stt_[:, 5:6])
                    e.drain()
                    e.tensor_scalar(stt_[:, 6:7], stt_[:, 2:3], stt_[:, 7:8], -1.0,
                                    ALU.mult, ALU.mult)
                    e.drain()
                    e.tensor_scalar(hh[:], r1t[:], stt_[:, 7:8], stt_[:, 6:7],
                                    ALU.mult, ALU.add)
                    e.drain()
                    e.tensor_mul(hh[:], hh[:], lng_sb[:])
                    e.drain()
                    e.tensor_add(hb[:], hh[:], lnb_sb[:]).then_inc(sems.h["cv8"], 1)
                    sems.w(e, "mm8", m0 + 3 * tt + 2)
                    e.tensor_copy(lsb[0:8, :], pF[4][0:8, 0:128]).then_inc(sems.h["cv8"], 1)
                    sems.w(e, "mm8", m0 + 3 * tt + 3)
                    e.tensor_copy(logall[:, tt * NEXP:(tt + 1) * NEXP],
                                  pF[5][:, 0:8]).then_inc(sems.h["cv8"], 1)
                sems.n["cv8"] = v0 + 7 * NTT

            @b.scalar
            def _(e):
                for tt in range(NTT):
                    sems.w(e, "cv8", v0 + 7 * tt + 1)
                    e.activation(sq_t[:], res_t[:], AF.Square,
                                 accum_out=stt_[:, 1:2]).then_inc(sems.h["ca8"], 1)
                    sems.w(e, "cv8", v0 + 7 * tt + 2)
                    e.activation(stt_[:, 5:6], stt_[:, 4:5], AF.Sqrt).then_inc(sems.h["ca8"], 1)
                    sems.w(e, "cv8", v0 + 7 * tt + 3)
                    e.dma_start(out=S["r1n_hbm"][tt], in_=r1t[:]).then_inc(sems.h["sr1"], 16)
                    e.drain()
                    e.activation(sq_t[:], r1t[:], AF.Square,
                                 accum_out=stt_[:, 1:2]).then_inc(sems.h["ca8"], 1)
                    sems.w(e, "cv8", v0 + 7 * tt + 4)
                    e.activation(stt_[:, 5:6], stt_[:, 4:5], AF.Sqrt).then_inc(sems.h["ca8"], 1)
                    sems.w(e, "cv8", v0 + 7 * tt + 5)
                    for dc in range(8):
                        e.dma_start_transpose(
                            out=hstg[:, dc * 1024 + (tt % 8) * 128:dc * 1024 + (tt % 8) * 128 + 128],
                            in_=hb[:, dc * 128:(dc + 1) * 128]).then_inc(sems.h["sx8"], 16)
                    if tt % 8 == 7:
                        sems.w(e, "sx8", t0 + 16 * 8 * (tt + 1))
                        for dc in range(8):
                            e.dma_start(out=S["h_hbm"][dc][:, (tt // 8) * 1024:(tt // 8 + 1) * 1024],
                                        in_=hstg[:, dc * 1024:(dc + 1) * 1024]).then_inc(sems.h["shf"], 16)
                        sems.w(e, "shf", hf0 + 16 * 8 * ((tt + 1) // 8))
                sems.n["ca8"] = a0 + 4 * NTT
                sems.n["sx8"] = t0 + 16 * 8 * NTT
                sems.n["sr1"] = r10 + 16 * NTT
                sems.n["shf"] = hf0 + 16 * 8 * (NTT // 8)
                sems.w(e, "sr1", sems.n["sr1"])

        g0, gm0, gv0, gw0 = sems.n["ca9"], sems.n["mm9"], sems.n["cv9"], sems.n["sw9"]
        with nc.Block() as b:
            @b.vector
            def _(e):
                for tt in range(NTT):
                    lg = logall[:, tt * NEXP:(tt + 1) * NEXP]
                    e.reduce_max(stt_[:, 0:1], lg, axis=AX.X)
                    e.drain()
                    e.tensor_scalar_mul(stt_[:, 1:2], stt_[:, 0:1], -1.0).then_inc(sems.h["cv9"], 1)
                    sems.w(e, "ca9", g0 + tt + 1)
                    e.reduce_sum(stt_[:, 2:3], el[:], axis=AX.X)
                    e.drain()
                    e.reciprocal(stt_[:, 3:4], stt_[:, 2:3])
                    e.tensor_scalar(gt[:], lg, stt_[:, 0:1], None, ALU.is_ge)
                    e.drain()
                    e.tensor_scalar_mul(gt[:], gt[:], stt_[:, 3:4]).then_inc(sems.h["cv9"], 1)
                    sems.w(e, "mm9", gm0 + tt + 1)
                    e.tensor_copy(gates8[0:8, tt * 128:(tt + 1) * 128],
                                  pF[6][0:8, 0:128]).then_inc(sems.h["cv9"], 1)
                sems.n["cv9"] = gv0 + 3 * NTT

            @b.scalar
            def _(e):
                for tt in range(NTT):
                    sems.w(e, "cv9", gv0 + 3 * tt + 1)
                    e.activation(el[:], logall[:, tt * NEXP:(tt + 1) * NEXP],
                                 AF.Exp, bias=stt_[:, 1:2]).then_inc(sems.h["ca9"], 1)
                sems.n["ca9"] = g0 + NTT

            @b.tensor
            def _(e):
                for tt in range(NTT):
                    sems.w(e, "cv9", gv0 + 3 * tt + 2)
                    e.transpose(pF[6][0:8, 0:128], gt[:, 0:NEXP],
                                S["id_f32"][:, :]).then_inc(sems.h["mm9"], 1)
                sems.n["mm9"] = gm0 + NTT

        with nc.Block() as b:
            @b.scalar
            def _(e):
                e.dma_start(out=S["grows_hbm"][:, :],
                            in_=gates8[0:8, :]).then_inc(sems.h["sw9"], 16)
                sems.n["sw9"] = gw0 + 16
                sems.w(e, "sw9", sems.n["sw9"])


def _quant_tiles(e_id):
    pass


def emit_ffn(S):
    nc, sems = S["nc"], S["sems"]
    EXP = [("sh", S["sw1"], S["sw2"], None)] + \
          [(f"e{i}", S["ew1"][i], S["ew2"][i], i) for i in range(NEXP)]
    with ExitStack() as pst:
        sb = lambda *a, **k: pst.enter_context(nc.sbuf_tensor(*a, **k))
        ps_ = lambda *a, **k: pst.enter_context(nc.psum_tensor(*a, **k))
        rawb = [sb(f"rawb{i}", [128, FD], F32) for i in range(2)]
        q2b = sb("q2b", [128, FD], BF16)
        wqtb = [sb(f"wqtb{i}", [128, FD], BF16) for i in range(2)]
        growb = sb("growb", [1, 1024], F32)
        sums = sb("sums", [128, 4], F32)
        w1qT = [[sb(f"w1qT{bu}_{kc}", [128, FD], BF16, is_transpose_dest=True)
                 for kc in range(8)] for bu in range(2)]
        w2qT = [sb(f"w2qT{fc}", [128, D], BF16, is_transpose_dest=True) for fc in range(16)]
        hsb = [[sb(f"hsb{bu}_{kc}", [128, 1024], BF16) for kc in range(8)] for bu in range(2)]
        h1b = sb("h1b", [128, 16 * 1024], BF16)
        gate_sb = sb("gate_sb", [128, 1024], BF16)
        eo = [sb(f"eo{i}", [128, 1024], F32) for i in range(2)]
        pW = [ps_(f"pW{i}", [128, 512], F32) for i in range(8)]

        def emit_quant(b, pi):
            nm, w1src, w2src, _ = EXP[pi]
            par = pi % 2
            r0p = [sems.n["rwl0"], sems.n["rwl1"]]
            s0p = [sems.n["scw0"], sems.n["scw1"]]
            c0 = sems.n["ca9"]
            v0 = sems.n["cv9"]
            tiles = [("w1", r, w1src[r], 1024, 1.0 / 1024) for r in range(16)] + \
                    [("w2", r, w2src[r], 2048, 1.0 / 2048) for r in range(8)]

            @b.scalar
            def _(e):
                for t, (kind, r, src, width, inv) in enumerate(tiles):
                    q = t % 2
                    if t >= 2:
                        sems.w(e, "cv9", v0 + t - 1)
                    e.dma_start(out=rawb[q][:, 0:width], in_=src).then_inc(sems.h[f"rwl{q}"], 16)
                    sems.w(e, f"rwl{q}", r0p[q] + 16 * (t // 2 + 1))
                    e.drain()
                    e.activation(q2b[:, 0:width], rawb[q][:, 0:width], AF.Abs,
                                 accum_out=sums[:, 0:1]).then_inc(sems.h["ca9"], 1)
                    sems.w(e, "cv9", v0 + t + 1)
                    if kind == "w1":
                        e.dma_start(out=S["w1scr"][par][:, r * 128:(r + 1) * 128, :].rearrange(
                                        "c q d -> q c d"),
                                    in_=wqtb[q][:, 0:1024].rearrange(
                                        "q (c d) -> q c d", d=128)).then_inc(sems.h[f"scw{q}"], 16)
                    else:
                        e.dma_start(out=S["w2scr"][par][:, r * 128:(r + 1) * 128, :].rearrange(
                                        "c q d -> q c d"),
                                    in_=wqtb[q][:, 0:2048].rearrange(
                                        "q (c d) -> q c d", d=128)).then_inc(sems.h[f"scw{q}"], 16)
                sems.n["rwl0"] = r0p[0] + 16 * 12
                sems.n["rwl1"] = r0p[1] + 16 * 12
                sems.n["scw0"] = s0p[0] + 16 * 12
                sems.n["scw1"] = s0p[1] + 16 * 12
                sems.n["ca9"] = c0 + 24

            @b.vector
            def _(e):
                for t, (kind, r, src, width, inv) in enumerate(tiles):
                    q = t % 2
                    sems.w(e, "ca9", c0 + t + 1)
                    if t >= 2:
                        sems.w(e, f"scw{q}", s0p[q] + 16 * (t // 2))
                    e.tensor_scalar(sums[:, 1:2], sums[:, 0:1], inv, 1e-5, ALU.mult, ALU.max)
                    e.drain()
                    e.tensor_scalar_mul(sums[:, 2:3], sums[:, 1:2], 0.5)
                    e.tensor_scalar_mul(sums[:, 3:4], sums[:, 1:2], -0.5)
                    e.drain()
                    e.tensor_scalar(wqtb[q][:, 0:width], rawb[q][:, 0:width],
                                    sums[:, 2:3], sums[:, 1:2], ALU.is_ge, ALU.mult)
                    e.tensor_scalar(q2b[:, 0:width], rawb[q][:, 0:width],
                                    sums[:, 3:4], sums[:, 1:2], ALU.is_le, ALU.mult)
                    e.drain()
                    e.tensor_sub(wqtb[q][:, 0:width], wqtb[q][:, 0:width],
                                 q2b[:, 0:width]).then_inc(sems.h["cv9"], 1)
                sems.n["cv9"] = v0 + 24

        def emit_xbar(b, pi):
            par = pi % 2
            bu = pi % 2
            s_need0, s_need1 = sems.n["scw0"], sems.n["scw1"]
            m_need = sems.n["mm9"]
            S["xbar_base"] = sems.n["sxw"]

            @b.sync
            def _(e):
                sems.w(e, "scw0", s_need0)
                sems.w(e, "scw1", s_need1)
                for dc in range(8):
                    e.dma_start_transpose(out=w1qT[bu][dc][:],
                                          in_=S["w1scr"][par][dc]).then_inc(sems.h["sxw"], 16)
                sems.w(e, "mm9", m_need)
                for fc in range(16):
                    e.dma_start_transpose(out=w2qT[fc][:],
                                          in_=S["w2scr"][par][fc]).then_inc(sems.h["sxw"], 16)
                sems.n["sxw"] += 16 * 24

        with nc.Block() as b:
            emit_quant(b, 0)
        with nc.Block() as b:
            emit_xbar(b, 0)

        for pi in range(len(EXP)):
            nm, w1src, w2src, grow = EXP[pi]
            bu = pi % 2
            is_sh = grow is None
            x9 = S["xbar_base"]
            m0, v0, c0 = sems.n["mm9"], sems.n["cv9"], sems.n["ca9"]
            gm = 0 if is_sh else 1
            per_grp_mm = gm + 16 + 8
            g0 = [sems.n["gda"], sems.n["gdb"]]
            per_grp_cv = (0 if is_sh else 1) + 8
            gld0 = sems.n["sgr"]
            h0p = [sems.n["shh0"], sems.n["shh1"]]
            with nc.Block() as b:
                @b.sync
                def _(e):
                    for grp in range(4):
                        if not is_sh:
                            if grp > 0:
                                sems.w(e, "mm9", m0 + (grp - 1) * per_grp_mm + 1)
                            e.dma_start(out=growb[0:1, :],
                                        in_=S["grows_hbm"][grow:grow + 1, grp * 1024:(grp + 1) * 1024]).then_inc(sems.h["sgr"], 16)
                        if grp >= 2:
                            sems.w(e, "mm9", m0 + (grp - 1) * per_grp_mm + gm + 16)
                        for kc in range(8):
                            e.dma_start(out=hsb[grp % 2][kc][:],
                                        in_=S["h_hbm"][kc][:, grp * 1024:(grp + 1) * 1024]).then_inc(sems.h[f"shh{grp % 2}"], 16)
                    if not is_sh:
                        sems.n["sgr"] = gld0 + 16 * 4
                    sems.n["shh0"] = h0p[0] + 16 * 16
                    sems.n["shh1"] = h0p[1] + 16 * 16

                @b.tensor
                def _(e):
                    for grp in range(4):
                        gb = m0 + grp * per_grp_mm
                        if not is_sh:
                            if grp > 0:
                                sems.w(e, "cv9", v0 + grp * per_grp_cv)
                            sems.w(e, "sgr", gld0 + 16 * (grp + 1))
                            e.matmul(pW[6][:], S["ones_f32"][0:1, :],
                                     growb[0:1, 0:512], start=True, stop=True)
                            e.matmul(pW[7][:], S["ones_f32"][0:1, :],
                                     growb[0:1, 512:1024], start=True, stop=True)
                            e.drain().then_inc(sems.h["mm9"], 1)
                        sems.w(e, "sxw", x9 + 16 * 24)
                        sems.w(e, f"shh{grp % 2}", h0p[grp % 2] + 16 * 8 * (grp // 2 + 1))
                        for fc in range(16):
                            if grp * 16 + fc >= 2:
                                sems.w(e, "ca9", c0 + grp * 16 + fc - 1)
                            for kc in range(8):
                                lw = w1qT[bu][kc][:, fc * 128:(fc + 1) * 128]
                                e.ldweights(lw)
                                for wi in range(2):
                                    e.matmul(pW[(fc % 2) * 2 + wi][:], lw,
                                             hsb[grp % 2][kc][:, wi * 512:(wi + 1) * 512],
                                             start=(kc == 0), stop=(kc == 7))
                            e.drain().then_inc(sems.h["mm9"], 1)
                        sems.w(e, "sxw", x9 + 16 * 24)
                        sems.w(e, "ca9", c0 + (grp + 1) * 16)
                        for dc in range(8):
                            if dc >= 2:
                                sems.w(e, "cv9", v0 + grp * per_grp_cv + (0 if is_sh else 1) + dc - 1)
                            bk = 4 if (dc % 2 == 0) else 6
                            if dc <= 1:
                                sems.w(e, "cv9", v0 + grp * per_grp_cv + (0 if is_sh else 1))
                            for fc in range(16):
                                lw = w2qT[fc][:, dc * 128:(dc + 1) * 128]
                                e.ldweights(lw)
                                for wi in range(2):
                                    e.matmul(pW[bk + wi][:], lw,
                                             h1b[:, fc * 1024 + wi * 512:fc * 1024 + (wi + 1) * 512],
                                             start=(fc == 0), stop=(fc == 15))
                            e.drain().then_inc(sems.h["mm9"], 1)
                    sems.n["mm9"] = m0 + 4 * per_grp_mm

                @b.scalar
                def _(e):
                    for grp in range(4):
                        for fc in range(16):
                            sems.w(e, "mm9", m0 + grp * per_grp_mm + gm + fc + 1)
                            for wi in range(2):
                                e.activation(h1b[:, fc * 1024 + wi * 512:fc * 1024 + (wi + 1) * 512],
                                             pW[(fc % 2) * 2 + wi][:], AF.Silu)
                            e.drain().then_inc(sems.h["ca9"], 1)
                    sems.n["ca9"] = c0 + 64

                @b.vector
                def _(e):
                    for grp in range(4):
                        if not is_sh:
                            sems.w(e, "mm9", m0 + grp * per_grp_mm + 1)
                            e.tensor_copy(gate_sb[:, 0:512], pW[6][:])
                            e.tensor_copy(gate_sb[:, 512:1024], pW[7][:]).then_inc(sems.h["cv9"], 1)
                        for dc in range(8):
                            sems.w(e, "mm9", m0 + grp * per_grp_mm + gm + 16 + dc + 1)
                            sems.w(e, f"gd{'ab'[dc % 2]}",
                                   g0[dc % 2] + 16 * ((grp * 8 + dc) // 2))
                            bk = 4 if (dc % 2 == 0) else 6
                            for wi in range(2):
                                if is_sh:
                                    e.tensor_copy(eo[dc % 2][:, wi * 512:(wi + 1) * 512],
                                                  pW[bk + wi][:])
                                else:
                                    e.scalar_tensor_tensor(eo[dc % 2][:, wi * 512:(wi + 1) * 512],
                                                           pW[bk + wi][:], 0.0,
                                                           gate_sb[:, wi * 512:(wi + 1) * 512],
                                                           ALU.add, ALU.mult)
                            e.drain().then_inc(sems.h["cv9"], 1)
                    sems.n["cv9"] = v0 + 4 * per_grp_cv

                @b.gpsimd
                def _(e):
                    for grp in range(4):
                        for dc in range(8):
                            sems.w(e, "cv9", v0 + grp * per_grp_cv + (0 if is_sh else 1) + dc + 1)
                            e.dma_start(out=S["acc_hbm"][dc][:, grp * 1024:(grp + 1) * 1024],
                                        in_=eo[dc % 2][:],
                                        accum_op=(ALU.bypass if is_sh else ALU.add)).then_inc(
                                            sems.h[f"gd{'ab'[dc % 2]}"], 16)
                    sems.n["gda"] = g0[0] + 16 * 16
                    sems.n["gdb"] = g0[1] + 16 * 16

                if pi + 1 < len(EXP):
                    emit_quant(b, pi + 1)
            if pi + 1 < len(EXP):
                with nc.Block() as b2:
                    emit_xbar(b2, pi + 1)


def emit_tail(S):
    nc, sems = S["nc"], S["sems"]
    NTT = NT // 128
    RS = AF.Abs_reciprocal_sqrt
    with ExitStack() as pst:
        sb = lambda *a, **k: pst.enter_context(nc.sbuf_tensor(*a, **k))
        ps_ = lambda *a, **k: pst.enter_context(nc.psum_tensor(*a, **k))
        ac8 = [sb(f"ac8_{i}", [128, 1024], F32) for i in range(2)]
        r1b = [sb(f"r1b_{i}", [128, D], F32) for i in range(2)]
        ffn_t = sb("ffn_t", [128, D], F32)
        res2 = sb("res2", [128, D], F32)
        sq2 = sb("sq2", [128, D], F32)
        yt = [sb(f"yt{i}", [128, D], F32) for i in range(2)]
        st2 = sb("st2", [128, 8], F32)
        pT = [ps_(f"pT{i}", [128, 512], F32) for i in range(4)]
        a0p = [sems.n["sac0"], sems.n["sac1"]]
        y0p = [sems.n["sy0"], sems.n["sy1"]]
        m0, v0, a0 = sems.n["mm9"], sems.n["cv9"], sems.n["ca9"]
        gall = [sems.n["gda"], sems.n["gdb"]]
        with nc.Block() as b:
            @b.sync
            def _(e):
                sems.w(e, "gda", gall[0])
                sems.w(e, "gdb", gall[1])
                for tt in range(NTT):
                    if tt >= 2:
                        sems.w(e, "cv9", v0 + 4 * (tt - 1))
                    for dc in range(8):
                        e.dma_start(out=ac8[tt % 2][:, dc * 128:(dc + 1) * 128],
                                    in_=S["acc_hbm"][dc][:, tt * 128:(tt + 1) * 128]).then_inc(sems.h[f"sac{tt % 2}"], 16)
                    e.dma_start(out=r1b[tt % 2][:], in_=S["r1n_hbm"][tt]).then_inc(sems.h[f"sac{tt % 2}"], 16)
                sems.n["sac0"] = a0p[0] + 16 * 9 * (NTT // 2)
                sems.n["sac1"] = a0p[1] + 16 * 9 * (NTT // 2)

            @b.tensor
            def _(e):
                for tt in range(NTT):
                    sems.w(e, f"sac{tt % 2}", a0p[tt % 2] + 16 * 9 * (tt // 2 + 1))
                    if tt >= 1:
                        sems.w(e, "cv9", v0 + 4 * (tt - 1) + 1)
                    for dc in range(8):
                        e.transpose(pT[dc // 2][:, (dc % 2) * 128:(dc % 2) * 128 + 128],
                                    ac8[tt % 2][:, dc * 128:(dc + 1) * 128],
                                    S["id_f32"][:, :])
                    e.drain().then_inc(sems.h["mm9"], 1)
                sems.n["mm9"] = m0 + NTT

            @b.vector
            def _(e):
                for tt in range(NTT):
                    sems.w(e, "mm9", m0 + tt + 1)
                    for dc in range(8):
                        e.tensor_copy(ffn_t[:, dc * 128:(dc + 1) * 128],
                                      pT[dc // 2][:, (dc % 2) * 128:(dc % 2) * 128 + 128])
                    e.drain().then_inc(sems.h["cv9"], 1)
                    sems.w(e, f"sac{tt % 2}", a0p[tt % 2] + 16 * 9 * (tt // 2 + 1))
                    if tt >= 1:
                        sems.w(e, "ca9", a0 + 2 * (tt - 1) + 1)
                    e.drain()
                    e.tensor_add(res2[:], ffn_t[:], r1b[tt % 2][:])
                    e.drain()
                    e.reduce_sum(st2[:, 0:1], res2[:], axis=AX.X).then_inc(sems.h["cv9"], 1)
                    sems.w(e, "ca9", a0 + 2 * tt + 1)
                    e.drain()
                    e.tensor_scalar(st2[:, 2:3], st2[:, 0:1], 1.0 / D, None, ALU.mult)
                    e.tensor_scalar(st2[:, 3:4], st2[:, 1:2], 1.0 / D, None, ALU.mult)
                    e.drain()
                    e.tensor_mul(st2[:, 6:7], st2[:, 2:3], st2[:, 2:3])
                    e.drain()
                    e.tensor_sub(st2[:, 4:5], st2[:, 3:4], st2[:, 6:7])
                    e.drain()
                    e.tensor_scalar(st2[:, 4:5], st2[:, 4:5], EPS, None, ALU.add).then_inc(sems.h["cv9"], 1)
                    sems.w(e, "ca9", a0 + 2 * tt + 2)
                    if tt >= 2:
                        sems.w(e, f"sy{tt % 2}", y0p[tt % 2] + 16 * (tt // 2))
                    e.reciprocal(st2[:, 7:8], st2[:, 5:6])
                    e.drain()
                    e.tensor_scalar(st2[:, 6:7], st2[:, 2:3], st2[:, 7:8], -1.0,
                                    ALU.mult, ALU.mult)
                    e.drain()
                    e.tensor_scalar(yt[tt % 2][:], res2[:], st2[:, 7:8], st2[:, 6:7],
                                    ALU.mult, ALU.add).then_inc(sems.h["cv9"], 1)
                sems.n["cv9"] = v0 + 4 * NTT

            @b.scalar
            def _(e):
                for tt in range(NTT):
                    sems.w(e, "cv9", v0 + 4 * tt + 2)
                    e.activation(sq2[:], res2[:], AF.Square,
                                 accum_out=st2[:, 1:2]).then_inc(sems.h["ca9"], 1)
                    sems.w(e, "cv9", v0 + 4 * tt + 3)
                    e.activation(st2[:, 5:6], st2[:, 4:5], AF.Sqrt).then_inc(sems.h["ca9"], 1)
                    sems.w(e, "cv9", v0 + 4 * tt + 4)
                    e.dma_start(out=S["y"][tt], in_=yt[tt % 2][:]).then_inc(sems.h[f"sy{tt % 2}"], 16)
                sems.n["ca9"] = a0 + 2 * NTT
                sems.n["sy0"] = y0p[0] + 16 * (NTT // 2)
                sems.n["sy1"] = y0p[1] + 16 * (NTT // 2)

        with nc.Block() as b:
            @b.sync
            def _(e):
                sems.w(e, "sy0", sems.n["sy0"])
                sems.w(e, "sy1", sems.n["sy1"])


def emit_dummy_tail(S):
    nc, sems = S["nc"], S["sems"]
    with nc.Block() as b:
        @b.sync
        def _(e):
            sems.w(e, "sr1", sems.n["sr1"])
            for tt in range(NT // 128):
                e.dma_start(out=S["y"][tt], in_=S["r1n_hbm"][tt]).then_inc(sems.h["sy0"], 16)
            sems.n["sy0"] += 16 * (NT // 128)
            sems.w(e, "sy0", sems.n["sy0"])


def host_in_maps(inputs):
    x = np.asarray(inputs["L0_attn_x"])
    w_qkv = np.asarray(inputs["L0_attn_w_qkv"])
    w_conv = np.asarray(inputs["L0_attn_w_conv"])
    w_fc = np.asarray(inputs["L0_attn_w_fc"])
    b_fc = np.asarray(inputs["L0_attn_b_fc"])
    sw1 = np.asarray(inputs["L0_ffn_shared_w1"])
    sw2 = np.asarray(inputs["L0_ffn_shared_w2"])
    ew1 = np.asarray(inputs["L0_ffn_expert_w1"])
    ew2 = np.asarray(inputs["L0_ffn_expert_w2"])
    rw = np.asarray(inputs["L0_ffn_router_w"])
    lng = np.asarray(inputs["L0_ffn_ln_gamma"])
    lnb = np.asarray(inputs["L0_ffn_ln_beta"])

    idbf = np.eye(128, dtype=BF)
    idf32 = np.eye(128, dtype=np.float32)
    onesbf = np.ones((128, 128), dtype=BF)
    identc = np.zeros((128, 6 * M), dtype=np.float32)
    for ci, coef in enumerate((7.0, 15.0, 3.25)):
        for pc in range(2):
            blk = np.zeros((128, M), dtype=np.float32)
            blk[np.arange(128), pc * 128 + np.arange(128)] = coef
            identc[:, ci * 2 * M + pc * M:(ci * 2 + pc + 1) * M] = blk
    identc = identc.astype(BF)

    maps = []
    for c in range(8):
        b, j = c // 2, c % 2
        hoff = 8 * j
        poff = 8 * (1 - j)
        xT = np.ascontiguousarray(x[b].T).reshape(8, 128, N)
        x_own = np.ascontiguousarray(x[b, j * NT:(j + 1) * NT]).reshape(NT // 128, 128, D)
        wq = np.concatenate([w_qkv[:, hoff * DH:hoff * DH + 512],
                             w_qkv[:, D + hoff * DH:D + hoff * DH + 512],
                             w_qkv[:, 2 * D + hoff * DH:2 * D + hoff * DH + 512]],
                            axis=1)
        wq = np.ascontiguousarray(wq).reshape(8, 128, 3 * NH * DH)
        wf = np.concatenate([w_fc[hoff * DH:hoff * DH + 512],
                             w_fc[poff * DH:poff * DH + 512]], axis=0)
        wf = np.ascontiguousarray(wf).reshape(8, 128, D)
        convd = np.zeros((4, CK, 128, 128), dtype=np.float32)
        for p in range(4):
            for jj in range(CK):
                dv = np.concatenate([np.full(64, w_conv[hoff + 2 * p, 0, jj, 0], np.float32),
                                     np.full(64, w_conv[hoff + 2 * p + 1, 0, jj, 0], np.float32)])
                convd[p, jj, np.arange(128), np.arange(128)] = dv
        meta = np.array([[(1 - j) * NT, 1 - j, j * NT, 0, 0, 0, 0, 0]], dtype=np.int32)
        maps.append({
            "xT": xT, "x_own": x_own, "wqkv": wq, "wfc": wf,
            "bfc_rep": np.ascontiguousarray(np.broadcast_to(b_fc, (128, D))),
            "lng_rep": np.ascontiguousarray(np.broadcast_to(lng, (128, D))),
            "lnb_rep": np.ascontiguousarray(np.broadcast_to(lnb, (128, D))),
            "convd": convd, "identc": identc, "idbf": idbf, "idf32": idf32,
            "onesbf": onesbf, "onesf": np.ones((128, 128), dtype=np.float32), "meta": meta,
            "rwT": np.ascontiguousarray(rw.T).reshape(8, 128, NEXP),
            "sw1": sw1.reshape(FD // 128, 128, D),
            "sw2": sw2.reshape(D // 128, 128, FD),
            "ew1": ew1.reshape(NEXP, FD // 128, 128, D),
            "ew2": ew2.reshape(NEXP, D // 128, 128, FD),
        })
    return maps


def kernel(**inputs):
    from concourse.bass_utils import run_bass_kernel_spmd
    nc = build(n_pass=4, do_ffn=True)
    nc.finalize()
    maps = host_in_maps(inputs)
    res = run_bass_kernel_spmd(nc, maps, core_ids=list(range(8)))
    out = np.zeros((4, N, D), dtype=np.float32)
    for c in range(8):
        b, j = c // 2, c % 2
        out[b, j * NT:(j + 1) * NT] = res.results[c]["y"].reshape(NT, D)
    return out


# version 02
# speedup vs baseline: 1.0346x; 1.0346x over previous
import math
from contextlib import ExitStack

import numpy as np
import ml_dtypes

import concourse.bass as bass
import concourse.bacc as bacc
import concourse.mybir as mybir

F32 = mybir.dt.float32
BF16 = mybir.dt.bfloat16
I32 = mybir.dt.int32
AF = mybir.ActivationFunctionType
ALU = mybir.AluOpType
AX = mybir.AxisListType
DS = bass.ds

N, D, NH, DH, M = 8192, 1024, 8, 64, 256
MP_ITERS, CK = 6, 13
SCL = DH ** -0.5
NT = 4096
FD, NEXP, EPS = 2048, 8, 1e-5
BF = ml_dtypes.bfloat16


class Sems:
    def __init__(self, nc, st, names):
        self.h = {nm: st.enter_context(nc.semaphore(nm)) for nm in names}
        self.n = {nm: 0 for nm in names}

    def inc(self, instr, nm, k=1):
        instr.then_inc(self.h[nm], k)
        self.n[nm] += k

    def w(self, eng, nm, v):
        eng.wait_ge(self.h[nm], v)


SEM_NAMES = ["sx", "cc",
             "cvw", "cvx", "cvl", "cvq", "mmq",
             "mm1", "ca1", "cve", "sgn", "sgo",
             "mm2", "ca2", "mm3", "ca3", "cv3",
             "mm4", "cv4", "mm5", "cv5", "mm6", "cv6",
             "mm7", "cv7",
             "mm8", "cv8", "ca8", "sx8",
             "cv9", "mm9", "ca9", "sw9",
             "swq0", "swq1", "swl0", "swl1", "sxx0", "sxx1", "scv",
             "sxv", "sl3", "sst0", "sst1", "sci",
             "sfc0", "sfc1", "sr1", "shf",
             "rwl0", "rwl1", "scw0", "scw1", "sxw", "shh0", "shh1", "sgr",
             "gda", "gdb", "sac0", "sac1", "sy0", "sy1"]


def build(n_pass=4, do_ffn=True, dumps=(), dumpreq=()):
    nc = bacc.Bacc("TRN2", target_bir_lowering=False)
    st = ExitStack()

    S = {"nc": nc, "st": st}
    S["xT"] = nc.dram_tensor("xT", [8, 128, N], F32, kind="ExternalInput").ap()
    S["x_own"] = nc.dram_tensor("x_own", [NT // 128, 128, D], F32, kind="ExternalInput").ap()
    S["wqkv"] = nc.dram_tensor("wqkv", [8, 128, 3 * NH * DH], F32, kind="ExternalInput").ap()
    S["wfc"] = nc.dram_tensor("wfc", [8, 128, D], F32, kind="ExternalInput").ap()
    S["bfc_rep"] = nc.dram_tensor("bfc_rep", [128, D], F32, kind="ExternalInput").ap()
    S["lng_rep"] = nc.dram_tensor("lng_rep", [128, D], F32, kind="ExternalInput").ap()
    S["lnb_rep"] = nc.dram_tensor("lnb_rep", [128, D], F32, kind="ExternalInput").ap()
    S["convd"] = nc.dram_tensor("convd", [4, CK, 128, 128], F32, kind="ExternalInput").ap()
    S["identc"] = nc.dram_tensor("identc", [128, 6 * M], BF16, kind="ExternalInput").ap()
    S["idbf"] = nc.dram_tensor("idbf", [128, 128], BF16, kind="ExternalInput").ap()
    S["idf32"] = nc.dram_tensor("idf32", [128, 128], F32, kind="ExternalInput").ap()
    S["onesbf"] = nc.dram_tensor("onesbf", [128, 128], BF16, kind="ExternalInput").ap()
    S["onesf"] = nc.dram_tensor("onesf", [128, 128], F32, kind="ExternalInput").ap()
    S["meta"] = nc.dram_tensor("meta", [1, 8], I32, kind="ExternalInput").ap()
    S["rwT"] = nc.dram_tensor("rwT", [8, 128, NEXP], F32, kind="ExternalInput").ap()
    S["sw1"] = nc.dram_tensor("sw1", [FD // 128, 128, D], F32, kind="ExternalInput").ap()
    S["sw2"] = nc.dram_tensor("sw2", [D // 128, 128, FD], F32, kind="ExternalInput").ap()
    S["ew1"] = nc.dram_tensor("ew1", [NEXP, FD // 128, 128, D], F32, kind="ExternalInput").ap()
    S["ew2"] = nc.dram_tensor("ew2", [NEXP, D // 128, 128, FD], F32, kind="ExternalInput").ap()
    S["y"] = nc.dram_tensor("y", [NT // 128, 128, D], F32, kind="ExternalOutput").ap()

    S["outall"] = nc.dram_tensor("outall", [4, 128, N], BF16, kind="Internal").ap()
    S["cin"] = nc.dram_tensor("cin", [4, 128, NT], BF16, kind="Internal").ap()
    S["cout"] = nc.dram_tensor("cout", [2, 4, 128, NT], BF16, kind="Internal").ap()
    S["gin"] = nc.dram_tensor("gin", [1, 2], F32, kind="Internal").ap()
    S["gout"] = nc.dram_tensor("gout", [1, 2], F32, kind="Internal").ap()
    S["r1n_hbm"] = nc.dram_tensor("r1n_hbm", [NT // 128, 128, D], F32, kind="Internal").ap()
    S["h_hbm"] = nc.dram_tensor("h_hbm", [8, 128, NT], BF16, kind="Internal").ap()
    S["w1scr"] = nc.dram_tensor("w1scr", [2, 8, FD, 128], BF16, kind="Internal").ap()
    S["w2scr"] = nc.dram_tensor("w2scr", [2, 16, D, 128], BF16, kind="Internal").ap()
    S["acc_hbm"] = nc.dram_tensor("acc_hbm", [8, 128, NT], F32, kind="Internal").ap()
    S["grows_hbm"] = nc.dram_tensor("grows_hbm", [8, NT], F32, kind="Internal").ap()

    S["dump"] = {}
    S["dumpreq"] = dumpreq

    sb = lambda *a, **k: st.enter_context(nc.sbuf_tensor(*a, **k))
    S["mt"] = sb("mt", [1, 8], I32)
    S["id_bf"] = sb("id_bf", [128, 128], BF16)
    S["id_f32"] = sb("id_f32", [128, 128], F32)
    S["ones_bf"] = sb("ones_bf", [128, 128], BF16)
    S["ones_f32"] = sb("ones_f32", [128, 128], F32)
    S["identc_sb"] = sb("identc_sb", [128, 6 * M], BF16)
    ast = ExitStack()
    S["ql_all"] = ast.enter_context(nc.sbuf_tensor("ql_all", [128, NH * M], BF16, side="right"))
    S["kl_all"] = ast.enter_context(nc.sbuf_tensor("kl_all", [128, NH * M], BF16, side="right"))
    S["a2T_all"] = ast.enter_context(nc.sbuf_tensor("a2T_all", [128, NH * 2 * M], BF16, side="right"))
    S["junk"] = sb("junk", [1, 8], BF16)
    S["statv"] = sb("statv", [128, 8], F32)
    S["gsc"] = sb("gsc", [1, 8], F32)
    S["rden_rep"] = sb("rden_rep", [128, 1], F32)

    sems = Sems(nc, st, SEM_NAMES)
    S["sems"] = sems

    with nc.Block() as b:
        @b.sync
        def _(e):
            for dst, src in ((S["mt"], S["meta"]), (S["id_bf"], S["idbf"]),
                             (S["id_f32"], S["idf32"]), (S["ones_bf"], S["onesbf"]),
                             (S["ones_f32"], S["onesf"]),
                             (S["identc_sb"], S["identc"])):
                e.dma_start(out=dst[:], in_=src[:, :]).then_inc(sems.h["sx"], 16)
            sems.n["sx"] += 96

        @b.vector
        def _(e):
            sems.w(e, "sx", sems.n["sx"])
            e.memset(S["junk"][0:1, 0:4], 1.0)
            e.memset(S["junk"][0:1, 4:8], 0.0)
            e.memset(S["statv"][:], 0.0)
            e.memset(S["gsc"][0:1, 0:4], 1.0)
            e.memset(S["gsc"][0:1, 4:8], 0.0)

    S["v_oth"] = nc.values_load(S["mt"][0:1, 0:1], engines=[mybir.EngineType.SP, mybir.EngineType.Activation])
    S["v_slot"] = nc.values_load(S["mt"][0:1, 1:2], engines=[mybir.EngineType.SP])
    S["v_own"] = nc.values_load(S["mt"][0:1, 2:3], engines=[mybir.EngineType.SP])

    for p in range(n_pass):
        emit_pass(S, p)
    ast.close()
    emit_exchange(S)
    emit_fc_ln(S)
    if do_ffn:
        emit_ffn(S)
        emit_tail(S)
    else:
        emit_dummy_tail(S)

    if dumps:
        dt_ = {}
        for key in dumps:
            t = S[key].tensor
            dt_[key] = nc.dram_tensor("d_" + key, list(t.shape), t.dtype,
                                      kind="ExternalOutput").ap()
        with nc.Block() as b:
            @b.scalar
            def _(e):
                for key in dumps:
                    e.dma_start(out=dt_[key][...], in_=S[key][...]).then_inc(sems.h["scv"], 16)
                    sems.n["scv"] += 16
                sems.w(e, "scv", sems.n["scv"])

    st.close()
    return nc


def emit_pass(S, p):
    nc, sems = S["nc"], S["sems"]
    with ExitStack() as pst:
        sb = lambda *a, **k: pst.enter_context(nc.sbuf_tensor(*a, **k))
        ps_ = lambda *a, **k: pst.enter_context(nc.psum_tensor(*a, **k))
        qkv = [sb(f"qkv{p}_{i}", [128, N], BF16) for i in range(3)]
        cvb = sb(f"cvb{p}", [128, CK * 128], BF16)
        outp = sb(f"outp{p}", [128, N + 2 * (CK // 2)], BF16)
        ostg = [sb(f"ostg{p}_{i}", [128, 2048], BF16) for i in range(2)]
        pA = [ps_(f"pA{p}_{i}", [128, 512], F32) for i in range(8)]
        if p == 0:
            xlb = [sb(f"xlb_{i}", [128, M], BF16) for i in range(8)]
        inner = ExitStack()
        sbi = lambda *a, **k: inner.enter_context(nc.sbuf_tensor(*a, **k))
        xw = [sbi(f"xw{p}_{i}", [128, 2048], F32) for i in range(2)]
        xbg = [sbi(f"xbg{p}_{i}", [128, 2048], BF16) for i in range(8)]
        wqf = [sbi(f"wqf{p}_{i}", [128, 384], F32) for i in range(2)]
        wqb = [sbi(f"wqb{p}_{i}", [128, 384], BF16) for i in range(8)]
        cvf = sbi(f"cvf{p}", [128, CK * 128], F32)
        if p == 0:
            xl = [sbi(f"xl_{i}", [128, M], F32) for i in range(8)]
            wlf = [sbi(f"wlf_{i}", [128, 1024], F32) for i in range(2)]
            wlb = [sbi(f"wlb_{i}", [128, 1024], BF16) for i in range(8)]

        qc, kc, vc = 2 * p * DH, NH * DH + 2 * p * DH, 2 * NH * DH + 2 * p * DH
        cw0, cx0, cl0 = sems.n["cvw"], sems.n["cvx"], sems.n["cvl"]
        mq0, cq0 = sems.n["mmq"], sems.n["cvq"]
        q0 = [sems.n["swq0"], sems.n["swq1"]]
        x0 = [sems.n["sxx0"], sems.n["sxx1"]]
        v0s = sems.n["scv"]
        nwl = 8 if p == 0 else 0
        nwq_cast = 8 + nwl + 1

        with nc.Block() as b:
            @b.sync
            def _(e):
                for dc in range(8):
                    if dc >= 2:
                        sems.w(e, "cvw", cw0 + dc - 1)
                    for ci, c0 in enumerate((qc, kc, vc)):
                        e.dma_start(out=wqf[dc % 2][:, ci * 128:(ci + 1) * 128],
                                    in_=S["wqkv"][dc][:, c0:c0 + 128]).then_inc(sems.h[f"swq{dc % 2}"], 16)
                if p == 0:
                    for dc in range(8):
                        if dc >= 2:
                            sems.w(e, "cvw", cw0 + 8 + dc - 1)
                        e.dma_start(out=wlf[dc % 2][:],
                                    in_=S["wqkv"][dc][:, 0:1024]).then_inc(sems.h[f"swl{dc % 2}"], 16)
                for jj in range(CK):
                    e.dma_start(out=cvf[:, jj * 128:(jj + 1) * 128],
                                in_=S["convd"][p, jj]).then_inc(sems.h["scv"], 16)
                for g in range(4):
                    for dc in range(8):
                        i = g * 8 + dc
                        if i >= 2:
                            sems.w(e, "cvx", cx0 + i - 1)
                            if p == 0:
                                sems.w(e, "cvl", cl0 + i - 1)
                        e.dma_start(out=xw[i % 2][:],
                                    in_=S["xT"][dc][:, g * 2048:(g + 1) * 2048]).then_inc(sems.h[f"sxx{i % 2}"], 16)
                sems.n["swq0"] = q0[0] + 16 * 12
                sems.n["swq1"] = q0[1] + 16 * 12
                if p == 0:
                    sems.n["swl0"] += 16 * 4
                    sems.n["swl1"] += 16 * 4
                sems.n["scv"] = v0s + 16 * CK
                sems.n["sxx0"] = x0[0] + 16 * 16
                sems.n["sxx1"] = x0[1] + 16 * 16

            @b.vector
            def _(e):
                for dc in range(8):
                    sems.w(e, f"swq{dc % 2}", q0[dc % 2] + 16 * 3 * (dc // 2 + 1))
                    e.tensor_copy(wqb[dc][:], wqf[dc % 2][:]).then_inc(sems.h["cvw"], 1)
                if p == 0:
                    for dc in range(8):
                        sems.w(e, f"swl{dc % 2}", sems.n[f"swl{dc % 2}"] - 16 * 4 + 16 * (dc // 2 + 1))
                        e.tensor_copy(wlb[dc][:], wlf[dc % 2][:]).then_inc(sems.h["cvw"], 1)
                sems.w(e, "scv", v0s + 16 * CK)
                e.tensor_copy(cvb[:], cvf[:]).then_inc(sems.h["cvw"], 1)
                sems.n["cvw"] = cw0 + nwq_cast
                for g in range(4):
                    for dc in range(8):
                        i = g * 8 + dc
                        sems.w(e, f"sxx{i % 2}", x0[i % 2] + 16 * (i // 2 + 1))
                        if g > 0:
                            sems.w(e, "mmq", mq0 + 3 * g)
                        e.tensor_copy(xbg[dc][:], xw[i % 2][:]).then_inc(sems.h["cvx"], 1)
                        if p == 0:
                            e.reduce_sum(xl[dc][:, g * 64:(g + 1) * 64],
                                         xw[i % 2][:].rearrange("q (a w) -> q a w", w=32),
                                         axis=AX.X).then_inc(sems.h["cvl"], 1)
                    for cc in range(3):
                        k = g * 3 + cc
                        sems.w(e, "mmq", mq0 + k + 1)
                        for wi in range(4):
                            bank = (k % 2) * 4 + wi
                            li = e.tensor_copy(qkv[cc][:, g * 2048 + wi * 512:g * 2048 + (wi + 1) * 512],
                                               pA[bank][:])
                        li.then_inc(sems.h["cvq"], 1)
                sems.n["cvx"] = cx0 + 32
                sems.n["cvq"] = cq0 + 12
                if p == 0:
                    sems.n["cvl"] = cl0 + 32

            @b.tensor
            def _(e):
                sems.w(e, "cvw", cw0 + 8)
                for g in range(4):
                    sems.w(e, "cvx", cx0 + (g + 1) * 8)
                    for cc in range(3):
                        k = g * 3 + cc
                        if k >= 2:
                            sems.w(e, "cvq", cq0 + k - 1)
                        for dc in range(8):
                            e.ldweights(wqb[dc][:, cc * 128:(cc + 1) * 128])
                            for wi in range(4):
                                bank = (k % 2) * 4 + wi
                                li = e.matmul(pA[bank][:], wqb[dc][:, cc * 128:(cc + 1) * 128],
                                              xbg[dc][:, wi * 512:(wi + 1) * 512],
                                              start=(dc == 0), stop=(dc == 7))
                        li.then_inc(sems.h["mmq"], 1)
                sems.n["mmq"] = mq0 + 12

        if p == 0:
            for ci in range(3):
                dump_sb(S, f"qkv{p}_{ci}", qkv[ci][:])
            emit_attn2prep(S, xl, xlb, wlb, pA)
            dump_sb(S, "ql", S["ql_all"][:])
            dump_sb(S, "kl", S["kl_all"][:])
            dump_sb(S, "a2T", S["a2T_all"][:])
        inner.close()

        for hl in range(2):
            emit_head(S, p, hl, qkv, outp, pA)
        dump_sb(S, f"outp{p}", outp[:])
        emit_conv_store(S, p, outp, cvb, ostg, pA)


def emit_attn2prep(S, xl, xlb, wlb, pA):
    nc, sems = S["nc"], S["sems"]
    ql, kl, a2T = S["ql_all"], S["kl_all"], S["a2T_all"]
    ones = S["ones_bf"]
    with ExitStack() as pst:
        sb = lambda *a, **k: pst.enter_context(nc.sbuf_tensor(*a, **k))
        e2b = sb("e2b", [128, 2 * M], BF16)
        rl2 = sb("rl2", [128, M], F32)
        rsum = sb("rsum", [128, 1], F32)
        m0, a0, v0, x0 = sems.n["mm1"], sems.n["ca1"], sems.n["cve"], sems.n["cvx"]

        with nc.Block() as b:
            @b.vector
            def _(e):
                e.memset(ql[:], 0.0)
                e.memset(kl[:], 0.0)
                for dc in range(8):
                    e.tensor_scalar_mul(xlb[dc][:], xl[dc][:], 1.0 / 32.0).then_inc(sems.h["cvx"], 1)
                for h in range(NH):
                    ob = (h % 2) * 64
                    sems.w(e, "mm1", m0 + 6 * h + 2)
                    e.tensor_copy(ql[ob:ob + 64, h * M:(h + 1) * M], pA[0][ob:ob + 64, 0:M])
                    e.tensor_copy(kl[ob:ob + 64, h * M:(h + 1) * M],
                                  pA[1][ob:ob + 64, 0:M]).then_inc(sems.h["cve"], 1)
                    sems.w(e, "mm1", m0 + 6 * h + 5)
                    e.reciprocal(rl2[:], pA[4][:, 0:M])
                    e.drain()
                    e.tensor_mul(a2T[:, h * 2 * M:h * 2 * M + M], e2b[:, 0:M], rl2[:])
                    e.tensor_mul(a2T[:, h * 2 * M + M:(h + 1) * 2 * M], e2b[:, M:2 * M], rl2[:]).then_inc(sems.h["cve"], 1)
                    e.drain()
                    e.reduce_sum(rsum[:], a2T[:, h * 2 * M:h * 2 * M + M], axis=AX.X)
                    e.reduce_sum(S["statv"][:, 1:2], a2T[:, h * 2 * M + M:(h + 1) * 2 * M], axis=AX.X)
                    e.drain()
                    e.tensor_max(rsum[:], rsum[:], S["statv"][:, 1:2])
                    e.drain()
                    if h == 0:
                        e.tensor_copy(S["statv"][:, 0:1], rsum[:])
                    else:
                        e.tensor_max(S["statv"][:, 0:1], S["statv"][:, 0:1], rsum[:])
                    sems.w(e, "mm1", m0 + 6 * h + 6)
                    e.reduce_max(S["gsc"][0:1, 0:1] if h == 0 else S["gsc"][0:1, 2:3],
                                 pA[5][0:1, 0:M], axis=AX.X)
                    e.drain()
                    if h > 0:
                        li = e.tensor_max(S["gsc"][0:1, 0:1], S["gsc"][0:1, 0:1], S["gsc"][0:1, 2:3])
                    else:
                        li = e.engine_nop()
                    li.then_inc(sems.h["cve"], 1)
                sems.w(e, "mm1", m0 + 6 * NH + 1)
                e.reduce_max(S["gsc"][0:1, 1:2], pA[6][0:1, 0:128], axis=AX.X).then_inc(sems.h["cve"], 1)
                sems.n["cvx"] = x0 + 8
                sems.n["cve"] = v0 + 3 * NH + 1

            @b.tensor
            def _(e):
                sems.w(e, "cvx", x0 + 8)
                for h in range(NH):
                    ob = (h % 2) * 64
                    if h > 0:
                        sems.w(e, "cve", v0 + 3 * (h - 1) + 3)
                    for side, dst in ((0, 0), (1, 1)):
                        for dc in range(8):
                            wslice = wlb[dc][:, side * 512 + h * DH:side * 512 + (h + 1) * DH]
                            e.ldweights(wslice)
                            li = e.matmul(pA[dst][ob:ob + 64, 0:M], wslice, xlb[dc][:],
                                          start=(dc == 0), stop=(dc == 7))
                        li.then_inc(sems.h["mm1"], 1)
                    sems.w(e, "cve", v0 + 3 * h + 1)
                    for mc in range(2):
                        lw = kl[:, h * M + mc * 128:h * M + (mc + 1) * 128]
                        e.ldweights(lw)
                        e.matmul(pA[2 + mc][:, 0:M], lw, ql[:, h * M:(h + 1) * M],
                                 start=True, stop=True).then_inc(sems.h["mm1"], 1)
                    sems.w(e, "ca1", a0 + 2 * (h + 1))
                    e.ldweights(ones[:, :])
                    e.matmul(pA[4][:, 0:M], ones[:, :], e2b[:, 0:M], start=True, stop=False)
                    e.matmul(pA[4][:, 0:M], ones[:, :], e2b[:, M:2 * M],
                             start=False, stop=True).then_inc(sems.h["mm1"], 1)
                    sems.w(e, "cve", v0 + 3 * h + 2)
                    e.ldweights(ones[:, 0:64])
                    e.matmul(pA[5][0:64, 0:M], ones[:, 0:64], a2T[:, h * 2 * M:h * 2 * M + M],
                             start=True, stop=False)
                    e.matmul(pA[5][0:64, 0:M], ones[:, 0:64],
                             a2T[:, h * 2 * M + M:(h + 1) * 2 * M],
                             start=False, stop=True).then_inc(sems.h["mm1"], 1)
                sems.w(e, "cve", v0 + 3 * NH)
                e.transpose(pA[6][0:1, 0:128], S["statv"][:, 0:1],
                            S["id_f32"][:, :]).then_inc(sems.h["mm1"], 1)
                sems.n["mm1"] = m0 + 6 * NH + 1

            @b.scalar
            def _(e):
                for h in range(NH):
                    for mc in range(2):
                        sems.w(e, "mm1", m0 + 6 * h + 3 + mc)
                        if h > 0 and mc == 0:
                            sems.w(e, "cve", v0 + 3 * (h - 1) + 2)
                        e.activation(e2b[:, mc * M:(mc + 1) * M], pA[2 + mc][:, 0:M],
                                     AF.Exp, scale=SCL).then_inc(sems.h["ca1"], 1)
                sems.w(e, "cve", v0 + 3 * NH + 1)
                e.dma_start(out=S["gin"][:, :], in_=S["gsc"][0:1, 0:2]).then_inc(sems.h["sgn"], 16)
                sems.n["ca1"] = a0 + 2 * NH
                sems.n["sgn"] += 16

        with nc.Block() as b:
            @b.gpsimd
            def _(e):
                sems.w(e, "sgn", sems.n["sgn"])
                e.collective_compute("AllReduce", ALU.max,
                                     ins=[S["gin"][:, :]], outs=[S["gout"][:, :]],
                                     replica_groups=[[0, 1, 2, 3, 4, 5, 6, 7]]).then_inc(sems.h["cc"], 1)
                sems.n["cc"] += 1
                sems.w(e, "cc", sems.n["cc"])
                e.dma_start(out=S["gsc"][0:1, 4:6], in_=S["gout"][:, :]).then_inc(sems.h["sgo"], 16)
                sems.n["sgo"] += 16

        with nc.Block() as b:
            m1 = sems.n["mm1"]

            @b.vector
            def _(e):
                sems.w(e, "sgo", sems.n["sgo"])
                e.tensor_mul(S["gsc"][0:1, 6:7], S["gsc"][0:1, 4:5], S["gsc"][0:1, 5:6])
                e.drain()
                e.reciprocal(S["gsc"][0:1, 7:8], S["gsc"][0:1, 6:7]).then_inc(sems.h["cve"], 1)
                sems.w(e, "mm1", m1 + 1)
                e.tensor_copy(S["rden_rep"][:], pA[7][:, 0:1])
                sems.n["cve"] += 1

            @b.tensor
            def _(e):
                sems.w(e, "cve", sems.n["cve"])
                e.matmul(pA[7][:, 0:1], S["ones_f32"][0:1, :], S["gsc"][0:1, 7:8],
                         start=True, stop=True).then_inc(sems.h["mm1"], 1)
                sems.n["mm1"] = m1 + 1


def emit_head(S, p, hl, qkv, outp, pA):
    nc, sems = S["nc"], S["sems"]
    h = 2 * p + hl
    r0, r1 = hl * 64, hl * 64 + 64
    ql, kl, a2T = S["ql_all"], S["kl_all"], S["a2T_all"]
    ones, idc = S["ones_bf"], S["identc_sb"]

    with ExitStack() as hst:
        sb = lambda *a, **k: hst.enter_context(nc.sbuf_tensor(*a, **k))
        vaug = sb(f"vaug{h}", [128, 128 * 64], BF16, is_transpose_dest=True)
        exp1 = [sb(f"exp1_{h}_{i}", [128, N], BF16) for i in range(2)]
        e3b = [sb(f"e3b{h}_{i}", [128, M], BF16) for i in range(2)]
        t1f = sb(f"t1f{h}", [65, M], F32)
        t1id = [sb(f"t1id{h}_{i}", [128, 65], BF16) for i in range(2)]
        l3r = sb(f"l3r{h}", [1, M], F32)
        rl3r = sb(f"rl3r{h}", [1, M], F32)
        rl3pp = sb(f"rl3pp{h}", [128, 2], F32)
        a2f = sb(f"a2f{h}", [128, 512], F32)
        a2b = sb(f"a2b{h}", [128, 512], BF16)
        zb = sb(f"zb{h}", [128, 512], BF16)
        ztb = sb(f"ztb{h}", [128, 512], BF16)
        aztb = sb(f"aztb{h}", [128, 512], BF16)
        mpb = sb(f"mpb{h}", [128, 512], BF16)
        zstb = sb(f"zstb{h}", [128, 512], BF16)
        t2b = [sb(f"t2b{h}_{i}", [128, 64], BF16) for i in range(2)]
        rl1 = sb(f"rl1{h}", [128, 512], F32)

        a2s = a2T[:, h * 2 * M:(h + 1) * 2 * M]

        sx0, m20, a20 = sems.n["sxv"], sems.n["mm2"], sems.n["ca2"]
        with nc.Block() as b:
            @b.vector
            def _(e):
                if hl == 0:
                    e.memset(outp[:, 0:CK // 2], 0.0)
                    e.memset(outp[:, CK // 2 + N:CK // 2 + N + CK // 2], 0.0)
                e.memset(vaug[:], 1.0).then_inc(sems.h["cv3"], 1)

            @b.sync
            def _(e):
                sems.w(e, "cv3", sems.n["cv3"] + 1)
                for c in range(64):
                    e.dma_start_transpose(out=vaug[:, c * 128:c * 128 + 64],
                                          in_=qkv[2][r0:r1, c * 128:(c + 1) * 128]).then_inc(sems.h["sxv"], 16)
                sems.n["sxv"] = sx0 + 64 * 16
            sems.n["cv3"] += 1

            @b.tensor
            def _(e):
                for wg in range(4):
                    for mc in range(2):
                        j = 2 * wg + mc
                        if j >= 2:
                            sems.w(e, "ca2", a20 + j - 1)
                        lw = kl[:, h * M + mc * 128:h * M + (mc + 1) * 128]
                        e.ldweights(lw)
                        for wi in range(4):
                            w = wg * 4 + wi
                            li = e.matmul(pA[mc * 4 + wi][:], lw,
                                          qkv[0][:, w * 512:(w + 1) * 512],
                                          start=True, stop=True)
                        li.then_inc(sems.h["mm2"], 1)
                sems.n["mm2"] = m20 + 8

            @b.scalar
            def _(e):
                for wg in range(4):
                    for mc in range(2):
                        j = 2 * wg + mc
                        sems.w(e, "mm2", m20 + j + 1)
                        for wi in range(4):
                            w = wg * 4 + wi
                            li = e.activation(exp1[mc][:, w * 512:(w + 1) * 512],
                                              pA[mc * 4 + wi][:], AF.Exp, scale=SCL)
                        li.then_inc(sems.h["ca2"], 1)
                sems.n["ca2"] = a20 + 8

        dump_sb(S, f"vaug{h}", vaug[:])
        dump_sb(S, f"exp1a{h}", exp1[0][:])
        m30, a30 = sems.n["mm3"], sems.n["ca3"]
        with nc.Block() as b:
            @b.tensor
            def _(e):
                sems.w(e, "sxv", sems.n["sxv"])
                for c in range(64):
                    if c >= 2:
                        sems.w(e, "ca3", a30 + c - 1)
                    lw = qkv[1][:, c * 128:(c + 1) * 128]
                    e.ldweights(lw)
                    e.matmul(pA[c % 2][:, 0:M], lw, ql[:, h * M:(h + 1) * M],
                             start=True, stop=True).then_inc(sems.h["mm3"], 1)
                    sems.w(e, "ca3", a30 + c + 1)
                    e.ldweights(vaug[:, c * 128:c * 128 + 65])
                    e.matmul(pA[2][0:65, 0:M], vaug[:, c * 128:c * 128 + 65], e3b[c % 2][:],
                             start=(c == 0), stop=(c == 63)).then_inc(sems.h["mm3"], 1)
                sems.n["mm3"] = m30 + 128

            @b.scalar
            def _(e):
                for c in range(64):
                    sems.w(e, "mm3", m30 + 2 * c + 1)
                    e.activation(e3b[c % 2][:], pA[c % 2][:, 0:M],
                                 AF.Exp, scale=SCL).then_inc(sems.h["ca3"], 1)
                sems.n["ca3"] = a30 + 64

        m40, v40, w40 = sems.n["mm4"], sems.n["cv4"], sems.n["sl3"]
        with nc.Block() as b:
            @b.vector
            def _(e):
                e.tensor_copy(t1f[:], pA[2][0:65, 0:M]).then_inc(sems.h["cv4"], 1)
                e.tensor_copy(a2f[:], a2s).then_inc(sems.h["cv4"], 1)
                sems.w(e, "sl3", w40 + 16)
                e.reciprocal(rl3r[:], l3r[:]).then_inc(sems.h["cv4"], 1)
                sems.w(e, "mm4", m40 + 2)
                e.tensor_copy(rl3pp[:, 0:1], pA[3][:, 0:1])
                e.tensor_copy(rl3pp[:, 1:2], pA[3][:, 1:2]).then_inc(sems.h["cv4"], 1)
                sems.w(e, "mm4", m40 + 4)
                e.tensor_copy(t1id[0][:], pA[4][:, 0:65])
                e.tensor_copy(t1id[1][:], pA[4][:, 65:130]).then_inc(sems.h["cv4"], 1)
                sems.w(e, "mm4", m40 + 8)
                for pc in range(2):
                    for rc in range(2):
                        li = e.tensor_copy(a2b[:, pc * 256 + rc * 128:pc * 256 + (rc + 1) * 128],
                                           pA[5 + pc][:, rc * 128:(rc + 1) * 128])
                li.then_inc(sems.h["cv4"], 1)
                sems.n["cv4"] = v40 + 6

            @b.scalar
            def _(e):
                sems.w(e, "cv4", v40 + 1)
                e.dma_start(out=l3r[0:1, :], in_=t1f[64:65, :]).then_inc(sems.h["sl3"], 16)
                sems.n["sl3"] = w40 + 16

            @b.tensor
            def _(e):
                sems.w(e, "cv4", v40 + 3)
                e.transpose(pA[3][:, 0:1], rl3r[0:1, 0:128], S["id_f32"][0:1, 0:1]).then_inc(sems.h["mm4"], 1)
                e.transpose(pA[3][:, 1:2], rl3r[0:1, 128:256], S["id_f32"][0:1, 0:1]).then_inc(sems.h["mm4"], 1)
                e.transpose(pA[4][:, 0:65], t1f[0:65, 0:128], S["id_f32"][0:65, 0:65]).then_inc(sems.h["mm4"], 1)
                e.transpose(pA[4][:, 65:130], t1f[0:65, 128:256], S["id_f32"][0:65, 0:65]).then_inc(sems.h["mm4"], 1)
                sems.w(e, "cv4", v40 + 2)
                for pc in range(2):
                    for rc in range(2):
                        e.transpose(pA[5 + pc][:, rc * 128:(rc + 1) * 128],
                                    a2f[:, rc * 256 + pc * 128:rc * 256 + (pc + 1) * 128],
                                    S["id_f32"][:, :]).then_inc(sems.h["mm4"], 1)
                sems.n["mm4"] = m40 + 8

        dump_sb(S, f"t1f{h}", t1f[:])
        dump_sb(S, f"rl3pp{h}", rl3pp[:])
        dump_sb(S, f"a2b{h}", a2b[:])
        m50, v50 = sems.n["mm5"], sems.n["cv5"]
        with nc.Block() as b:
            @b.vector
            def _(e):
                e.tensor_scalar_mul(zb[:], a2s, S["rden_rep"][:, 0:1])
                e.tensor_scalar_mul(ztb[:], a2b[:], S["rden_rep"][:, 0:1]).then_inc(sems.h["cv5"], 1)
                for i in range(MP_ITERS):
                    last = i == MP_ITERS - 1
                    base = m50 + 12 * i
                    vb = v50 + 1 + 6 * i
                    sems.w(e, "mm5", base + 2)
                    for pc in range(2):
                        li = e.scalar_tensor_tensor(mpb[:, pc * 256:(pc + 1) * 256],
                                               pA[pc][:, 0:M], -1.0,
                                               idc[:, 0 * 512 + pc * 256:0 * 512 + (pc + 1) * 256],
                                               ALU.mult, ALU.add)
                    li.then_inc(sems.h["cv5"], 1)
                    sems.w(e, "mm5", base + 4)
                    for pc in range(2):
                        li = e.tensor_copy(aztb[:, pc * 256:(pc + 1) * 256], pA[2 + pc][:, 0:M])
                    li.then_inc(sems.h["cv5"], 1)
                    sems.w(e, "mm5", base + 6)
                    for pc in range(2):
                        li = e.scalar_tensor_tensor(mpb[:, pc * 256:(pc + 1) * 256],
                                               pA[4 + pc][:, 0:M], -1.0,
                                               idc[:, 1 * 512 + pc * 256:1 * 512 + (pc + 1) * 256],
                                               ALU.mult, ALU.add)
                    li.then_inc(sems.h["cv5"], 1)
                    sems.w(e, "mm5", base + 8)
                    for pc in range(2):
                        li = e.scalar_tensor_tensor(mpb[:, pc * 256:(pc + 1) * 256],
                                               pA[4 + pc][:, 0:M], -0.25,
                                               idc[:, 2 * 512 + pc * 256:2 * 512 + (pc + 1) * 256],
                                               ALU.mult, ALU.add)
                    li.then_inc(sems.h["cv5"], 1)
                    sems.w(e, "mm5", base + 10)
                    li = None
                    if not last:
                        for pc in range(2):
                            li = e.tensor_copy(zb[:, pc * 256:(pc + 1) * 256], pA[6 + pc][:, 0:M])
                    (li if li is not None else e.engine_nop()).then_inc(sems.h["cv5"], 1)
                    sems.w(e, "mm5", base + 12)
                    for pc in range(2):
                        if last:
                            li = e.tensor_scalar_mul(zstb[:, pc * 256:(pc + 1) * 256],
                                                     pA[0 + pc][:, 0:M], rl3pp[:, pc:pc + 1])
                        else:
                            li = e.tensor_copy(ztb[:, pc * 256:(pc + 1) * 256], pA[0 + pc][:, 0:M])
                    li.then_inc(sems.h["cv5"], 1)
                sems.w(e, "mm5", m50 + 12 * MP_ITERS + 2)
                for mc in range(2):
                    li = e.tensor_copy(t2b[mc][:], pA[2 + mc][:, 0:64])
                li.then_inc(sems.h["cv5"], 1)
                sems.n["cv5"] = v50 + 2 + 6 * MP_ITERS

            @b.tensor
            def _(e):
                sems.w(e, "cv5", v50 + 1)
                for i in range(MP_ITERS):
                    base = m50 + 12 * i
                    vb = v50 + 1 + 6 * i
                    if i > 0:
                        sems.w(e, "cv5", vb)
                    for rc in range(2):
                        for mc in range(2):
                            lw = a2s[:, mc * 256 + rc * 128:mc * 256 + rc * 128 + 128]
                            e.ldweights(lw)
                            li = e.matmul(pA[rc][:, 0:M], lw, zb[:, mc * 256:(mc + 1) * 256],
                                     start=(mc == 0), stop=(mc == 1))
                        li.then_inc(sems.h["mm5"], 1)
                    for ic in range(2):
                        for mc in range(2):
                            lw = zb[:, mc * 256 + ic * 128:mc * 256 + ic * 128 + 128]
                            e.ldweights(lw)
                            li = e.matmul(pA[2 + ic][:, 0:M], lw, a2s[:, mc * 256:(mc + 1) * 256],
                                     start=(mc == 0), stop=(mc == 1))
                        li.then_inc(sems.h["mm5"], 1)
                    sems.w(e, "cv5", vb + 2)
                    for rc in range(2):
                        for ic in range(2):
                            lw = aztb[:, ic * 256 + rc * 128:ic * 256 + rc * 128 + 128]
                            e.ldweights(lw)
                            li = e.matmul(pA[4 + rc][:, 0:M], lw, mpb[:, ic * 256:(ic + 1) * 256],
                                     start=(ic == 0), stop=(ic == 1))
                        li.then_inc(sems.h["mm5"], 1)
                    sems.w(e, "cv5", vb + 3)
                    for rc in range(2):
                        for ic in range(2):
                            lw = aztb[:, ic * 256 + rc * 128:ic * 256 + rc * 128 + 128]
                            e.ldweights(lw)
                            li = e.matmul(pA[4 + rc][:, 0:M], lw, mpb[:, ic * 256:(ic + 1) * 256],
                                     start=(ic == 0), stop=(ic == 1))
                        li.then_inc(sems.h["mm5"], 1)
                    sems.w(e, "cv5", vb + 4)
                    for rc in range(2):
                        for ic in range(2):
                            lw = ztb[:, ic * 256 + rc * 128:ic * 256 + rc * 128 + 128]
                            e.ldweights(lw)
                            li = e.matmul(pA[6 + rc][:, 0:M], lw, mpb[:, ic * 256:(ic + 1) * 256],
                                     start=(ic == 0), stop=(ic == 1))
                        li.then_inc(sems.h["mm5"], 1)
                    for jc in range(2):
                        for ic in range(2):
                            lw = mpb[:, ic * 256 + jc * 128:ic * 256 + jc * 128 + 128]
                            e.ldweights(lw)
                            li = e.matmul(pA[0 + jc][:, 0:M], lw, ztb[:, ic * 256:(ic + 1) * 256],
                                     start=(ic == 0), stop=(ic == 1))
                        li.then_inc(sems.h["mm5"], 1)
                sems.w(e, "cv5", v50 + 1 + 6 * MP_ITERS)
                for mc in range(2):
                    for ic in range(2):
                        lw = zstb[:, ic * 256 + mc * 128:ic * 256 + mc * 128 + 128]
                        e.ldweights(lw)
                        li = e.matmul(pA[2 + mc][:, 0:64], lw, t1id[ic][:, 0:64],
                                 start=(ic == 0), stop=(ic == 1))
                    li.then_inc(sems.h["mm5"], 1)
                sems.n["mm5"] = m50 + 12 * MP_ITERS + 2

        dump_sb(S, f"zstb{h}", zstb[:])
        dump_sb(S, f"t2b{h}", t2b[0][:])
        m60, v60 = sems.n["mm6"], sems.n["cv6"]
        ob = hl * 64
        with nc.Block() as b:
            @b.tensor
            def _(e):
                for wg in range(4):
                    if wg >= 1:
                        sems.w(e, "cv6", v60 + wg)
                    for mc in range(2):
                        e.ldweights(t2b[mc][:])
                        for wi in range(4):
                            w = wg * 4 + wi
                            e.matmul(pA[wi][ob:ob + 64, :], t2b[mc][:],
                                     exp1[mc][:, w * 512:(w + 1) * 512],
                                     start=(mc == 0), stop=(mc == 1))
                    e.ldweights(ones[:, 0:64])
                    for wi in range(4):
                        w = wg * 4 + wi
                        e.matmul(pA[4 + wi][ob:ob + 64, :], ones[:, 0:64],
                                 exp1[0][:, w * 512:(w + 1) * 512], start=True, stop=False)
                        li = e.matmul(pA[4 + wi][ob:ob + 64, :], ones[:, 0:64],
                                 exp1[1][:, w * 512:(w + 1) * 512], start=False, stop=True)
                    li.then_inc(sems.h["mm6"], 1)
                sems.n["mm6"] = m60 + 4

            @b.vector
            def _(e):
                for wg in range(4):
                    sems.w(e, "mm6", m60 + wg + 1)
                    for wi in range(4):
                        w = wg * 4 + wi
                        e.reciprocal(rl1[ob:ob + 64, :], pA[4 + wi][ob:ob + 64, :])
                        e.drain()
                        li = e.tensor_mul(outp[ob:ob + 64, CK // 2 + w * 512:CK // 2 + (w + 1) * 512],
                                     pA[wi][ob:ob + 64, :], rl1[ob:ob + 64, :])
                        e.drain()
                    li.then_inc(sems.h["cv6"], 1)
                sems.n["cv6"] = v60 + 4


def dump_sb(S, name, ap):
    if name not in S.get("dumpreq", ()):
        return
    nc, sems = S["nc"], S["sems"]
    t = nc.dram_tensor("d_" + name, list(ap.shape), ap.dtype, kind="ExternalOutput").ap()
    with nc.Block() as b:
        @b.scalar
        def _(e):
            e.dma_start(out=t[...], in_=ap).then_inc(sems.h["scv"], 16)
            sems.n["scv"] += 16
            sems.w(e, "scv", sems.n["scv"])


def maybe_dump(S, name, src_ap):
    if name not in S["dump"]:
        return
    nc, sems = S["nc"], S["sems"]
    with nc.Block() as b:
        @b.scalar
        def _(e):
            e.dma_start(out=S["dump"][name][...], in_=src_ap).then_inc(sems.h["scv"], 16)
            sems.n["scv"] += 16
            sems.w(e, "scv", sems.n["scv"])


def emit_conv_store(S, p, outp, cvb, ostg, pA):
    nc, sems = S["nc"], S["sems"]
    m70, v70 = sems.n["mm7"], sems.n["cv7"]
    s70 = [sems.n["sst0"], sems.n["sst1"]]
    with nc.Block() as b:
        @b.tensor
        def _(e):
            for wg in range(4):
                if wg >= 2:
                    sems.w(e, "cv7", v70 + wg - 1)
                for j in range(CK):
                    lw = cvb[:, j * 128:(j + 1) * 128]
                    e.ldweights(lw)
                    for wi in range(4):
                        w = wg * 4 + wi
                        li = e.matmul(pA[(wg % 2) * 4 + wi][:],
                                 lw, outp[:, w * 512 + j:w * 512 + j + 512],
                                 start=(j == 0), stop=(j == CK - 1))
                li.then_inc(sems.h["mm7"], 1)
            sems.n["mm7"] = m70 + 4

        @b.vector
        def _(e):
            for wg in range(4):
                sems.w(e, "mm7", m70 + wg + 1)
                if wg >= 2:
                    sems.w(e, f"sst{wg % 2}", s70[wg % 2] + 16)
                for wi in range(4):
                    w = wg * 4 + wi
                    li = e.scalar_tensor_tensor(ostg[wg % 2][:, wi * 512:(wi + 1) * 512],
                                           pA[(wg % 2) * 4 + wi][:], 0.0,
                                           outp[:, CK // 2 + w * 512:CK // 2 + (w + 1) * 512],
                                           ALU.add, ALU.add)
                li.then_inc(sems.h["cv7"], 1)
            sems.n["cv7"] = v70 + 4

        @b.scalar
        def _(e):
            for wg in range(4):
                sems.w(e, "cv7", v70 + wg + 1)
                e.dma_start(out=S["outall"][p][:, wg * 2048:(wg + 1) * 2048],
                            in_=ostg[wg % 2][:]).then_inc(sems.h[f"sst{wg % 2}"], 16)
            sems.w(e, "sst0", s70[0] + 32)
            sems.w(e, "sst1", s70[1] + 32)
            e.dma_start(out=S["cin"][p][:, :],
                        in_=S["outall"][p][:, DS(S["v_oth"], NT)]).then_inc(sems.h["sci"], 16)
            sems.n["sst0"] = s70[0] + 32
            sems.n["sst1"] = s70[1] + 32
            sems.n["sci"] += 16


def emit_exchange(S):
    nc, sems = S["nc"], S["sems"]
    with nc.Block() as b:
        @b.gpsimd
        def _(e):
            sems.w(e, "sci", sems.n["sci"])
            e.collective_compute("AllGather", ALU.bypass,
                                 ins=[S["cin"][:, :, :]], outs=[S["cout"][:, :, :, :]],
                                 replica_groups=[[0, 1], [2, 3], [4, 5], [6, 7]]).then_inc(sems.h["cc"], 1)
            sems.n["cc"] += 1


def emit_fc_ln(S):
    nc, sems = S["nc"], S["sems"]
    NTT = NT // 128
    RS = AF.Abs_reciprocal_sqrt
    with ExitStack() as pst:
        sb = lambda *a, **k: pst.enter_context(nc.sbuf_tensor(*a, **k))
        ps_ = lambda *a, **k: pst.enter_context(nc.psum_tensor(*a, **k))
        wff = [sb(f"wff{i}", [128, D], F32) for i in range(2)]
        wfb = [sb(f"wfb{i}", [128, D], BF16) for i in range(8)]
        rwf = sb("rwf", [128, NEXP], F32)
        rwb = [sb(f"rwb{i}", [128, NEXP], BF16) for i in range(8)]
        bfc_sb = sb("bfc_sb", [128, D], F32)
        lng_sb = sb("lng_sb", [128, D], F32)
        lnb_sb = sb("lnb_sb", [128, D], F32)
        lt = [sb(f"lt{i}", [128, 128], BF16) for i in range(16)]
        xo = [sb(f"xo{i}", [128, D], F32) for i in range(2)]
        res_t = sb("res_t", [128, D], F32)
        sq_t = sb("sq_t", [128, D], F32)
        r1t = sb("r1t", [128, D], F32)
        hh = sb("hh", [128, D], F32)
        hb = sb("hb", [128, D], BF16)
        hstg = sb("hstg", [128, 8192], BF16, is_transpose_dest=True)
        lsb = sb("lsb", [8, 128], F32)
        logall = sb("logall", [128, NEXP * NTT], F32)
        el = sb("el", [128, NEXP], F32)
        stt_ = sb("stt_", [128, 8], F32)
        gt = sb("gt", [128, NEXP], F32)
        gates8 = sb("gates8", [8, NT], F32)
        pF = [ps_(f"pF{i}", [128, 512], F32) for i in range(8)]

        v0a = sems.n["cv8"]
        q0f = [sems.n["swq0"], sems.n["swq1"]]
        g0f, c0f = sems.n["sgr"], sems.n["scv"]
        with nc.Block() as b:
            @b.sync
            def _(e):
                for dc in range(8):
                    if dc >= 2:
                        sems.w(e, "cv8", v0a + dc - 1)
                    e.dma_start(out=wff[dc % 2][:], in_=S["wfc"][dc]).then_inc(sems.h[f"swq{dc % 2}"], 16)
                e.dma_start(out=bfc_sb[:], in_=S["bfc_rep"][:, :]).then_inc(sems.h["scv"], 16)
                e.dma_start(out=lng_sb[:], in_=S["lng_rep"][:, :]).then_inc(sems.h["scv"], 16)
                e.dma_start(out=lnb_sb[:], in_=S["lnb_rep"][:, :]).then_inc(sems.h["scv"], 16)
                for dc in range(8):
                    sems.w(e, "cv8", v0a + 8 + dc)
                    e.dma_start(out=rwf[:], in_=S["rwT"][dc]).then_inc(sems.h["sgr"], 16)
                sems.n["swq0"] = q0f[0] + 16 * 4
                sems.n["swq1"] = q0f[1] + 16 * 4
                sems.n["scv"] = c0f + 16 * 3
                sems.n["sgr"] = g0f + 16 * 8
                sems.w(e, "scv", sems.n["scv"])

            @b.vector
            def _(e):
                for dc in range(8):
                    sems.w(e, f"swq{dc % 2}", q0f[dc % 2] + 16 * (dc // 2 + 1))
                    e.tensor_copy(wfb[dc][:], wff[dc % 2][:]).then_inc(sems.h["cv8"], 1)
                for dc in range(8):
                    sems.w(e, "sgr", g0f + 16 * (dc + 1))
                    e.tensor_copy(rwb[dc][:], rwf[:]).then_inc(sems.h["cv8"], 1)
                sems.n["cv8"] = v0a + 16

        f0 = [sems.n["sfc0"], sems.n["sfc1"]]
        r10, hf0 = sems.n["sr1"], sems.n["shf"]
        m0, v0, a0, t0 = sems.n["mm8"], sems.n["cv8"], sems.n["ca8"], sems.n["sx8"]
        with nc.Block() as b:
            @b.sync
            def _(e):
                sems.w(e, "cc", sems.n["cc"])
                for tt in range(NTT):
                    if tt >= 2:
                        sems.w(e, "mm8", m0 + 3 * (tt - 1))
                        sems.w(e, "cv8", v0 + 7 * (tt - 1))
                    for kc in range(8):
                        if kc < 4:
                            src = S["outall"][kc][:, DS(S["v_own"] + tt * 128, 128)]
                        else:
                            src = S["cout"][DS(S["v_slot"], 1), kc - 4].rearrange(
                                "s p t -> p (s t)")[:, tt * 128:(tt + 1) * 128]
                        e.dma_start(out=lt[(tt % 2) * 8 + kc][:], in_=src).then_inc(sems.h[f"sfc{tt % 2}"], 16)
                    e.dma_start(out=xo[tt % 2][:], in_=S["x_own"][tt]).then_inc(sems.h[f"sfc{tt % 2}"], 16)
                sems.n["sfc0"] = f0[0] + 16 * 9 * (NTT // 2)
                sems.n["sfc1"] = f0[1] + 16 * 9 * (NTT // 2)

            @b.tensor
            def _(e):
                for tt in range(NTT):
                    sems.w(e, f"sfc{tt % 2}", f0[tt % 2] + 16 * 9 * (tt // 2 + 1))
                    if tt >= 2:
                        sems.w(e, "cv8", v0 + 7 * (tt - 1))
                    for kc in range(8):
                        e.ldweights(lt[(tt % 2) * 8 + kc][:])
                        for half in range(2):
                            li = e.matmul(pF[(tt % 2) * 2 + half][:], lt[(tt % 2) * 8 + kc][:],
                                     wfb[kc][:, half * 512:(half + 1) * 512],
                                     start=(kc == 0), stop=(kc == 7))
                    li.then_inc(sems.h["mm8"], 1)
                    if tt >= 1:
                        sems.w(e, "cv8", v0 + 7 * (tt - 1) + 6)
                    sems.w(e, "sx8", t0 + 16 * 8 * (tt + 1))
                    for dc in range(8):
                        e.ldweights(rwb[dc][:])
                        li = e.matmul(pF[4][0:8, 0:128], rwb[dc][:],
                                 hstg[:, dc * 1024 + (tt % 8) * 128:dc * 1024 + (tt % 8) * 128 + 128],
                                 start=(dc == 0), stop=(dc == 7))
                    li.then_inc(sems.h["mm8"], 1)
                    sems.w(e, "cv8", v0 + 7 * tt + 6)
                    e.transpose(pF[5][:, 0:8], lsb[0:8, :], S["id_f32"][0:8, 0:8]).then_inc(sems.h["mm8"], 1)
                sems.n["mm8"] = m0 + 3 * NTT

            @b.vector
            def _(e):
                for tt in range(NTT):
                    sems.w(e, "mm8", m0 + 3 * tt + 1)
                    if tt >= 1:
                        sems.w(e, "ca8", a0 + 4 * (tt - 1) + 1)
                    e.scalar_tensor_tensor(res_t[:, 0:512], pF[(tt % 2) * 2][:], 0.0,
                                           bfc_sb[:, 0:512], ALU.add, ALU.add)
                    e.scalar_tensor_tensor(res_t[:, 512:1024], pF[(tt % 2) * 2 + 1][:], 0.0,
                                           bfc_sb[:, 512:1024], ALU.add, ALU.add)
                    sems.w(e, f"sfc{tt % 2}", f0[tt % 2] + 16 * 9 * (tt // 2 + 1))
                    e.drain()
                    e.tensor_add(res_t[:], res_t[:], xo[tt % 2][:])
                    e.drain()
                    e.reduce_sum(stt_[:, 0:1], res_t[:], axis=AX.X).then_inc(sems.h["cv8"], 1)
                    sems.w(e, "ca8", a0 + 4 * tt + 1)
                    e.drain()
                    e.tensor_scalar(stt_[:, 2:3], stt_[:, 0:1], 1.0 / D, None, ALU.mult)
                    e.tensor_scalar(stt_[:, 3:4], stt_[:, 1:2], 1.0 / D, None, ALU.mult)
                    e.drain()
                    e.tensor_mul(stt_[:, 6:7], stt_[:, 2:3], stt_[:, 2:3])
                    e.drain()
                    e.tensor_sub(stt_[:, 4:5], stt_[:, 3:4], stt_[:, 6:7])
                    e.drain()
                    e.tensor_scalar(stt_[:, 4:5], stt_[:, 4:5], EPS, None, ALU.add).then_inc(sems.h["cv8"], 1)
                    sems.w(e, "ca8", a0 + 4 * tt + 2)
                    if tt >= 1:
                        sems.w(e, "sr1", r10 + 16 * tt)
                    e.reciprocal(stt_[:, 7:8], stt_[:, 5:6])
                    e.drain()
                    e.tensor_scalar(stt_[:, 6:7], stt_[:, 2:3], stt_[:, 7:8], -1.0,
                                    ALU.mult, ALU.mult)
                    e.drain()
                    e.tensor_scalar(r1t[:], res_t[:], stt_[:, 7:8], stt_[:, 6:7],
                                    ALU.mult, ALU.add).then_inc(sems.h["cv8"], 1)
                    sems.w(e, "ca8", a0 + 4 * tt + 3)
                    e.drain()
                    e.reduce_sum(stt_[:, 0:1], r1t[:], axis=AX.X)
                    e.drain()
                    e.tensor_scalar(stt_[:, 2:3], stt_[:, 0:1], 1.0 / D, None, ALU.mult)
                    e.tensor_scalar(stt_[:, 3:4], stt_[:, 1:2], 1.0 / D, None, ALU.mult)
                    e.drain()
                    e.tensor_mul(stt_[:, 6:7], stt_[:, 2:3], stt_[:, 2:3])
                    e.drain()
                    e.tensor_sub(stt_[:, 4:5], stt_[:, 3:4], stt_[:, 6:7])
                    e.drain()
                    e.tensor_scalar(stt_[:, 4:5], stt_[:, 4:5], EPS, None, ALU.add).then_inc(sems.h["cv8"], 1)
                    sems.w(e, "ca8", a0 + 4 * tt + 4)
                    if tt >= 1:
                        sems.w(e, "sx8", t0 + 16 * 8 * tt)
                    e.reciprocal(stt_[:, 7:8], stt_[:, 5:6])
                    e.drain()
                    e.tensor_scalar(stt_[:, 6:7], stt_[:, 2:3], stt_[:, 7:8], -1.0,
                                    ALU.mult, ALU.mult)
                    e.drain()
                    e.tensor_scalar(hh[:], r1t[:], stt_[:, 7:8], stt_[:, 6:7],
                                    ALU.mult, ALU.add)
                    e.drain()
                    e.tensor_mul(hh[:], hh[:], lng_sb[:])
                    e.drain()
                    e.tensor_add(hb[:], hh[:], lnb_sb[:]).then_inc(sems.h["cv8"], 1)
                    sems.w(e, "mm8", m0 + 3 * tt + 2)
                    e.tensor_copy(lsb[0:8, :], pF[4][0:8, 0:128]).then_inc(sems.h["cv8"], 1)
                    sems.w(e, "mm8", m0 + 3 * tt + 3)
                    e.tensor_copy(logall[:, tt * NEXP:(tt + 1) * NEXP],
                                  pF[5][:, 0:8]).then_inc(sems.h["cv8"], 1)
                sems.n["cv8"] = v0 + 7 * NTT

            @b.scalar
            def _(e):
                for tt in range(NTT):
                    sems.w(e, "cv8", v0 + 7 * tt + 1)
                    e.activation(sq_t[:], res_t[:], AF.Square,
                                 accum_out=stt_[:, 1:2]).then_inc(sems.h["ca8"], 1)
                    sems.w(e, "cv8", v0 + 7 * tt + 2)
                    e.activation(stt_[:, 5:6], stt_[:, 4:5], AF.Sqrt).then_inc(sems.h["ca8"], 1)
                    sems.w(e, "cv8", v0 + 7 * tt + 3)
                    e.dma_start(out=S["r1n_hbm"][tt], in_=r1t[:]).then_inc(sems.h["sr1"], 16)
                    e.drain()
                    e.activation(sq_t[:], r1t[:], AF.Square,
                                 accum_out=stt_[:, 1:2]).then_inc(sems.h["ca8"], 1)
                    sems.w(e, "cv8", v0 + 7 * tt + 4)
                    e.activation(stt_[:, 5:6], stt_[:, 4:5], AF.Sqrt).then_inc(sems.h["ca8"], 1)
                    sems.w(e, "cv8", v0 + 7 * tt + 5)
                    for dc in range(8):
                        e.dma_start_transpose(
                            out=hstg[:, dc * 1024 + (tt % 8) * 128:dc * 1024 + (tt % 8) * 128 + 128],
                            in_=hb[:, dc * 128:(dc + 1) * 128]).then_inc(sems.h["sx8"], 16)
                    if tt % 8 == 7:
                        sems.w(e, "sx8", t0 + 16 * 8 * (tt + 1))
                        for dc in range(8):
                            e.dma_start(out=S["h_hbm"][dc][:, (tt // 8) * 1024:(tt // 8 + 1) * 1024],
                                        in_=hstg[:, dc * 1024:(dc + 1) * 1024]).then_inc(sems.h["shf"], 16)
                        sems.w(e, "shf", hf0 + 16 * 8 * ((tt + 1) // 8))
                sems.n["ca8"] = a0 + 4 * NTT
                sems.n["sx8"] = t0 + 16 * 8 * NTT
                sems.n["sr1"] = r10 + 16 * NTT
                sems.n["shf"] = hf0 + 16 * 8 * (NTT // 8)
                sems.w(e, "sr1", sems.n["sr1"])

        g0, gm0, gv0, gw0 = sems.n["ca9"], sems.n["mm9"], sems.n["cv9"], sems.n["sw9"]
        with nc.Block() as b:
            @b.vector
            def _(e):
                for tt in range(NTT):
                    lg = logall[:, tt * NEXP:(tt + 1) * NEXP]
                    e.reduce_max(stt_[:, 0:1], lg, axis=AX.X)
                    e.drain()
                    e.tensor_scalar_mul(stt_[:, 1:2], stt_[:, 0:1], -1.0).then_inc(sems.h["cv9"], 1)
                    sems.w(e, "ca9", g0 + tt + 1)
                    e.reduce_sum(stt_[:, 2:3], el[:], axis=AX.X)
                    e.drain()
                    e.reciprocal(stt_[:, 3:4], stt_[:, 2:3])
                    e.tensor_scalar(gt[:], lg, stt_[:, 0:1], None, ALU.is_ge)
                    e.drain()
                    e.tensor_scalar_mul(gt[:], gt[:], stt_[:, 3:4]).then_inc(sems.h["cv9"], 1)
                    sems.w(e, "mm9", gm0 + tt + 1)
                    e.tensor_copy(gates8[0:8, tt * 128:(tt + 1) * 128],
                                  pF[6][0:8, 0:128]).then_inc(sems.h["cv9"], 1)
                sems.n["cv9"] = gv0 + 3 * NTT

            @b.scalar
            def _(e):
                for tt in range(NTT):
                    sems.w(e, "cv9", gv0 + 3 * tt + 1)
                    e.activation(el[:], logall[:, tt * NEXP:(tt + 1) * NEXP],
                                 AF.Exp, bias=stt_[:, 1:2]).then_inc(sems.h["ca9"], 1)
                sems.n["ca9"] = g0 + NTT

            @b.tensor
            def _(e):
                for tt in range(NTT):
                    sems.w(e, "cv9", gv0 + 3 * tt + 2)
                    e.transpose(pF[6][0:8, 0:128], gt[:, 0:NEXP],
                                S["id_f32"][:, :]).then_inc(sems.h["mm9"], 1)
                sems.n["mm9"] = gm0 + NTT

        with nc.Block() as b:
            @b.scalar
            def _(e):
                e.dma_start(out=S["grows_hbm"][:, :],
                            in_=gates8[0:8, :]).then_inc(sems.h["sw9"], 16)
                sems.n["sw9"] = gw0 + 16
                sems.w(e, "sw9", sems.n["sw9"])


def _quant_tiles(e_id):
    pass


def emit_ffn(S):
    nc, sems = S["nc"], S["sems"]
    EXP = [("sh", S["sw1"], S["sw2"], None)] + \
          [(f"e{i}", S["ew1"][i], S["ew2"][i], i) for i in range(NEXP)]
    with ExitStack() as pst:
        sb = lambda *a, **k: pst.enter_context(nc.sbuf_tensor(*a, **k))
        ps_ = lambda *a, **k: pst.enter_context(nc.psum_tensor(*a, **k))
        rawb = [sb(f"rawb{i}", [128, FD], F32) for i in range(2)]
        q2b = sb("q2b", [128, FD], BF16)
        wqtb = [sb(f"wqtb{i}", [128, FD], BF16) for i in range(2)]
        growb = sb("growb", [1, 1024], F32)
        sums = sb("sums", [128, 4], F32)
        w1qT = [[sb(f"w1qT{bu}_{kc}", [128, FD], BF16, is_transpose_dest=True)
                 for kc in range(8)] for bu in range(2)]
        w2qT = [sb(f"w2qT{fc}", [128, D], BF16, is_transpose_dest=True) for fc in range(16)]
        hsb = [[sb(f"hsb{bu}_{kc}", [128, 1024], BF16) for kc in range(8)] for bu in range(2)]
        h1b = sb("h1b", [128, 16 * 1024], BF16)
        gate_sb = sb("gate_sb", [128, 1024], BF16)
        eo = [sb(f"eo{i}", [128, 1024], F32) for i in range(2)]
        pW = [ps_(f"pW{i}", [128, 512], F32) for i in range(8)]

        def emit_quant(b, pi):
            nm, w1src, w2src, _ = EXP[pi]
            par = pi % 2
            r0p = [sems.n["rwl0"], sems.n["rwl1"]]
            s0p = [sems.n["scw0"], sems.n["scw1"]]
            c0 = sems.n["ca9"]
            v0 = sems.n["cv9"]
            tiles = [("w1", r, w1src[r], 1024, 1.0 / 1024) for r in range(16)] + \
                    [("w2", r, w2src[r], 2048, 1.0 / 2048) for r in range(8)]

            @b.scalar
            def _(e):
                for t, (kind, r, src, width, inv) in enumerate(tiles):
                    q = t % 2
                    if t >= 2:
                        sems.w(e, "cv9", v0 + t - 1)
                    e.dma_start(out=rawb[q][:, 0:width], in_=src).then_inc(sems.h[f"rwl{q}"], 16)
                    sems.w(e, f"rwl{q}", r0p[q] + 16 * (t // 2 + 1))
                    e.drain()
                    e.activation(q2b[:, 0:width], rawb[q][:, 0:width], AF.Abs,
                                 accum_out=sums[:, 0:1]).then_inc(sems.h["ca9"], 1)
                    sems.w(e, "cv9", v0 + t + 1)
                    if kind == "w1":
                        e.dma_start(out=S["w1scr"][par][:, r * 128:(r + 1) * 128, :].rearrange(
                                        "c q d -> q c d"),
                                    in_=wqtb[q][:, 0:1024].rearrange(
                                        "q (c d) -> q c d", d=128)).then_inc(sems.h[f"scw{q}"], 16)
                    else:
                        e.dma_start(out=S["w2scr"][par][:, r * 128:(r + 1) * 128, :].rearrange(
                                        "c q d -> q c d"),
                                    in_=wqtb[q][:, 0:2048].rearrange(
                                        "q (c d) -> q c d", d=128)).then_inc(sems.h[f"scw{q}"], 16)
                sems.n["rwl0"] = r0p[0] + 16 * 12
                sems.n["rwl1"] = r0p[1] + 16 * 12
                sems.n["scw0"] = s0p[0] + 16 * 12
                sems.n["scw1"] = s0p[1] + 16 * 12
                sems.n["ca9"] = c0 + 24

            @b.vector
            def _(e):
                for t, (kind, r, src, width, inv) in enumerate(tiles):
                    q = t % 2
                    sems.w(e, "ca9", c0 + t + 1)
                    if t >= 2:
                        sems.w(e, f"scw{q}", s0p[q] + 16 * (t // 2))
                    e.tensor_scalar(sums[:, 1:2], sums[:, 0:1], inv, 1e-5, ALU.mult, ALU.max)
                    e.drain()
                    e.tensor_scalar_mul(sums[:, 2:3], sums[:, 1:2], 0.5)
                    e.tensor_scalar_mul(sums[:, 3:4], sums[:, 1:2], -0.5)
                    e.drain()
                    e.tensor_scalar(wqtb[q][:, 0:width], rawb[q][:, 0:width],
                                    sums[:, 2:3], sums[:, 1:2], ALU.is_ge, ALU.mult)
                    e.tensor_scalar(q2b[:, 0:width], rawb[q][:, 0:width],
                                    sums[:, 3:4], sums[:, 1:2], ALU.is_le, ALU.mult)
                    e.drain()
                    e.tensor_sub(wqtb[q][:, 0:width], wqtb[q][:, 0:width],
                                 q2b[:, 0:width]).then_inc(sems.h["cv9"], 1)
                sems.n["cv9"] = v0 + 24

        def emit_xbar(b, pi):
            par = pi % 2
            bu = pi % 2
            s_need0, s_need1 = sems.n["scw0"], sems.n["scw1"]
            m_need = sems.n["mm9"]
            S["xbar_base"] = sems.n["sxw"]

            @b.sync
            def _(e):
                sems.w(e, "scw0", s_need0)
                sems.w(e, "scw1", s_need1)
                for dc in range(8):
                    e.dma_start_transpose(out=w1qT[bu][dc][:],
                                          in_=S["w1scr"][par][dc]).then_inc(sems.h["sxw"], 16)
                sems.w(e, "mm9", m_need)
                for fc in range(16):
                    e.dma_start_transpose(out=w2qT[fc][:],
                                          in_=S["w2scr"][par][fc]).then_inc(sems.h["sxw"], 16)
                sems.n["sxw"] += 16 * 24

        with nc.Block() as b:
            emit_quant(b, 0)
        with nc.Block() as b:
            emit_xbar(b, 0)

        for pi in range(len(EXP)):
            nm, w1src, w2src, grow = EXP[pi]
            bu = pi % 2
            is_sh = grow is None
            x9 = S["xbar_base"]
            m0, v0, c0 = sems.n["mm9"], sems.n["cv9"], sems.n["ca9"]
            gm = 0 if is_sh else 1
            per_grp_mm = gm + 16 + 8
            g0 = [sems.n["gda"], sems.n["gdb"]]
            per_grp_cv = (0 if is_sh else 1) + 8
            gld0 = sems.n["sgr"]
            h0p = [sems.n["shh0"], sems.n["shh1"]]
            with nc.Block() as b:
                @b.sync
                def _(e):
                    for grp in range(4):
                        if not is_sh:
                            if grp > 0:
                                sems.w(e, "mm9", m0 + (grp - 1) * per_grp_mm + 1)
                            e.dma_start(out=growb[0:1, :],
                                        in_=S["grows_hbm"][grow:grow + 1, grp * 1024:(grp + 1) * 1024]).then_inc(sems.h["sgr"], 16)
                        if grp >= 2:
                            sems.w(e, "mm9", m0 + (grp - 1) * per_grp_mm + gm + 16)
                        for kc in range(8):
                            e.dma_start(out=hsb[grp % 2][kc][:],
                                        in_=S["h_hbm"][kc][:, grp * 1024:(grp + 1) * 1024]).then_inc(sems.h[f"shh{grp % 2}"], 16)
                    if not is_sh:
                        sems.n["sgr"] = gld0 + 16 * 4
                    sems.n["shh0"] = h0p[0] + 16 * 16
                    sems.n["shh1"] = h0p[1] + 16 * 16

                @b.tensor
                def _(e):
                    for grp in range(4):
                        gb = m0 + grp * per_grp_mm
                        if not is_sh:
                            if grp > 0:
                                sems.w(e, "cv9", v0 + grp * per_grp_cv)
                            sems.w(e, "sgr", gld0 + 16 * (grp + 1))
                            e.matmul(pW[6][:], S["ones_f32"][0:1, :],
                                     growb[0:1, 0:512], start=True, stop=True)
                            li = e.matmul(pW[7][:], S["ones_f32"][0:1, :],
                                     growb[0:1, 512:1024], start=True, stop=True)
                            li.then_inc(sems.h["mm9"], 1)
                        sems.w(e, "sxw", x9 + 16 * 24)
                        sems.w(e, f"shh{grp % 2}", h0p[grp % 2] + 16 * 8 * (grp // 2 + 1))
                        for fc in range(16):
                            if grp * 16 + fc >= 2:
                                sems.w(e, "ca9", c0 + grp * 16 + fc - 1)
                            for kc in range(8):
                                lw = w1qT[bu][kc][:, fc * 128:(fc + 1) * 128]
                                e.ldweights(lw)
                                for wi in range(2):
                                    li = e.matmul(pW[(fc % 2) * 2 + wi][:], lw,
                                             hsb[grp % 2][kc][:, wi * 512:(wi + 1) * 512],
                                             start=(kc == 0), stop=(kc == 7))
                            li.then_inc(sems.h["mm9"], 1)
                        sems.w(e, "sxw", x9 + 16 * 24)
                        sems.w(e, "ca9", c0 + (grp + 1) * 16)
                        for dc in range(8):
                            if dc >= 2:
                                sems.w(e, "cv9", v0 + grp * per_grp_cv + (0 if is_sh else 1) + dc - 1)
                            bk = 4 if (dc % 2 == 0) else 6
                            if dc <= 1:
                                sems.w(e, "cv9", v0 + grp * per_grp_cv + (0 if is_sh else 1))
                            for fc in range(16):
                                lw = w2qT[fc][:, dc * 128:(dc + 1) * 128]
                                e.ldweights(lw)
                                for wi in range(2):
                                    li = e.matmul(pW[bk + wi][:], lw,
                                             h1b[:, fc * 1024 + wi * 512:fc * 1024 + (wi + 1) * 512],
                                             start=(fc == 0), stop=(fc == 15))
                            li.then_inc(sems.h["mm9"], 1)
                    sems.n["mm9"] = m0 + 4 * per_grp_mm

                @b.scalar
                def _(e):
                    for grp in range(4):
                        for fc in range(16):
                            sems.w(e, "mm9", m0 + grp * per_grp_mm + gm + fc + 1)
                            for wi in range(2):
                                li = e.activation(h1b[:, fc * 1024 + wi * 512:fc * 1024 + (wi + 1) * 512],
                                             pW[(fc % 2) * 2 + wi][:], AF.Silu)
                            li.then_inc(sems.h["ca9"], 1)
                    sems.n["ca9"] = c0 + 64

                @b.vector
                def _(e):
                    for grp in range(4):
                        if not is_sh:
                            sems.w(e, "mm9", m0 + grp * per_grp_mm + 1)
                            e.tensor_copy(gate_sb[:, 0:512], pW[6][:])
                            e.tensor_copy(gate_sb[:, 512:1024], pW[7][:]).then_inc(sems.h["cv9"], 1)
                        for dc in range(8):
                            sems.w(e, "mm9", m0 + grp * per_grp_mm + gm + 16 + dc + 1)
                            sems.w(e, f"gd{'ab'[dc % 2]}",
                                   g0[dc % 2] + 16 * ((grp * 8 + dc) // 2))
                            bk = 4 if (dc % 2 == 0) else 6
                            for wi in range(2):
                                if is_sh:
                                    li = e.tensor_copy(eo[dc % 2][:, wi * 512:(wi + 1) * 512],
                                                       pW[bk + wi][:])
                                else:
                                    li = e.scalar_tensor_tensor(eo[dc % 2][:, wi * 512:(wi + 1) * 512],
                                                                pW[bk + wi][:], 0.0,
                                                                gate_sb[:, wi * 512:(wi + 1) * 512],
                                                                ALU.add, ALU.mult)
                            li.then_inc(sems.h["cv9"], 1)
                    sems.n["cv9"] = v0 + 4 * per_grp_cv

                @b.gpsimd
                def _(e):
                    for grp in range(4):
                        for dc in range(8):
                            sems.w(e, "cv9", v0 + grp * per_grp_cv + (0 if is_sh else 1) + dc + 1)
                            e.dma_start(out=S["acc_hbm"][dc][:, grp * 1024:(grp + 1) * 1024],
                                        in_=eo[dc % 2][:],
                                        accum_op=(ALU.bypass if is_sh else ALU.add)).then_inc(
                                            sems.h[f"gd{'ab'[dc % 2]}"], 16)
                    sems.n["gda"] = g0[0] + 16 * 16
                    sems.n["gdb"] = g0[1] + 16 * 16

                if pi + 1 < len(EXP):
                    emit_quant(b, pi + 1)
            if pi + 1 < len(EXP):
                with nc.Block() as b2:
                    emit_xbar(b2, pi + 1)


def emit_tail(S):
    nc, sems = S["nc"], S["sems"]
    NTT = NT // 128
    RS = AF.Abs_reciprocal_sqrt
    with ExitStack() as pst:
        sb = lambda *a, **k: pst.enter_context(nc.sbuf_tensor(*a, **k))
        ps_ = lambda *a, **k: pst.enter_context(nc.psum_tensor(*a, **k))
        ac8 = [sb(f"ac8_{i}", [128, 1024], F32) for i in range(2)]
        r1b = [sb(f"r1b_{i}", [128, D], F32) for i in range(2)]
        ffn_t = sb("ffn_t", [128, D], F32)
        res2 = sb("res2", [128, D], F32)
        sq2 = sb("sq2", [128, D], F32)
        yt = [sb(f"yt{i}", [128, D], F32) for i in range(2)]
        st2 = sb("st2", [128, 8], F32)
        pT = [ps_(f"pT{i}", [128, 512], F32) for i in range(4)]
        a0p = [sems.n["sac0"], sems.n["sac1"]]
        y0p = [sems.n["sy0"], sems.n["sy1"]]
        m0, v0, a0 = sems.n["mm9"], sems.n["cv9"], sems.n["ca9"]
        gall = [sems.n["gda"], sems.n["gdb"]]
        with nc.Block() as b:
            @b.sync
            def _(e):
                sems.w(e, "gda", gall[0])
                sems.w(e, "gdb", gall[1])
                for tt in range(NTT):
                    if tt >= 2:
                        sems.w(e, "cv9", v0 + 4 * (tt - 1))
                    for dc in range(8):
                        e.dma_start(out=ac8[tt % 2][:, dc * 128:(dc + 1) * 128],
                                    in_=S["acc_hbm"][dc][:, tt * 128:(tt + 1) * 128]).then_inc(sems.h[f"sac{tt % 2}"], 16)
                    e.dma_start(out=r1b[tt % 2][:], in_=S["r1n_hbm"][tt]).then_inc(sems.h[f"sac{tt % 2}"], 16)
                sems.n["sac0"] = a0p[0] + 16 * 9 * (NTT // 2)
                sems.n["sac1"] = a0p[1] + 16 * 9 * (NTT // 2)

            @b.tensor
            def _(e):
                for tt in range(NTT):
                    sems.w(e, f"sac{tt % 2}", a0p[tt % 2] + 16 * 9 * (tt // 2 + 1))
                    if tt >= 1:
                        sems.w(e, "cv9", v0 + 4 * (tt - 1) + 1)
                    for dc in range(8):
                        li = e.transpose(pT[dc // 2][:, (dc % 2) * 128:(dc % 2) * 128 + 128],
                                    ac8[tt % 2][:, dc * 128:(dc + 1) * 128],
                                    S["id_f32"][:, :])
                    li.then_inc(sems.h["mm9"], 1)
                sems.n["mm9"] = m0 + NTT

            @b.vector
            def _(e):
                for tt in range(NTT):
                    sems.w(e, "mm9", m0 + tt + 1)
                    for dc in range(8):
                        li = e.tensor_copy(ffn_t[:, dc * 128:(dc + 1) * 128],
                                      pT[dc // 2][:, (dc % 2) * 128:(dc % 2) * 128 + 128])
                    li.then_inc(sems.h["cv9"], 1)
                    sems.w(e, f"sac{tt % 2}", a0p[tt % 2] + 16 * 9 * (tt // 2 + 1))
                    if tt >= 1:
                        sems.w(e, "ca9", a0 + 2 * (tt - 1) + 1)
                    e.drain()
                    e.tensor_add(res2[:], ffn_t[:], r1b[tt % 2][:])
                    e.drain()
                    e.reduce_sum(st2[:, 0:1], res2[:], axis=AX.X).then_inc(sems.h["cv9"], 1)
                    sems.w(e, "ca9", a0 + 2 * tt + 1)
                    e.drain()
                    e.tensor_scalar(st2[:, 2:3], st2[:, 0:1], 1.0 / D, None, ALU.mult)
                    e.tensor_scalar(st2[:, 3:4], st2[:, 1:2], 1.0 / D, None, ALU.mult)
                    e.drain()
                    e.tensor_mul(st2[:, 6:7], st2[:, 2:3], st2[:, 2:3])
                    e.drain()
                    e.tensor_sub(st2[:, 4:5], st2[:, 3:4], st2[:, 6:7])
                    e.drain()
                    e.tensor_scalar(st2[:, 4:5], st2[:, 4:5], EPS, None, ALU.add).then_inc(sems.h["cv9"], 1)
                    sems.w(e, "ca9", a0 + 2 * tt + 2)
                    if tt >= 2:
                        sems.w(e, f"sy{tt % 2}", y0p[tt % 2] + 16 * (tt // 2))
                    e.reciprocal(st2[:, 7:8], st2[:, 5:6])
                    e.drain()
                    e.tensor_scalar(st2[:, 6:7], st2[:, 2:3], st2[:, 7:8], -1.0,
                                    ALU.mult, ALU.mult)
                    e.drain()
                    e.tensor_scalar(yt[tt % 2][:], res2[:], st2[:, 7:8], st2[:, 6:7],
                                    ALU.mult, ALU.add).then_inc(sems.h["cv9"], 1)
                sems.n["cv9"] = v0 + 4 * NTT

            @b.scalar
            def _(e):
                for tt in range(NTT):
                    sems.w(e, "cv9", v0 + 4 * tt + 2)
                    e.activation(sq2[:], res2[:], AF.Square,
                                 accum_out=st2[:, 1:2]).then_inc(sems.h["ca9"], 1)
                    sems.w(e, "cv9", v0 + 4 * tt + 3)
                    e.activation(st2[:, 5:6], st2[:, 4:5], AF.Sqrt).then_inc(sems.h["ca9"], 1)
                    sems.w(e, "cv9", v0 + 4 * tt + 4)
                    e.dma_start(out=S["y"][tt], in_=yt[tt % 2][:]).then_inc(sems.h[f"sy{tt % 2}"], 16)
                sems.n["ca9"] = a0 + 2 * NTT
                sems.n["sy0"] = y0p[0] + 16 * (NTT // 2)
                sems.n["sy1"] = y0p[1] + 16 * (NTT // 2)

        with nc.Block() as b:
            @b.sync
            def _(e):
                sems.w(e, "sy0", sems.n["sy0"])
                sems.w(e, "sy1", sems.n["sy1"])


def emit_dummy_tail(S):
    nc, sems = S["nc"], S["sems"]
    with nc.Block() as b:
        @b.sync
        def _(e):
            sems.w(e, "sr1", sems.n["sr1"])
            for tt in range(NT // 128):
                e.dma_start(out=S["y"][tt], in_=S["r1n_hbm"][tt]).then_inc(sems.h["sy0"], 16)
            sems.n["sy0"] += 16 * (NT // 128)
            sems.w(e, "sy0", sems.n["sy0"])


def host_in_maps(inputs):
    x = np.asarray(inputs["L0_attn_x"])
    w_qkv = np.asarray(inputs["L0_attn_w_qkv"])
    w_conv = np.asarray(inputs["L0_attn_w_conv"])
    w_fc = np.asarray(inputs["L0_attn_w_fc"])
    b_fc = np.asarray(inputs["L0_attn_b_fc"])
    sw1 = np.asarray(inputs["L0_ffn_shared_w1"])
    sw2 = np.asarray(inputs["L0_ffn_shared_w2"])
    ew1 = np.asarray(inputs["L0_ffn_expert_w1"])
    ew2 = np.asarray(inputs["L0_ffn_expert_w2"])
    rw = np.asarray(inputs["L0_ffn_router_w"])
    lng = np.asarray(inputs["L0_ffn_ln_gamma"])
    lnb = np.asarray(inputs["L0_ffn_ln_beta"])

    idbf = np.eye(128, dtype=BF)
    idf32 = np.eye(128, dtype=np.float32)
    onesbf = np.ones((128, 128), dtype=BF)
    identc = np.zeros((128, 6 * M), dtype=np.float32)
    for ci, coef in enumerate((7.0, 15.0, 3.25)):
        for pc in range(2):
            blk = np.zeros((128, M), dtype=np.float32)
            blk[np.arange(128), pc * 128 + np.arange(128)] = coef
            identc[:, ci * 2 * M + pc * M:(ci * 2 + pc + 1) * M] = blk
    identc = identc.astype(BF)

    maps = []
    for c in range(8):
        b, j = c // 2, c % 2
        hoff = 8 * j
        poff = 8 * (1 - j)
        xT = np.ascontiguousarray(x[b].T).reshape(8, 128, N)
        x_own = np.ascontiguousarray(x[b, j * NT:(j + 1) * NT]).reshape(NT // 128, 128, D)
        wq = np.concatenate([w_qkv[:, hoff * DH:hoff * DH + 512],
                             w_qkv[:, D + hoff * DH:D + hoff * DH + 512],
                             w_qkv[:, 2 * D + hoff * DH:2 * D + hoff * DH + 512]],
                            axis=1)
        wq = np.ascontiguousarray(wq).reshape(8, 128, 3 * NH * DH)
        wf = np.concatenate([w_fc[hoff * DH:hoff * DH + 512],
                             w_fc[poff * DH:poff * DH + 512]], axis=0)
        wf = np.ascontiguousarray(wf).reshape(8, 128, D)
        convd = np.zeros((4, CK, 128, 128), dtype=np.float32)
        for p in range(4):
            for jj in range(CK):
                dv = np.concatenate([np.full(64, w_conv[hoff + 2 * p, 0, jj, 0], np.float32),
                                     np.full(64, w_conv[hoff + 2 * p + 1, 0, jj, 0], np.float32)])
                convd[p, jj, np.arange(128), np.arange(128)] = dv
        meta = np.array([[(1 - j) * NT, 1 - j, j * NT, 0, 0, 0, 0, 0]], dtype=np.int32)
        maps.append({
            "xT": xT, "x_own": x_own, "wqkv": wq, "wfc": wf,
            "bfc_rep": np.ascontiguousarray(np.broadcast_to(b_fc, (128, D))),
            "lng_rep": np.ascontiguousarray(np.broadcast_to(lng, (128, D))),
            "lnb_rep": np.ascontiguousarray(np.broadcast_to(lnb, (128, D))),
            "convd": convd, "identc": identc, "idbf": idbf, "idf32": idf32,
            "onesbf": onesbf, "onesf": np.ones((128, 128), dtype=np.float32), "meta": meta,
            "rwT": np.ascontiguousarray(rw.T).reshape(8, 128, NEXP),
            "sw1": sw1.reshape(FD // 128, 128, D),
            "sw2": sw2.reshape(D // 128, 128, FD),
            "ew1": ew1.reshape(NEXP, FD // 128, 128, D),
            "ew2": ew2.reshape(NEXP, D // 128, 128, FD),
        })
    return maps


def kernel(**inputs):
    from concourse.bass_utils import run_bass_kernel_spmd
    nc = build(n_pass=4, do_ffn=True)
    nc.finalize()
    maps = host_in_maps(inputs)
    res = run_bass_kernel_spmd(nc, maps, core_ids=list(range(8)))
    out = np.zeros((4, N, D), dtype=np.float32)
    for c in range(8):
        b, j = c // 2, c % 2
        out[b, j * NT:(j + 1) * NT] = res.results[c]["y"].reshape(NT, D)
    return out


# version 03
# speedup vs baseline: 1.0835x; 1.0472x over previous
import math
from contextlib import ExitStack

import numpy as np
import ml_dtypes

import concourse.bass as bass
import concourse.bacc as bacc
import concourse.mybir as mybir

F32 = mybir.dt.float32
BF16 = mybir.dt.bfloat16
I32 = mybir.dt.int32
AF = mybir.ActivationFunctionType
ALU = mybir.AluOpType
AX = mybir.AxisListType
DS = bass.ds

N, D, NH, DH, M = 8192, 1024, 8, 64, 256
MP_ITERS, CK = 6, 13
SCL = DH ** -0.5
NT = 4096
FD, NEXP, EPS = 2048, 8, 1e-5
BF = ml_dtypes.bfloat16


class Sems:
    def __init__(self, nc, st, names):
        self.h = {nm: st.enter_context(nc.semaphore(nm)) for nm in names}
        self.n = {nm: 0 for nm in names}

    def inc(self, instr, nm, k=1):
        instr.then_inc(self.h[nm], k)
        self.n[nm] += k

    def w(self, eng, nm, v):
        eng.wait_ge(self.h[nm], v)


SEM_NAMES = ["sx", "cc",
             "cvw", "cvx", "cvl", "cvq", "mmq",
             "mm1", "ca1", "cve", "sgn", "sgo",
             "mm2", "ca2", "mm3", "ca3", "cv3",
             "mm4", "cv4", "mm5", "cv5", "mm6", "cv6",
             "mm7", "cv7",
             "mm8", "cv8", "ca8", "sx8",
             "cv9", "mm9", "ca9", "sw9",
             "swq0", "swq1", "swl0", "swl1", "sxx0", "sxx1", "scv",
             "sxv", "sl3", "sst0", "sst1", "sci",
             "sfc0", "sfc1", "sr1", "shf",
             "rwl0", "rwl1", "scw0", "scw1", "sxw", "shh0", "shh1", "sgr",
             "gda", "gdb", "sac0", "sac1", "sy0", "sy1", "cvz", "caz"]


def build(n_pass=4, do_ffn=True, dumps=(), dumpreq=()):
    nc = bacc.Bacc("TRN2", target_bir_lowering=False)
    st = ExitStack()

    S = {"nc": nc, "st": st}
    S["xT"] = nc.dram_tensor("xT", [8, 128, N], F32, kind="ExternalInput").ap()
    S["x_own"] = nc.dram_tensor("x_own", [NT // 128, 128, D], F32, kind="ExternalInput").ap()
    S["wqkv"] = nc.dram_tensor("wqkv", [8, 128, 3 * NH * DH], F32, kind="ExternalInput").ap()
    S["wfc"] = nc.dram_tensor("wfc", [8, 128, D], F32, kind="ExternalInput").ap()
    S["bfc_rep"] = nc.dram_tensor("bfc_rep", [128, D], F32, kind="ExternalInput").ap()
    S["lng_rep"] = nc.dram_tensor("lng_rep", [128, D], F32, kind="ExternalInput").ap()
    S["lnb_rep"] = nc.dram_tensor("lnb_rep", [128, D], F32, kind="ExternalInput").ap()
    S["convd"] = nc.dram_tensor("convd", [4, CK, 128, 128], F32, kind="ExternalInput").ap()
    S["identc"] = nc.dram_tensor("identc", [128, 6 * M], BF16, kind="ExternalInput").ap()
    S["idbf"] = nc.dram_tensor("idbf", [128, 128], BF16, kind="ExternalInput").ap()
    S["idf32"] = nc.dram_tensor("idf32", [128, 128], F32, kind="ExternalInput").ap()
    S["onesbf"] = nc.dram_tensor("onesbf", [128, 128], BF16, kind="ExternalInput").ap()
    S["onesf"] = nc.dram_tensor("onesf", [128, 128], F32, kind="ExternalInput").ap()
    S["meta"] = nc.dram_tensor("meta", [1, 8], I32, kind="ExternalInput").ap()
    S["rwT"] = nc.dram_tensor("rwT", [8, 128, NEXP], F32, kind="ExternalInput").ap()
    S["sw1"] = nc.dram_tensor("sw1", [FD // 128, 128, D], F32, kind="ExternalInput").ap()
    S["sw2"] = nc.dram_tensor("sw2", [D // 128, 128, FD], F32, kind="ExternalInput").ap()
    S["ew1"] = nc.dram_tensor("ew1", [NEXP, FD // 128, 128, D], F32, kind="ExternalInput").ap()
    S["ew2"] = nc.dram_tensor("ew2", [NEXP, D // 128, 128, FD], F32, kind="ExternalInput").ap()
    S["y"] = nc.dram_tensor("y", [NT // 128, 128, D], F32, kind="ExternalOutput").ap()

    S["outall"] = nc.dram_tensor("outall", [4, 128, N], BF16, kind="Internal").ap()
    S["cin"] = nc.dram_tensor("cin", [4, 128, NT], BF16, kind="Internal").ap()
    S["cout"] = nc.dram_tensor("cout", [2, 4, 128, NT], BF16, kind="Internal").ap()
    S["gin"] = nc.dram_tensor("gin", [1, 2], F32, kind="Internal").ap()
    S["gout"] = nc.dram_tensor("gout", [1, 2], F32, kind="Internal").ap()
    S["r1n_hbm"] = nc.dram_tensor("r1n_hbm", [NT // 128, 128, D], F32, kind="Internal").ap()
    S["h_hbm"] = nc.dram_tensor("h_hbm", [8, 128, NT], BF16, kind="Internal").ap()
    S["w1scr"] = nc.dram_tensor("w1scr", [2, 8, FD, 128], BF16, kind="Internal").ap()
    S["w2scr"] = nc.dram_tensor("w2scr", [2, 16, D, 128], BF16, kind="Internal").ap()
    S["acc_hbm"] = nc.dram_tensor("acc_hbm", [8, 128, NT], F32, kind="Internal").ap()
    S["grows_hbm"] = nc.dram_tensor("grows_hbm", [8, NT], F32, kind="Internal").ap()

    S["dump"] = {}
    S["dumpreq"] = dumpreq

    sb = lambda *a, **k: st.enter_context(nc.sbuf_tensor(*a, **k))
    S["mt"] = sb("mt", [1, 8], I32)
    S["id_bf"] = sb("id_bf", [128, 128], BF16)
    S["id_f32"] = sb("id_f32", [128, 128], F32)
    S["ones_bf"] = sb("ones_bf", [128, 128], BF16)
    S["ones_f32"] = sb("ones_f32", [128, 128], F32)
    S["identc_sb"] = sb("identc_sb", [128, 6 * M], BF16)
    ast = ExitStack()
    S["ql_all"] = ast.enter_context(nc.sbuf_tensor("ql_all", [128, NH * M], BF16, side="right"))
    S["kl_all"] = ast.enter_context(nc.sbuf_tensor("kl_all", [128, NH * M], BF16, side="right"))
    S["a2T_all"] = ast.enter_context(nc.sbuf_tensor("a2T_all", [128, NH * 2 * M], BF16, side="right"))
    S["junk"] = sb("junk", [1, 8], BF16)
    S["statv"] = sb("statv", [128, 8], F32)
    S["gsc"] = sb("gsc", [1, 8], F32)
    S["rden_rep"] = sb("rden_rep", [128, 1], F32)

    sems = Sems(nc, st, SEM_NAMES)
    S["sems"] = sems

    with nc.Block() as b:
        @b.sync
        def _(e):
            for dst, src in ((S["mt"], S["meta"]), (S["id_bf"], S["idbf"]),
                             (S["id_f32"], S["idf32"]), (S["ones_bf"], S["onesbf"]),
                             (S["ones_f32"], S["onesf"]),
                             (S["identc_sb"], S["identc"])):
                e.dma_start(out=dst[:], in_=src[:, :]).then_inc(sems.h["sx"], 16)
            sems.n["sx"] += 96

        @b.vector
        def _(e):
            sems.w(e, "sx", sems.n["sx"])
            e.memset(S["junk"][0:1, 0:4], 1.0)
            e.memset(S["junk"][0:1, 4:8], 0.0)
            e.memset(S["statv"][:], 0.0)
            e.memset(S["gsc"][0:1, 0:4], 1.0)
            e.memset(S["gsc"][0:1, 4:8], 0.0)

    S["v_oth"] = nc.values_load(S["mt"][0:1, 0:1], engines=[mybir.EngineType.SP, mybir.EngineType.Activation])
    S["v_slot"] = nc.values_load(S["mt"][0:1, 1:2], engines=[mybir.EngineType.SP])
    S["v_own"] = nc.values_load(S["mt"][0:1, 2:3], engines=[mybir.EngineType.SP])

    for p in range(n_pass):
        emit_pass(S, p)
    ast.close()
    emit_exchange(S)
    emit_fc_ln(S)
    if do_ffn:
        emit_ffn(S)
        emit_tail(S)
    else:
        emit_dummy_tail(S)

    if dumps:
        dt_ = {}
        for key in dumps:
            t = S[key].tensor
            dt_[key] = nc.dram_tensor("d_" + key, list(t.shape), t.dtype,
                                      kind="ExternalOutput").ap()
        with nc.Block() as b:
            @b.scalar
            def _(e):
                for key in dumps:
                    e.dma_start(out=dt_[key][...], in_=S[key][...]).then_inc(sems.h["scv"], 16)
                    sems.n["scv"] += 16
                sems.w(e, "scv", sems.n["scv"])

    st.close()
    return nc


def emit_pass(S, p):
    nc, sems = S["nc"], S["sems"]
    with ExitStack() as pst:
        sb = lambda *a, **k: pst.enter_context(nc.sbuf_tensor(*a, **k))
        ps_ = lambda *a, **k: pst.enter_context(nc.psum_tensor(*a, **k))
        qkv = [sb(f"qkv{p}_{i}", [128, N], BF16) for i in range(3)]
        cvb = sb(f"cvb{p}", [128, CK * 128], BF16)
        outp = sb(f"outp{p}", [128, N + 2 * (CK // 2)], BF16)
        ostg = [sb(f"ostg{p}_{i}", [128, 2048], BF16) for i in range(2)]
        pA = [ps_(f"pA{p}_{i}", [128, 512], F32) for i in range(8)]
        if p == 0:
            xlb = [sb(f"xlb_{i}", [128, M], BF16) for i in range(8)]
        inner = ExitStack()
        sbi = lambda *a, **k: inner.enter_context(nc.sbuf_tensor(*a, **k))
        xw = [sbi(f"xw{p}_{i}", [128, 2048], F32) for i in range(2)]
        xbg = [sbi(f"xbg{p}_{i}", [128, 2048], BF16) for i in range(8)]
        wqf = [sbi(f"wqf{p}_{i}", [128, 384], F32) for i in range(2)]
        wqb = [sbi(f"wqb{p}_{i}", [128, 384], BF16) for i in range(8)]
        cvf = sbi(f"cvf{p}", [128, CK * 128], F32)
        if p == 0:
            xl = [sbi(f"xl_{i}", [128, M], F32) for i in range(8)]
            wlf = [sbi(f"wlf_{i}", [128, 1024], F32) for i in range(2)]
            wlb = [sbi(f"wlb_{i}", [128, 1024], BF16) for i in range(8)]

        qc, kc, vc = 2 * p * DH, NH * DH + 2 * p * DH, 2 * NH * DH + 2 * p * DH
        cw0, cx0, cl0 = sems.n["cvw"], sems.n["cvx"], sems.n["cvl"]
        mq0, cq0 = sems.n["mmq"], sems.n["cvq"]
        q0 = [sems.n["swq0"], sems.n["swq1"]]
        x0 = [sems.n["sxx0"], sems.n["sxx1"]]
        v0s = sems.n["scv"]
        nwl = 8 if p == 0 else 0
        nwq_cast = 8 + nwl + 1

        with nc.Block() as b:
            @b.sync
            def _(e):
                for dc in range(8):
                    if dc >= 2:
                        sems.w(e, "cvw", cw0 + dc - 1)
                    for ci, c0 in enumerate((qc, kc, vc)):
                        e.dma_start(out=wqf[dc % 2][:, ci * 128:(ci + 1) * 128],
                                    in_=S["wqkv"][dc][:, c0:c0 + 128]).then_inc(sems.h[f"swq{dc % 2}"], 16)
                if p == 0:
                    for dc in range(8):
                        if dc >= 2:
                            sems.w(e, "cvw", cw0 + 8 + dc - 1)
                        e.dma_start(out=wlf[dc % 2][:],
                                    in_=S["wqkv"][dc][:, 0:1024]).then_inc(sems.h[f"swl{dc % 2}"], 16)
                for jj in range(CK):
                    e.dma_start(out=cvf[:, jj * 128:(jj + 1) * 128],
                                in_=S["convd"][p, jj]).then_inc(sems.h["scv"], 16)
                for g in range(4):
                    for dc in range(8):
                        i = g * 8 + dc
                        if i >= 2:
                            sems.w(e, "cvx", cx0 + i - 1)
                            if p == 0:
                                sems.w(e, "cvl", cl0 + i - 1)
                        e.dma_start(out=xw[i % 2][:],
                                    in_=S["xT"][dc][:, g * 2048:(g + 1) * 2048]).then_inc(sems.h[f"sxx{i % 2}"], 16)
                sems.n["swq0"] = q0[0] + 16 * 12
                sems.n["swq1"] = q0[1] + 16 * 12
                if p == 0:
                    sems.n["swl0"] += 16 * 4
                    sems.n["swl1"] += 16 * 4
                sems.n["scv"] = v0s + 16 * CK
                sems.n["sxx0"] = x0[0] + 16 * 16
                sems.n["sxx1"] = x0[1] + 16 * 16

            @b.vector
            def _(e):
                for dc in range(8):
                    sems.w(e, f"swq{dc % 2}", q0[dc % 2] + 16 * 3 * (dc // 2 + 1))
                    e.tensor_copy(wqb[dc][:], wqf[dc % 2][:]).then_inc(sems.h["cvw"], 1)
                if p == 0:
                    for dc in range(8):
                        sems.w(e, f"swl{dc % 2}", sems.n[f"swl{dc % 2}"] - 16 * 4 + 16 * (dc // 2 + 1))
                        e.tensor_copy(wlb[dc][:], wlf[dc % 2][:]).then_inc(sems.h["cvw"], 1)
                sems.w(e, "scv", v0s + 16 * CK)
                e.tensor_copy(cvb[:], cvf[:]).then_inc(sems.h["cvw"], 1)
                sems.n["cvw"] = cw0 + nwq_cast
                for g in range(4):
                    for dc in range(8):
                        i = g * 8 + dc
                        sems.w(e, f"sxx{i % 2}", x0[i % 2] + 16 * (i // 2 + 1))
                        if g > 0:
                            sems.w(e, "mmq", mq0 + 3 * g)
                        e.tensor_copy(xbg[dc][:], xw[i % 2][:]).then_inc(sems.h["cvx"], 1)
                        if p == 0:
                            e.reduce_sum(xl[dc][:, g * 64:(g + 1) * 64],
                                         xw[i % 2][:].rearrange("q (a w) -> q a w", w=32),
                                         axis=AX.X).then_inc(sems.h["cvl"], 1)
                    for cc in range(3):
                        k = g * 3 + cc
                        sems.w(e, "mmq", mq0 + k + 1)
                        for wi in range(4):
                            bank = (k % 2) * 4 + wi
                            li = e.tensor_copy(qkv[cc][:, g * 2048 + wi * 512:g * 2048 + (wi + 1) * 512],
                                               pA[bank][:])
                        li.then_inc(sems.h["cvq"], 1)
                sems.n["cvx"] = cx0 + 32
                sems.n["cvq"] = cq0 + 12
                if p == 0:
                    sems.n["cvl"] = cl0 + 32

            @b.tensor
            def _(e):
                sems.w(e, "cvw", cw0 + 8)
                for g in range(4):
                    sems.w(e, "cvx", cx0 + (g + 1) * 8)
                    for cc in range(3):
                        k = g * 3 + cc
                        if k >= 2:
                            sems.w(e, "cvq", cq0 + k - 1)
                        for dc in range(8):
                            e.ldweights(wqb[dc][:, cc * 128:(cc + 1) * 128])
                            for wi in range(4):
                                bank = (k % 2) * 4 + wi
                                li = e.matmul(pA[bank][:], wqb[dc][:, cc * 128:(cc + 1) * 128],
                                              xbg[dc][:, wi * 512:(wi + 1) * 512],
                                              start=(dc == 0), stop=(dc == 7))
                        li.then_inc(sems.h["mmq"], 1)
                sems.n["mmq"] = mq0 + 12

        if p == 0:
            for ci in range(3):
                dump_sb(S, f"qkv{p}_{ci}", qkv[ci][:])
            emit_attn2prep(S, xl, xlb, wlb, pA)
            dump_sb(S, "ql", S["ql_all"][:])
            dump_sb(S, "kl", S["kl_all"][:])
            dump_sb(S, "a2T", S["a2T_all"][:])
        inner.close()

        for hl in range(2):
            emit_head(S, p, hl, qkv, outp, pA)
        dump_sb(S, f"outp{p}", outp[:])
        emit_conv_store(S, p, outp, cvb, ostg, pA)


def emit_attn2prep(S, xl, xlb, wlb, pA):
    nc, sems = S["nc"], S["sems"]
    ql, kl, a2T = S["ql_all"], S["kl_all"], S["a2T_all"]
    ones = S["ones_bf"]
    with ExitStack() as pst:
        sb = lambda *a, **k: pst.enter_context(nc.sbuf_tensor(*a, **k))
        e2b = sb("e2b", [128, 2 * M], BF16)
        rl2 = sb("rl2", [128, M], F32)
        rsum = sb("rsum", [128, 1], F32)
        m0, a0, v0, x0 = sems.n["mm1"], sems.n["ca1"], sems.n["cve"], sems.n["cvx"]

        with nc.Block() as b:
            @b.vector
            def _(e):
                e.memset(ql[:], 0.0)
                e.memset(kl[:], 0.0)
                for dc in range(8):
                    e.tensor_scalar_mul(xlb[dc][:], xl[dc][:], 1.0 / 32.0).then_inc(sems.h["cvx"], 1)
                for h in range(NH):
                    ob = (h % 2) * 64
                    sems.w(e, "mm1", m0 + 6 * h + 2)
                    e.tensor_copy(ql[ob:ob + 64, h * M:(h + 1) * M], pA[0][ob:ob + 64, 0:M])
                    e.tensor_copy(kl[ob:ob + 64, h * M:(h + 1) * M],
                                  pA[1][ob:ob + 64, 0:M]).then_inc(sems.h["cve"], 1)
                    sems.w(e, "mm1", m0 + 6 * h + 5)
                    e.reciprocal(rl2[:], pA[4][:, 0:M])
                    e.drain()
                    e.tensor_mul(a2T[:, h * 2 * M:h * 2 * M + M], e2b[:, 0:M], rl2[:])
                    e.tensor_mul(a2T[:, h * 2 * M + M:(h + 1) * 2 * M], e2b[:, M:2 * M], rl2[:]).then_inc(sems.h["cve"], 1)
                    e.drain()
                    e.reduce_sum(rsum[:], a2T[:, h * 2 * M:h * 2 * M + M], axis=AX.X)
                    e.reduce_sum(S["statv"][:, 1:2], a2T[:, h * 2 * M + M:(h + 1) * 2 * M], axis=AX.X)
                    e.drain()
                    e.tensor_max(rsum[:], rsum[:], S["statv"][:, 1:2])
                    e.drain()
                    if h == 0:
                        e.tensor_copy(S["statv"][:, 0:1], rsum[:])
                    else:
                        e.tensor_max(S["statv"][:, 0:1], S["statv"][:, 0:1], rsum[:])
                    sems.w(e, "mm1", m0 + 6 * h + 6)
                    e.reduce_max(S["gsc"][0:1, 0:1] if h == 0 else S["gsc"][0:1, 2:3],
                                 pA[5][0:1, 0:M], axis=AX.X)
                    e.drain()
                    if h > 0:
                        li = e.tensor_max(S["gsc"][0:1, 0:1], S["gsc"][0:1, 0:1], S["gsc"][0:1, 2:3])
                    else:
                        li = e.engine_nop()
                    li.then_inc(sems.h["cve"], 1)
                sems.w(e, "mm1", m0 + 6 * NH + 1)
                e.reduce_max(S["gsc"][0:1, 1:2], pA[6][0:1, 0:128], axis=AX.X).then_inc(sems.h["cve"], 1)
                sems.n["cvx"] = x0 + 8
                sems.n["cve"] = v0 + 3 * NH + 1

            @b.tensor
            def _(e):
                sems.w(e, "cvx", x0 + 8)
                for h in range(NH):
                    ob = (h % 2) * 64
                    if h > 0:
                        sems.w(e, "cve", v0 + 3 * (h - 1) + 3)
                    for side, dst in ((0, 0), (1, 1)):
                        for dc in range(8):
                            wslice = wlb[dc][:, side * 512 + h * DH:side * 512 + (h + 1) * DH]
                            e.ldweights(wslice)
                            li = e.matmul(pA[dst][ob:ob + 64, 0:M], wslice, xlb[dc][:],
                                          start=(dc == 0), stop=(dc == 7))
                        li.then_inc(sems.h["mm1"], 1)
                    sems.w(e, "cve", v0 + 3 * h + 1)
                    for mc in range(2):
                        lw = kl[:, h * M + mc * 128:h * M + (mc + 1) * 128]
                        e.ldweights(lw)
                        e.matmul(pA[2 + mc][:, 0:M], lw, ql[:, h * M:(h + 1) * M],
                                 start=True, stop=True).then_inc(sems.h["mm1"], 1)
                    sems.w(e, "ca1", a0 + 2 * (h + 1))
                    e.ldweights(ones[:, :])
                    e.matmul(pA[4][:, 0:M], ones[:, :], e2b[:, 0:M], start=True, stop=False)
                    e.matmul(pA[4][:, 0:M], ones[:, :], e2b[:, M:2 * M],
                             start=False, stop=True).then_inc(sems.h["mm1"], 1)
                    sems.w(e, "cve", v0 + 3 * h + 2)
                    e.ldweights(ones[:, 0:64])
                    e.matmul(pA[5][0:64, 0:M], ones[:, 0:64], a2T[:, h * 2 * M:h * 2 * M + M],
                             start=True, stop=False)
                    e.matmul(pA[5][0:64, 0:M], ones[:, 0:64],
                             a2T[:, h * 2 * M + M:(h + 1) * 2 * M],
                             start=False, stop=True).then_inc(sems.h["mm1"], 1)
                sems.w(e, "cve", v0 + 3 * NH)
                e.transpose(pA[6][0:1, 0:128], S["statv"][:, 0:1],
                            S["id_f32"][:, :]).then_inc(sems.h["mm1"], 1)
                sems.n["mm1"] = m0 + 6 * NH + 1

            @b.scalar
            def _(e):
                for h in range(NH):
                    for mc in range(2):
                        sems.w(e, "mm1", m0 + 6 * h + 3 + mc)
                        if h > 0 and mc == 0:
                            sems.w(e, "cve", v0 + 3 * (h - 1) + 2)
                        e.activation(e2b[:, mc * M:(mc + 1) * M], pA[2 + mc][:, 0:M],
                                     AF.Exp, scale=SCL).then_inc(sems.h["ca1"], 1)
                sems.w(e, "cve", v0 + 3 * NH + 1)
                e.dma_start(out=S["gin"][:, :], in_=S["gsc"][0:1, 0:2]).then_inc(sems.h["sgn"], 16)
                sems.n["ca1"] = a0 + 2 * NH
                sems.n["sgn"] += 16

        with nc.Block() as b:
            @b.gpsimd
            def _(e):
                sems.w(e, "sgn", sems.n["sgn"])
                e.collective_compute("AllReduce", ALU.max,
                                     ins=[S["gin"][:, :]], outs=[S["gout"][:, :]],
                                     replica_groups=[[0, 1, 2, 3, 4, 5, 6, 7]]).then_inc(sems.h["cc"], 1)
                sems.n["cc"] += 1
                sems.w(e, "cc", sems.n["cc"])
                e.dma_start(out=S["gsc"][0:1, 4:6], in_=S["gout"][:, :]).then_inc(sems.h["sgo"], 16)
                sems.n["sgo"] += 16

        with nc.Block() as b:
            m1 = sems.n["mm1"]

            @b.vector
            def _(e):
                sems.w(e, "sgo", sems.n["sgo"])
                e.tensor_mul(S["gsc"][0:1, 6:7], S["gsc"][0:1, 4:5], S["gsc"][0:1, 5:6])
                e.drain()
                e.reciprocal(S["gsc"][0:1, 7:8], S["gsc"][0:1, 6:7]).then_inc(sems.h["cve"], 1)
                sems.w(e, "mm1", m1 + 1)
                e.tensor_copy(S["rden_rep"][:], pA[7][:, 0:1])
                sems.n["cve"] += 1

            @b.tensor
            def _(e):
                sems.w(e, "cve", sems.n["cve"])
                e.matmul(pA[7][:, 0:1], S["ones_f32"][0:1, :], S["gsc"][0:1, 7:8],
                         start=True, stop=True).then_inc(sems.h["mm1"], 1)
                sems.n["mm1"] = m1 + 1


def emit_head(S, p, hl, qkv, outp, pA):
    nc, sems = S["nc"], S["sems"]
    h = 2 * p + hl
    r0, r1 = hl * 64, hl * 64 + 64
    ql, kl, a2T = S["ql_all"], S["kl_all"], S["a2T_all"]
    ones, idc = S["ones_bf"], S["identc_sb"]

    with ExitStack() as hst:
        sb = lambda *a, **k: hst.enter_context(nc.sbuf_tensor(*a, **k))
        vaug = sb(f"vaug{h}", [128, 128 * 64], BF16, is_transpose_dest=True)
        exp1 = [sb(f"exp1_{h}_{i}", [128, N], BF16) for i in range(2)]
        e3b = [sb(f"e3b{h}_{i}", [128, M], BF16) for i in range(2)]
        t1f = sb(f"t1f{h}", [65, M], F32)
        t1id = [sb(f"t1id{h}_{i}", [128, 65], BF16) for i in range(2)]
        l3r = sb(f"l3r{h}", [1, M], F32)
        rl3r = sb(f"rl3r{h}", [1, M], F32)
        rl3pp = sb(f"rl3pp{h}", [128, 2], F32)
        a2f = sb(f"a2f{h}", [128, 512], F32)
        a2b = sb(f"a2b{h}", [128, 512], BF16)
        zb = sb(f"zb{h}", [128, 512], BF16)
        ztb = sb(f"ztb{h}", [128, 512], BF16)
        aztb = sb(f"aztb{h}", [128, 512], BF16)
        mpb = sb(f"mpb{h}", [128, 512], BF16)
        zstb = sb(f"zstb{h}", [128, 512], BF16)
        t2b = [sb(f"t2b{h}_{i}", [128, 64], BF16) for i in range(2)]
        rl1 = sb(f"rl1{h}", [128, 512], F32)

        a2s = a2T[:, h * 2 * M:(h + 1) * 2 * M]

        sx0, m20, a20 = sems.n["sxv"], sems.n["mm2"], sems.n["ca2"]
        with nc.Block() as b:
            @b.vector
            def _(e):
                if hl == 0:
                    e.memset(outp[:, 0:CK // 2], 0.0)
                    e.memset(outp[:, CK // 2 + N:CK // 2 + N + CK // 2], 0.0)
                e.memset(vaug[:], 1.0).then_inc(sems.h["cv3"], 1)

            @b.sync
            def _(e):
                sems.w(e, "cv3", sems.n["cv3"] + 1)
                for c in range(64):
                    e.dma_start_transpose(out=vaug[:, c * 128:c * 128 + 64],
                                          in_=qkv[2][r0:r1, c * 128:(c + 1) * 128]).then_inc(sems.h["sxv"], 16)
                sems.n["sxv"] = sx0 + 64 * 16
            sems.n["cv3"] += 1

            @b.tensor
            def _(e):
                for wg in range(4):
                    for mc in range(2):
                        j = 2 * wg + mc
                        if j >= 2:
                            sems.w(e, "ca2", a20 + j - 1)
                        lw = kl[:, h * M + mc * 128:h * M + (mc + 1) * 128]
                        e.ldweights(lw)
                        for wi in range(4):
                            w = wg * 4 + wi
                            li = e.matmul(pA[mc * 4 + wi][:], lw,
                                          qkv[0][:, w * 512:(w + 1) * 512],
                                          start=True, stop=True)
                        li.then_inc(sems.h["mm2"], 1)
                sems.n["mm2"] = m20 + 8

            @b.scalar
            def _(e):
                for wg in range(4):
                    for mc in range(2):
                        j = 2 * wg + mc
                        sems.w(e, "mm2", m20 + j + 1)
                        for wi in range(4):
                            w = wg * 4 + wi
                            li = e.activation(exp1[mc][:, w * 512:(w + 1) * 512],
                                              pA[mc * 4 + wi][:], AF.Exp, scale=SCL)
                        li.then_inc(sems.h["ca2"], 1)
                sems.n["ca2"] = a20 + 8

        dump_sb(S, f"vaug{h}", vaug[:])
        dump_sb(S, f"exp1a{h}", exp1[0][:])
        m30, a30 = sems.n["mm3"], sems.n["ca3"]
        with nc.Block() as b:
            @b.tensor
            def _(e):
                sems.w(e, "sxv", sems.n["sxv"])
                for c in range(64):
                    if c >= 2:
                        sems.w(e, "ca3", a30 + c - 1)
                    lw = qkv[1][:, c * 128:(c + 1) * 128]
                    e.ldweights(lw)
                    e.matmul(pA[c % 2][:, 0:M], lw, ql[:, h * M:(h + 1) * M],
                             start=True, stop=True).then_inc(sems.h["mm3"], 1)
                    sems.w(e, "ca3", a30 + c + 1)
                    e.ldweights(vaug[:, c * 128:c * 128 + 65])
                    e.matmul(pA[2][0:65, 0:M], vaug[:, c * 128:c * 128 + 65], e3b[c % 2][:],
                             start=(c == 0), stop=(c == 63)).then_inc(sems.h["mm3"], 1)
                sems.n["mm3"] = m30 + 128

            @b.scalar
            def _(e):
                for c in range(64):
                    sems.w(e, "mm3", m30 + 2 * c + 1)
                    e.activation(e3b[c % 2][:], pA[c % 2][:, 0:M],
                                 AF.Exp, scale=SCL).then_inc(sems.h["ca3"], 1)
                sems.n["ca3"] = a30 + 64

        m40, v40, w40 = sems.n["mm4"], sems.n["cv4"], sems.n["sl3"]
        with nc.Block() as b:
            @b.vector
            def _(e):
                e.tensor_copy(t1f[:], pA[2][0:65, 0:M]).then_inc(sems.h["cv4"], 1)
                e.tensor_copy(a2f[:], a2s).then_inc(sems.h["cv4"], 1)
                sems.w(e, "sl3", w40 + 16)
                e.reciprocal(rl3r[:], l3r[:]).then_inc(sems.h["cv4"], 1)
                sems.w(e, "mm4", m40 + 2)
                e.tensor_copy(rl3pp[:, 0:1], pA[3][:, 0:1])
                e.tensor_copy(rl3pp[:, 1:2], pA[3][:, 1:2]).then_inc(sems.h["cv4"], 1)
                sems.w(e, "mm4", m40 + 4)
                e.tensor_copy(t1id[0][:], pA[4][:, 0:65])
                e.tensor_copy(t1id[1][:], pA[4][:, 65:130]).then_inc(sems.h["cv4"], 1)
                sems.w(e, "mm4", m40 + 8)
                for pc in range(2):
                    for rc in range(2):
                        li = e.tensor_copy(a2b[:, pc * 256 + rc * 128:pc * 256 + (rc + 1) * 128],
                                           pA[5 + pc][:, rc * 128:(rc + 1) * 128])
                li.then_inc(sems.h["cv4"], 1)
                sems.n["cv4"] = v40 + 6

            @b.scalar
            def _(e):
                sems.w(e, "cv4", v40 + 1)
                e.dma_start(out=l3r[0:1, :], in_=t1f[64:65, :]).then_inc(sems.h["sl3"], 16)
                sems.n["sl3"] = w40 + 16

            @b.tensor
            def _(e):
                sems.w(e, "cv4", v40 + 3)
                e.transpose(pA[3][:, 0:1], rl3r[0:1, 0:128], S["id_f32"][0:1, 0:1]).then_inc(sems.h["mm4"], 1)
                e.transpose(pA[3][:, 1:2], rl3r[0:1, 128:256], S["id_f32"][0:1, 0:1]).then_inc(sems.h["mm4"], 1)
                e.transpose(pA[4][:, 0:65], t1f[0:65, 0:128], S["id_f32"][0:65, 0:65]).then_inc(sems.h["mm4"], 1)
                e.transpose(pA[4][:, 65:130], t1f[0:65, 128:256], S["id_f32"][0:65, 0:65]).then_inc(sems.h["mm4"], 1)
                sems.w(e, "cv4", v40 + 2)
                for pc in range(2):
                    for rc in range(2):
                        e.transpose(pA[5 + pc][:, rc * 128:(rc + 1) * 128],
                                    a2f[:, rc * 256 + pc * 128:rc * 256 + (pc + 1) * 128],
                                    S["id_f32"][:, :]).then_inc(sems.h["mm4"], 1)
                sems.n["mm4"] = m40 + 8

        dump_sb(S, f"t1f{h}", t1f[:])
        dump_sb(S, f"rl3pp{h}", rl3pp[:])
        dump_sb(S, f"a2b{h}", a2b[:])
        m50, v50 = sems.n["mm5"], sems.n["cv5"]
        with nc.Block() as b:
            @b.vector
            def _(e):
                e.tensor_scalar_mul(zb[:], a2s, S["rden_rep"][:, 0:1])
                e.tensor_scalar_mul(ztb[:], a2b[:], S["rden_rep"][:, 0:1]).then_inc(sems.h["cv5"], 1)
                for i in range(MP_ITERS):
                    last = i == MP_ITERS - 1
                    base = m50 + 12 * i
                    vb = v50 + 1 + 6 * i
                    sems.w(e, "mm5", base + 2)
                    for pc in range(2):
                        li = e.scalar_tensor_tensor(mpb[:, pc * 256:(pc + 1) * 256],
                                               pA[pc][:, 0:M], -1.0,
                                               idc[:, 0 * 512 + pc * 256:0 * 512 + (pc + 1) * 256],
                                               ALU.mult, ALU.add)
                    li.then_inc(sems.h["cv5"], 1)
                    sems.w(e, "mm5", base + 4)
                    for pc in range(2):
                        li = e.tensor_copy(aztb[:, pc * 256:(pc + 1) * 256], pA[2 + pc][:, 0:M])
                    li.then_inc(sems.h["cv5"], 1)
                    sems.w(e, "mm5", base + 6)
                    for pc in range(2):
                        li = e.scalar_tensor_tensor(mpb[:, pc * 256:(pc + 1) * 256],
                                               pA[4 + pc][:, 0:M], -1.0,
                                               idc[:, 1 * 512 + pc * 256:1 * 512 + (pc + 1) * 256],
                                               ALU.mult, ALU.add)
                    li.then_inc(sems.h["cv5"], 1)
                    sems.w(e, "mm5", base + 8)
                    for pc in range(2):
                        li = e.scalar_tensor_tensor(mpb[:, pc * 256:(pc + 1) * 256],
                                               pA[4 + pc][:, 0:M], -0.25,
                                               idc[:, 2 * 512 + pc * 256:2 * 512 + (pc + 1) * 256],
                                               ALU.mult, ALU.add)
                    li.then_inc(sems.h["cv5"], 1)
                    sems.w(e, "mm5", base + 10)
                    li = None
                    if not last:
                        for pc in range(2):
                            li = e.tensor_copy(zb[:, pc * 256:(pc + 1) * 256], pA[6 + pc][:, 0:M])
                    (li if li is not None else e.engine_nop()).then_inc(sems.h["cv5"], 1)
                    sems.w(e, "mm5", base + 12)
                    for pc in range(2):
                        if last:
                            li = e.tensor_scalar_mul(zstb[:, pc * 256:(pc + 1) * 256],
                                                     pA[0 + pc][:, 0:M], rl3pp[:, pc:pc + 1])
                        else:
                            li = e.tensor_copy(ztb[:, pc * 256:(pc + 1) * 256], pA[0 + pc][:, 0:M])
                    li.then_inc(sems.h["cv5"], 1)
                sems.w(e, "mm5", m50 + 12 * MP_ITERS + 2)
                for mc in range(2):
                    li = e.tensor_copy(t2b[mc][:], pA[2 + mc][:, 0:64])
                li.then_inc(sems.h["cv5"], 1)
                sems.n["cv5"] = v50 + 2 + 6 * MP_ITERS

            @b.tensor
            def _(e):
                sems.w(e, "cv5", v50 + 1)
                for i in range(MP_ITERS):
                    base = m50 + 12 * i
                    vb = v50 + 1 + 6 * i
                    if i > 0:
                        sems.w(e, "cv5", vb)
                    for rc in range(2):
                        for mc in range(2):
                            lw = a2s[:, mc * 256 + rc * 128:mc * 256 + rc * 128 + 128]
                            e.ldweights(lw)
                            li = e.matmul(pA[rc][:, 0:M], lw, zb[:, mc * 256:(mc + 1) * 256],
                                     start=(mc == 0), stop=(mc == 1))
                        li.then_inc(sems.h["mm5"], 1)
                    for ic in range(2):
                        for mc in range(2):
                            lw = zb[:, mc * 256 + ic * 128:mc * 256 + ic * 128 + 128]
                            e.ldweights(lw)
                            li = e.matmul(pA[2 + ic][:, 0:M], lw, a2s[:, mc * 256:(mc + 1) * 256],
                                     start=(mc == 0), stop=(mc == 1))
                        li.then_inc(sems.h["mm5"], 1)
                    sems.w(e, "cv5", vb + 2)
                    for rc in range(2):
                        for ic in range(2):
                            lw = aztb[:, ic * 256 + rc * 128:ic * 256 + rc * 128 + 128]
                            e.ldweights(lw)
                            li = e.matmul(pA[4 + rc][:, 0:M], lw, mpb[:, ic * 256:(ic + 1) * 256],
                                     start=(ic == 0), stop=(ic == 1))
                        li.then_inc(sems.h["mm5"], 1)
                    sems.w(e, "cv5", vb + 3)
                    for rc in range(2):
                        for ic in range(2):
                            lw = aztb[:, ic * 256 + rc * 128:ic * 256 + rc * 128 + 128]
                            e.ldweights(lw)
                            li = e.matmul(pA[4 + rc][:, 0:M], lw, mpb[:, ic * 256:(ic + 1) * 256],
                                     start=(ic == 0), stop=(ic == 1))
                        li.then_inc(sems.h["mm5"], 1)
                    sems.w(e, "cv5", vb + 4)
                    for rc in range(2):
                        for ic in range(2):
                            lw = ztb[:, ic * 256 + rc * 128:ic * 256 + rc * 128 + 128]
                            e.ldweights(lw)
                            li = e.matmul(pA[6 + rc][:, 0:M], lw, mpb[:, ic * 256:(ic + 1) * 256],
                                     start=(ic == 0), stop=(ic == 1))
                        li.then_inc(sems.h["mm5"], 1)
                    for jc in range(2):
                        for ic in range(2):
                            lw = mpb[:, ic * 256 + jc * 128:ic * 256 + jc * 128 + 128]
                            e.ldweights(lw)
                            li = e.matmul(pA[0 + jc][:, 0:M], lw, ztb[:, ic * 256:(ic + 1) * 256],
                                     start=(ic == 0), stop=(ic == 1))
                        li.then_inc(sems.h["mm5"], 1)
                sems.w(e, "cv5", v50 + 1 + 6 * MP_ITERS)
                for mc in range(2):
                    for ic in range(2):
                        lw = zstb[:, ic * 256 + mc * 128:ic * 256 + mc * 128 + 128]
                        e.ldweights(lw)
                        li = e.matmul(pA[2 + mc][:, 0:64], lw, t1id[ic][:, 0:64],
                                 start=(ic == 0), stop=(ic == 1))
                    li.then_inc(sems.h["mm5"], 1)
                sems.n["mm5"] = m50 + 12 * MP_ITERS + 2

        dump_sb(S, f"zstb{h}", zstb[:])
        dump_sb(S, f"t2b{h}", t2b[0][:])
        m60, v60 = sems.n["mm6"], sems.n["cv6"]
        ob = hl * 64
        with nc.Block() as b:
            @b.tensor
            def _(e):
                for wg in range(4):
                    if wg >= 1:
                        sems.w(e, "cv6", v60 + wg)
                    for mc in range(2):
                        e.ldweights(t2b[mc][:])
                        for wi in range(4):
                            w = wg * 4 + wi
                            e.matmul(pA[wi][ob:ob + 64, :], t2b[mc][:],
                                     exp1[mc][:, w * 512:(w + 1) * 512],
                                     start=(mc == 0), stop=(mc == 1))
                    e.ldweights(ones[:, 0:64])
                    for wi in range(4):
                        w = wg * 4 + wi
                        e.matmul(pA[4 + wi][ob:ob + 64, :], ones[:, 0:64],
                                 exp1[0][:, w * 512:(w + 1) * 512], start=True, stop=False)
                        li = e.matmul(pA[4 + wi][ob:ob + 64, :], ones[:, 0:64],
                                 exp1[1][:, w * 512:(w + 1) * 512], start=False, stop=True)
                    li.then_inc(sems.h["mm6"], 1)
                sems.n["mm6"] = m60 + 4

            @b.vector
            def _(e):
                for wg in range(4):
                    sems.w(e, "mm6", m60 + wg + 1)
                    for wi in range(4):
                        w = wg * 4 + wi
                        e.reciprocal(rl1[ob:ob + 64, :], pA[4 + wi][ob:ob + 64, :])
                        e.drain()
                        li = e.tensor_mul(outp[ob:ob + 64, CK // 2 + w * 512:CK // 2 + (w + 1) * 512],
                                     pA[wi][ob:ob + 64, :], rl1[ob:ob + 64, :])
                        e.drain()
                    li.then_inc(sems.h["cv6"], 1)
                sems.n["cv6"] = v60 + 4


def dump_sb(S, name, ap):
    if name not in S.get("dumpreq", ()):
        return
    nc, sems = S["nc"], S["sems"]
    t = nc.dram_tensor("d_" + name, list(ap.shape), ap.dtype, kind="ExternalOutput").ap()
    with nc.Block() as b:
        @b.scalar
        def _(e):
            e.dma_start(out=t[...], in_=ap).then_inc(sems.h["scv"], 16)
            sems.n["scv"] += 16
            sems.w(e, "scv", sems.n["scv"])


def maybe_dump(S, name, src_ap):
    if name not in S["dump"]:
        return
    nc, sems = S["nc"], S["sems"]
    with nc.Block() as b:
        @b.scalar
        def _(e):
            e.dma_start(out=S["dump"][name][...], in_=src_ap).then_inc(sems.h["scv"], 16)
            sems.n["scv"] += 16
            sems.w(e, "scv", sems.n["scv"])


def emit_conv_store(S, p, outp, cvb, ostg, pA):
    nc, sems = S["nc"], S["sems"]
    m70, v70 = sems.n["mm7"], sems.n["cv7"]
    s70 = [sems.n["sst0"], sems.n["sst1"]]
    with nc.Block() as b:
        @b.tensor
        def _(e):
            for wg in range(4):
                if wg >= 2:
                    sems.w(e, "cv7", v70 + wg - 1)
                for j in range(CK):
                    lw = cvb[:, j * 128:(j + 1) * 128]
                    e.ldweights(lw)
                    for wi in range(4):
                        w = wg * 4 + wi
                        li = e.matmul(pA[(wg % 2) * 4 + wi][:],
                                 lw, outp[:, w * 512 + j:w * 512 + j + 512],
                                 start=(j == 0), stop=(j == CK - 1))
                li.then_inc(sems.h["mm7"], 1)
            sems.n["mm7"] = m70 + 4

        @b.vector
        def _(e):
            for wg in range(4):
                sems.w(e, "mm7", m70 + wg + 1)
                if wg >= 2:
                    sems.w(e, f"sst{wg % 2}", s70[wg % 2] + 16)
                for wi in range(4):
                    w = wg * 4 + wi
                    li = e.scalar_tensor_tensor(ostg[wg % 2][:, wi * 512:(wi + 1) * 512],
                                           pA[(wg % 2) * 4 + wi][:], 0.0,
                                           outp[:, CK // 2 + w * 512:CK // 2 + (w + 1) * 512],
                                           ALU.add, ALU.add)
                li.then_inc(sems.h["cv7"], 1)
            sems.n["cv7"] = v70 + 4

        @b.scalar
        def _(e):
            for wg in range(4):
                sems.w(e, "cv7", v70 + wg + 1)
                e.dma_start(out=S["outall"][p][:, wg * 2048:(wg + 1) * 2048],
                            in_=ostg[wg % 2][:]).then_inc(sems.h[f"sst{wg % 2}"], 16)
            sems.w(e, "sst0", s70[0] + 32)
            sems.w(e, "sst1", s70[1] + 32)
            e.dma_start(out=S["cin"][p][:, :],
                        in_=S["outall"][p][:, DS(S["v_oth"], NT)]).then_inc(sems.h["sci"], 16)
            sems.n["sst0"] = s70[0] + 32
            sems.n["sst1"] = s70[1] + 32
            sems.n["sci"] += 16


def emit_exchange(S):
    nc, sems = S["nc"], S["sems"]
    with nc.Block() as b:
        @b.gpsimd
        def _(e):
            sems.w(e, "sci", sems.n["sci"])
            e.collective_compute("AllGather", ALU.bypass,
                                 ins=[S["cin"][:, :, :]], outs=[S["cout"][:, :, :, :]],
                                 replica_groups=[[0, 1], [2, 3], [4, 5], [6, 7]]).then_inc(sems.h["cc"], 1)
            sems.n["cc"] += 1


def emit_fc_ln(S):
    nc, sems = S["nc"], S["sems"]
    NTT = NT // 128
    RS = AF.Abs_reciprocal_sqrt
    with ExitStack() as pst:
        sb = lambda *a, **k: pst.enter_context(nc.sbuf_tensor(*a, **k))
        ps_ = lambda *a, **k: pst.enter_context(nc.psum_tensor(*a, **k))
        wff = [sb(f"wff{i}", [128, D], F32) for i in range(2)]
        wfb = [sb(f"wfb{i}", [128, D], BF16) for i in range(8)]
        rwf = sb("rwf", [128, NEXP], F32)
        rwb = [sb(f"rwb{i}", [128, NEXP], BF16) for i in range(8)]
        bfc_sb = sb("bfc_sb", [128, D], F32)
        lng_sb = sb("lng_sb", [128, D], F32)
        lnb_sb = sb("lnb_sb", [128, D], F32)
        lt = [sb(f"lt{i}", [128, 128], BF16) for i in range(16)]
        xo = [sb(f"xo{i}", [128, D], F32) for i in range(2)]
        res_t = sb("res_t", [128, D], F32)
        sq_t = sb("sq_t", [128, D], F32)
        r1t = sb("r1t", [128, D], F32)
        hh = sb("hh", [128, D], F32)
        hb = sb("hb", [128, D], BF16)
        hstg = sb("hstg", [128, 8192], BF16, is_transpose_dest=True)
        lsb = sb("lsb", [8, 128], F32)
        logall = sb("logall", [128, NEXP * NTT], F32)
        el = sb("el", [128, NEXP], F32)
        stt_ = sb("stt_", [128, 8], F32)
        gt = sb("gt", [128, NEXP], F32)
        gates8 = sb("gates8", [8, NT], F32)
        pF = [ps_(f"pF{i}", [128, 512], F32) for i in range(8)]

        v0a = sems.n["cv8"]
        q0f = [sems.n["swq0"], sems.n["swq1"]]
        g0f, c0f = sems.n["sgr"], sems.n["scv"]
        with nc.Block() as b:
            @b.sync
            def _(e):
                for dc in range(8):
                    if dc >= 2:
                        sems.w(e, "cv8", v0a + dc - 1)
                    e.dma_start(out=wff[dc % 2][:], in_=S["wfc"][dc]).then_inc(sems.h[f"swq{dc % 2}"], 16)
                e.dma_start(out=bfc_sb[:], in_=S["bfc_rep"][:, :]).then_inc(sems.h["scv"], 16)
                e.dma_start(out=lng_sb[:], in_=S["lng_rep"][:, :]).then_inc(sems.h["scv"], 16)
                e.dma_start(out=lnb_sb[:], in_=S["lnb_rep"][:, :]).then_inc(sems.h["scv"], 16)
                for dc in range(8):
                    sems.w(e, "cv8", v0a + 8 + dc)
                    e.dma_start(out=rwf[:], in_=S["rwT"][dc]).then_inc(sems.h["sgr"], 16)
                sems.n["swq0"] = q0f[0] + 16 * 4
                sems.n["swq1"] = q0f[1] + 16 * 4
                sems.n["scv"] = c0f + 16 * 3
                sems.n["sgr"] = g0f + 16 * 8
                sems.w(e, "scv", sems.n["scv"])

            @b.vector
            def _(e):
                for dc in range(8):
                    sems.w(e, f"swq{dc % 2}", q0f[dc % 2] + 16 * (dc // 2 + 1))
                    e.tensor_copy(wfb[dc][:], wff[dc % 2][:]).then_inc(sems.h["cv8"], 1)
                for dc in range(8):
                    sems.w(e, "sgr", g0f + 16 * (dc + 1))
                    e.tensor_copy(rwb[dc][:], rwf[:]).then_inc(sems.h["cv8"], 1)
                sems.n["cv8"] = v0a + 16

        f0 = [sems.n["sfc0"], sems.n["sfc1"]]
        r10, hf0 = sems.n["sr1"], sems.n["shf"]
        m0, v0, a0, t0 = sems.n["mm8"], sems.n["cv8"], sems.n["ca8"], sems.n["sx8"]
        with nc.Block() as b:
            @b.sync
            def _(e):
                sems.w(e, "cc", sems.n["cc"])
                for tt in range(NTT):
                    if tt >= 2:
                        sems.w(e, "mm8", m0 + 3 * (tt - 1))
                        sems.w(e, "cv8", v0 + 7 * (tt - 1))
                    for kc in range(8):
                        if kc < 4:
                            src = S["outall"][kc][:, DS(S["v_own"] + tt * 128, 128)]
                        else:
                            src = S["cout"][DS(S["v_slot"], 1), kc - 4].rearrange(
                                "s p t -> p (s t)")[:, tt * 128:(tt + 1) * 128]
                        e.dma_start(out=lt[(tt % 2) * 8 + kc][:], in_=src).then_inc(sems.h[f"sfc{tt % 2}"], 16)
                    e.dma_start(out=xo[tt % 2][:], in_=S["x_own"][tt]).then_inc(sems.h[f"sfc{tt % 2}"], 16)
                sems.n["sfc0"] = f0[0] + 16 * 9 * (NTT // 2)
                sems.n["sfc1"] = f0[1] + 16 * 9 * (NTT // 2)

            @b.tensor
            def _(e):
                for tt in range(NTT):
                    sems.w(e, f"sfc{tt % 2}", f0[tt % 2] + 16 * 9 * (tt // 2 + 1))
                    if tt >= 2:
                        sems.w(e, "cv8", v0 + 7 * (tt - 1))
                    for kc in range(8):
                        e.ldweights(lt[(tt % 2) * 8 + kc][:])
                        for half in range(2):
                            li = e.matmul(pF[(tt % 2) * 2 + half][:], lt[(tt % 2) * 8 + kc][:],
                                     wfb[kc][:, half * 512:(half + 1) * 512],
                                     start=(kc == 0), stop=(kc == 7))
                    li.then_inc(sems.h["mm8"], 1)
                    if tt >= 1:
                        sems.w(e, "cv8", v0 + 7 * (tt - 1) + 6)
                    sems.w(e, "sx8", t0 + 16 * 8 * (tt + 1))
                    for dc in range(8):
                        e.ldweights(rwb[dc][:])
                        li = e.matmul(pF[4][0:8, 0:128], rwb[dc][:],
                                 hstg[:, dc * 1024 + (tt % 8) * 128:dc * 1024 + (tt % 8) * 128 + 128],
                                 start=(dc == 0), stop=(dc == 7))
                    li.then_inc(sems.h["mm8"], 1)
                    sems.w(e, "cv8", v0 + 7 * tt + 6)
                    e.transpose(pF[5][:, 0:8], lsb[0:8, :], S["id_f32"][0:8, 0:8]).then_inc(sems.h["mm8"], 1)
                sems.n["mm8"] = m0 + 3 * NTT

            @b.vector
            def _(e):
                for tt in range(NTT):
                    sems.w(e, "mm8", m0 + 3 * tt + 1)
                    if tt >= 1:
                        sems.w(e, "ca8", a0 + 4 * (tt - 1) + 1)
                    e.scalar_tensor_tensor(res_t[:, 0:512], pF[(tt % 2) * 2][:], 0.0,
                                           bfc_sb[:, 0:512], ALU.add, ALU.add)
                    e.scalar_tensor_tensor(res_t[:, 512:1024], pF[(tt % 2) * 2 + 1][:], 0.0,
                                           bfc_sb[:, 512:1024], ALU.add, ALU.add)
                    sems.w(e, f"sfc{tt % 2}", f0[tt % 2] + 16 * 9 * (tt // 2 + 1))
                    e.drain()
                    e.tensor_add(res_t[:], res_t[:], xo[tt % 2][:])
                    e.drain()
                    e.reduce_sum(stt_[:, 0:1], res_t[:], axis=AX.X).then_inc(sems.h["cv8"], 1)
                    sems.w(e, "ca8", a0 + 4 * tt + 1)
                    e.drain()
                    e.tensor_scalar(stt_[:, 2:3], stt_[:, 0:1], 1.0 / D, None, ALU.mult)
                    e.tensor_scalar(stt_[:, 3:4], stt_[:, 1:2], 1.0 / D, None, ALU.mult)
                    e.drain()
                    e.tensor_mul(stt_[:, 6:7], stt_[:, 2:3], stt_[:, 2:3])
                    e.drain()
                    e.tensor_sub(stt_[:, 4:5], stt_[:, 3:4], stt_[:, 6:7])
                    e.drain()
                    e.tensor_scalar(stt_[:, 4:5], stt_[:, 4:5], EPS, None, ALU.add).then_inc(sems.h["cv8"], 1)
                    sems.w(e, "ca8", a0 + 4 * tt + 2)
                    if tt >= 1:
                        sems.w(e, "sr1", r10 + 16 * tt)
                    e.reciprocal(stt_[:, 7:8], stt_[:, 5:6])
                    e.drain()
                    e.tensor_scalar(stt_[:, 6:7], stt_[:, 2:3], stt_[:, 7:8], -1.0,
                                    ALU.mult, ALU.mult)
                    e.drain()
                    e.tensor_scalar(r1t[:], res_t[:], stt_[:, 7:8], stt_[:, 6:7],
                                    ALU.mult, ALU.add).then_inc(sems.h["cv8"], 1)
                    sems.w(e, "ca8", a0 + 4 * tt + 3)
                    e.drain()
                    e.reduce_sum(stt_[:, 0:1], r1t[:], axis=AX.X)
                    e.drain()
                    e.tensor_scalar(stt_[:, 2:3], stt_[:, 0:1], 1.0 / D, None, ALU.mult)
                    e.tensor_scalar(stt_[:, 3:4], stt_[:, 1:2], 1.0 / D, None, ALU.mult)
                    e.drain()
                    e.tensor_mul(stt_[:, 6:7], stt_[:, 2:3], stt_[:, 2:3])
                    e.drain()
                    e.tensor_sub(stt_[:, 4:5], stt_[:, 3:4], stt_[:, 6:7])
                    e.drain()
                    e.tensor_scalar(stt_[:, 4:5], stt_[:, 4:5], EPS, None, ALU.add).then_inc(sems.h["cv8"], 1)
                    sems.w(e, "ca8", a0 + 4 * tt + 4)
                    if tt >= 1:
                        sems.w(e, "sx8", t0 + 16 * 8 * tt)
                    e.reciprocal(stt_[:, 7:8], stt_[:, 5:6])
                    e.drain()
                    e.tensor_scalar(stt_[:, 6:7], stt_[:, 2:3], stt_[:, 7:8], -1.0,
                                    ALU.mult, ALU.mult)
                    e.drain()
                    e.tensor_scalar(hh[:], r1t[:], stt_[:, 7:8], stt_[:, 6:7],
                                    ALU.mult, ALU.add)
                    e.drain()
                    e.tensor_mul(hh[:], hh[:], lng_sb[:])
                    e.drain()
                    e.tensor_add(hb[:], hh[:], lnb_sb[:]).then_inc(sems.h["cv8"], 1)
                    sems.w(e, "mm8", m0 + 3 * tt + 2)
                    e.tensor_copy(lsb[0:8, :], pF[4][0:8, 0:128]).then_inc(sems.h["cv8"], 1)
                    sems.w(e, "mm8", m0 + 3 * tt + 3)
                    e.tensor_copy(logall[:, tt * NEXP:(tt + 1) * NEXP],
                                  pF[5][:, 0:8]).then_inc(sems.h["cv8"], 1)
                sems.n["cv8"] = v0 + 7 * NTT

            @b.scalar
            def _(e):
                for tt in range(NTT):
                    sems.w(e, "cv8", v0 + 7 * tt + 1)
                    e.activation(sq_t[:], res_t[:], AF.Square,
                                 accum_out=stt_[:, 1:2]).then_inc(sems.h["ca8"], 1)
                    sems.w(e, "cv8", v0 + 7 * tt + 2)
                    e.activation(stt_[:, 5:6], stt_[:, 4:5], AF.Sqrt).then_inc(sems.h["ca8"], 1)
                    sems.w(e, "cv8", v0 + 7 * tt + 3)
                    e.dma_start(out=S["r1n_hbm"][tt], in_=r1t[:]).then_inc(sems.h["sr1"], 16)
                    e.drain()
                    e.activation(sq_t[:], r1t[:], AF.Square,
                                 accum_out=stt_[:, 1:2]).then_inc(sems.h["ca8"], 1)
                    sems.w(e, "cv8", v0 + 7 * tt + 4)
                    e.activation(stt_[:, 5:6], stt_[:, 4:5], AF.Sqrt).then_inc(sems.h["ca8"], 1)
                    sems.w(e, "cv8", v0 + 7 * tt + 5)
                    for dc in range(8):
                        e.dma_start_transpose(
                            out=hstg[:, dc * 1024 + (tt % 8) * 128:dc * 1024 + (tt % 8) * 128 + 128],
                            in_=hb[:, dc * 128:(dc + 1) * 128]).then_inc(sems.h["sx8"], 16)
                    if tt % 8 == 7:
                        sems.w(e, "sx8", t0 + 16 * 8 * (tt + 1))
                        for dc in range(8):
                            e.dma_start(out=S["h_hbm"][dc][:, (tt // 8) * 1024:(tt // 8 + 1) * 1024],
                                        in_=hstg[:, dc * 1024:(dc + 1) * 1024]).then_inc(sems.h["shf"], 16)
                        sems.w(e, "shf", hf0 + 16 * 8 * ((tt + 1) // 8))
                sems.n["ca8"] = a0 + 4 * NTT
                sems.n["sx8"] = t0 + 16 * 8 * NTT
                sems.n["sr1"] = r10 + 16 * NTT
                sems.n["shf"] = hf0 + 16 * 8 * (NTT // 8)
                sems.w(e, "sr1", sems.n["sr1"])

        g0, gm0, gv0, gw0 = sems.n["ca9"], sems.n["mm9"], sems.n["cv9"], sems.n["sw9"]
        with nc.Block() as b:
            @b.vector
            def _(e):
                for tt in range(NTT):
                    lg = logall[:, tt * NEXP:(tt + 1) * NEXP]
                    e.reduce_max(stt_[:, 0:1], lg, axis=AX.X)
                    e.drain()
                    e.tensor_scalar_mul(stt_[:, 1:2], stt_[:, 0:1], -1.0).then_inc(sems.h["cv9"], 1)
                    sems.w(e, "ca9", g0 + tt + 1)
                    e.reduce_sum(stt_[:, 2:3], el[:], axis=AX.X)
                    e.drain()
                    e.reciprocal(stt_[:, 3:4], stt_[:, 2:3])
                    e.tensor_scalar(gt[:], lg, stt_[:, 0:1], None, ALU.is_ge)
                    e.drain()
                    e.tensor_scalar_mul(gt[:], gt[:], stt_[:, 3:4]).then_inc(sems.h["cv9"], 1)
                    sems.w(e, "mm9", gm0 + tt + 1)
                    e.tensor_copy(gates8[0:8, tt * 128:(tt + 1) * 128],
                                  pF[6][0:8, 0:128]).then_inc(sems.h["cv9"], 1)
                sems.n["cv9"] = gv0 + 3 * NTT

            @b.scalar
            def _(e):
                for tt in range(NTT):
                    sems.w(e, "cv9", gv0 + 3 * tt + 1)
                    e.activation(el[:], logall[:, tt * NEXP:(tt + 1) * NEXP],
                                 AF.Exp, bias=stt_[:, 1:2]).then_inc(sems.h["ca9"], 1)
                sems.n["ca9"] = g0 + NTT

            @b.tensor
            def _(e):
                for tt in range(NTT):
                    sems.w(e, "cv9", gv0 + 3 * tt + 2)
                    e.transpose(pF[6][0:8, 0:128], gt[:, 0:NEXP],
                                S["id_f32"][:, :]).then_inc(sems.h["mm9"], 1)
                sems.n["mm9"] = gm0 + NTT

        with nc.Block() as b:
            @b.scalar
            def _(e):
                e.dma_start(out=S["grows_hbm"][:, :],
                            in_=gates8[0:8, :]).then_inc(sems.h["sw9"], 16)
                sems.n["sw9"] = gw0 + 16
                sems.w(e, "sw9", sems.n["sw9"])


def _quant_tiles(e_id):
    pass


def emit_ffn(S):
    nc, sems = S["nc"], S["sems"]
    EXP = [("sh", S["sw1"], S["sw2"], None)] + \
          [(f"e{i}", S["ew1"][i], S["ew2"][i], i) for i in range(NEXP)]
    with ExitStack() as pst:
        sb = lambda *a, **k: pst.enter_context(nc.sbuf_tensor(*a, **k))
        ps_ = lambda *a, **k: pst.enter_context(nc.psum_tensor(*a, **k))
        rawb = [sb(f"rawb{i}", [128, FD], F32) for i in range(2)]
        q2b = sb("q2b", [128, FD], BF16)
        wqtb = [sb(f"wqtb{i}", [128, FD], BF16) for i in range(2)]
        growb = sb("growb", [1, 1024], F32)
        sums = sb("sums", [128, 4], F32)
        w1qT = [[sb(f"w1qT{bu}_{kc}", [128, FD], BF16, is_transpose_dest=True)
                 for kc in range(8)] for bu in range(2)]
        w2qT = [sb(f"w2qT{fc}", [128, D], BF16, is_transpose_dest=True) for fc in range(16)]
        hsb = [[sb(f"hsb{bu}_{kc}", [128, 1024], BF16) for kc in range(8)] for bu in range(2)]
        h1b = sb("h1b", [128, 16 * 1024], BF16)
        gate_sb = sb("gate_sb", [128, 1024], BF16)
        eo = [sb(f"eo{i}", [128, 1024], F32) for i in range(2)]
        pW = [ps_(f"pW{i}", [128, 512], F32) for i in range(8)]

        def quant_tiles_of(pi):
            nm, w1src, w2src, _ = EXP[pi]
            return [("w1", r, w1src[r], 1024, 1.0 / 1024) for r in range(16)] + \
                   [("w2", r, w2src[r], 2048, 1.0 / 2048) for r in range(8)]

        def emit_quant_scalar(e, pi, tiles_rng, bases):
            par = pi % 2
            tiles = quant_tiles_of(pi)
            r0p, c0, v0, s0p = bases
            for t in tiles_rng:
                kind, r, src_, width, inv = tiles[t]
                q = t % 2
                if t >= 2:
                    sems.w(e, "cvz", v0 + t - 1)
                e.dma_start(out=rawb[q][:, 0:width], in_=src_).then_inc(sems.h[f"rwl{q}"], 16)
                sems.w(e, f"rwl{q}", r0p[q] + 16 * (t // 2 + 1))
                e.drain()
                e.activation(q2b[:, 0:width], rawb[q][:, 0:width], AF.Abs,
                             accum_out=sums[:, 0:1]).then_inc(sems.h["caz"], 1)
                sems.w(e, "cvz", v0 + t + 1)
                if kind == "w1":
                    e.dma_start(out=S["w1scr"][par][:, r * 128:(r + 1) * 128, :].rearrange(
                                    "c q d -> q c d"),
                                in_=wqtb[q][:, 0:1024].rearrange(
                                    "q (c d) -> q c d", d=128)).then_inc(sems.h[f"scw{q}"], 16)
                else:
                    e.dma_start(out=S["w2scr"][par][:, r * 128:(r + 1) * 128, :].rearrange(
                                    "c q d -> q c d"),
                                in_=wqtb[q][:, 0:2048].rearrange(
                                    "q (c d) -> q c d", d=128)).then_inc(sems.h[f"scw{q}"], 16)

        def emit_quant_vector(e, pi, tiles_rng, bases):
            tiles = quant_tiles_of(pi)
            r0p, c0, v0, s0p = bases
            for t in tiles_rng:
                kind, r, src_, width, inv = tiles[t]
                q = t % 2
                sems.w(e, "caz", c0 + t + 1)
                if t >= 2:
                    sems.w(e, f"scw{q}", s0p[q] + 16 * (t // 2))
                e.tensor_scalar(sums[:, 1:2], sums[:, 0:1], inv, 1e-5, ALU.mult, ALU.max)
                e.drain()
                e.tensor_scalar_mul(sums[:, 2:3], sums[:, 1:2], 0.5)
                e.tensor_scalar_mul(sums[:, 3:4], sums[:, 1:2], -0.5)
                e.drain()
                e.tensor_scalar(wqtb[q][:, 0:width], rawb[q][:, 0:width],
                                sums[:, 2:3], sums[:, 1:2], ALU.is_ge, ALU.mult)
                e.tensor_scalar(q2b[:, 0:width], rawb[q][:, 0:width],
                                sums[:, 3:4], sums[:, 1:2], ALU.is_le, ALU.mult)
                e.drain()
                e.tensor_sub(wqtb[q][:, 0:width], wqtb[q][:, 0:width],
                             q2b[:, 0:width]).then_inc(sems.h["cvz"], 1)

        def quant_bases():
            return ([sems.n["rwl0"], sems.n["rwl1"]], sems.n["caz"], sems.n["cvz"],
                    [sems.n["scw0"], sems.n["scw1"]])

        def quant_commit(bases):
            r0p, c0, v0, s0p = bases
            sems.n["rwl0"] = r0p[0] + 16 * 12
            sems.n["rwl1"] = r0p[1] + 16 * 12
            sems.n["scw0"] = s0p[0] + 16 * 12
            sems.n["scw1"] = s0p[1] + 16 * 12
            sems.n["caz"] = c0 + 24
            sems.n["cvz"] = v0 + 24

        def emit_xbar_sync(e, pi, s_need0, s_need1, m_need):
            par = pi % 2
            bu = pi % 2
            S["xbar_base"] = sems.n["sxw"]
            sems.w(e, "scw0", s_need0)
            sems.w(e, "scw1", s_need1)
            for dc in range(8):
                e.dma_start_transpose(out=w1qT[bu][dc][:],
                                      in_=S["w1scr"][par][dc]).then_inc(sems.h["sxw"], 16)
            sems.w(e, "mm9", m_need)
            for fc in range(16):
                e.dma_start_transpose(out=w2qT[fc][:],
                                      in_=S["w2scr"][par][fc]).then_inc(sems.h["sxw"], 16)
            sems.n["sxw"] += 16 * 24

        qb = quant_bases()
        with nc.Block() as b:
            @b.scalar
            def _(e):
                emit_quant_scalar(e, 0, range(24), qb)

            @b.vector
            def _(e):
                emit_quant_vector(e, 0, range(24), qb)
        quant_commit(qb)
        with nc.Block() as b:
            s0_, s1_, m_ = sems.n["scw0"], sems.n["scw1"], sems.n["mm9"]

            @b.sync
            def _(e):
                emit_xbar_sync(e, 0, s0_, s1_, m_)

        for pi in range(len(EXP)):
            nm, w1src, w2src, grow = EXP[pi]
            bu = pi % 2
            is_sh = grow is None
            x9 = S["xbar_base"]
            m0, v0, c0 = sems.n["mm9"], sems.n["cv9"], sems.n["ca9"]
            gm = 0 if is_sh else 1
            per_grp_mm = gm + 16 + 8
            g0 = [sems.n["gda"], sems.n["gdb"]]
            per_grp_cv = (0 if is_sh else 1) + 8
            gld0 = sems.n["sgr"]
            h0p = [sems.n["shh0"], sems.n["shh1"]]
            qb = quant_bases() if pi + 1 < len(EXP) else None
            if qb is not None:
                scw_next = [qb[3][0] + 16 * 12, qb[3][1] + 16 * 12]
            m_w2_end = m0 + 4 * per_grp_mm
            with nc.Block() as b:
                @b.sync
                def _(e):
                    for grp in range(4):
                        if not is_sh:
                            if grp > 0:
                                sems.w(e, "mm9", m0 + (grp - 1) * per_grp_mm + 1)
                            e.dma_start(out=growb[0:1, :],
                                        in_=S["grows_hbm"][grow:grow + 1, grp * 1024:(grp + 1) * 1024]).then_inc(sems.h["sgr"], 16)
                        if grp >= 2:
                            sems.w(e, "mm9", m0 + (grp - 1) * per_grp_mm + gm + 16)
                        for kc in range(8):
                            e.dma_start(out=hsb[grp % 2][kc][:],
                                        in_=S["h_hbm"][kc][:, grp * 1024:(grp + 1) * 1024]).then_inc(sems.h[f"shh{grp % 2}"], 16)
                    if not is_sh:
                        sems.n["sgr"] = gld0 + 16 * 4
                    sems.n["shh0"] = h0p[0] + 16 * 16
                    sems.n["shh1"] = h0p[1] + 16 * 16
                    if qb is not None:
                        emit_xbar_sync(e, pi + 1, scw_next[0], scw_next[1], m_w2_end)

                @b.tensor
                def _(e):
                    for grp in range(4):
                        gb = m0 + grp * per_grp_mm
                        if not is_sh:
                            if grp > 0:
                                sems.w(e, "cv9", v0 + grp * per_grp_cv)
                            sems.w(e, "sgr", gld0 + 16 * (grp + 1))
                            e.matmul(pW[6][:], S["ones_f32"][0:1, :],
                                     growb[0:1, 0:512], start=True, stop=True)
                            li = e.matmul(pW[7][:], S["ones_f32"][0:1, :],
                                     growb[0:1, 512:1024], start=True, stop=True)
                            li.then_inc(sems.h["mm9"], 1)
                        sems.w(e, "sxw", x9 + 16 * 24)
                        sems.w(e, f"shh{grp % 2}", h0p[grp % 2] + 16 * 8 * (grp // 2 + 1))
                        for fc in range(16):
                            if grp * 16 + fc >= 2:
                                sems.w(e, "ca9", c0 + grp * 16 + fc - 1)
                            for kc in range(8):
                                lw = w1qT[bu][kc][:, fc * 128:(fc + 1) * 128]
                                e.ldweights(lw)
                                for wi in range(2):
                                    li = e.matmul(pW[(fc % 2) * 2 + wi][:], lw,
                                             hsb[grp % 2][kc][:, wi * 512:(wi + 1) * 512],
                                             start=(kc == 0), stop=(kc == 7))
                            li.then_inc(sems.h["mm9"], 1)
                        sems.w(e, "sxw", x9 + 16 * 24)
                        sems.w(e, "ca9", c0 + (grp + 1) * 16)
                        for dc in range(8):
                            if dc >= 2:
                                sems.w(e, "cv9", v0 + grp * per_grp_cv + (0 if is_sh else 1) + dc - 1)
                            bk = 4 if (dc % 2 == 0) else 6
                            if dc <= 1:
                                sems.w(e, "cv9", v0 + grp * per_grp_cv + (0 if is_sh else 1))
                            for fc in range(16):
                                lw = w2qT[fc][:, dc * 128:(dc + 1) * 128]
                                e.ldweights(lw)
                                for wi in range(2):
                                    li = e.matmul(pW[bk + wi][:], lw,
                                             h1b[:, fc * 1024 + wi * 512:fc * 1024 + (wi + 1) * 512],
                                             start=(fc == 0), stop=(fc == 15))
                            li.then_inc(sems.h["mm9"], 1)
                    sems.n["mm9"] = m0 + 4 * per_grp_mm

                @b.scalar
                def _(e):
                    for grp in range(4):
                        if qb is not None:
                            emit_quant_scalar(e, pi + 1, range(6 * grp, 6 * grp + 6), qb)
                        for fc in range(16):
                            sems.w(e, "mm9", m0 + grp * per_grp_mm + gm + fc + 1)
                            for wi in range(2):
                                li = e.activation(h1b[:, fc * 1024 + wi * 512:fc * 1024 + (wi + 1) * 512],
                                             pW[(fc % 2) * 2 + wi][:], AF.Silu)
                            li.then_inc(sems.h["ca9"], 1)
                    sems.n["ca9"] = c0 + 64

                @b.vector
                def _(e):
                    for grp in range(4):
                        if qb is not None:
                            emit_quant_vector(e, pi + 1, range(6 * grp, 6 * grp + 6), qb)
                        if not is_sh:
                            sems.w(e, "mm9", m0 + grp * per_grp_mm + 1)
                            e.tensor_copy(gate_sb[:, 0:512], pW[6][:])
                            e.tensor_copy(gate_sb[:, 512:1024], pW[7][:]).then_inc(sems.h["cv9"], 1)
                        for dc in range(8):
                            sems.w(e, "mm9", m0 + grp * per_grp_mm + gm + 16 + dc + 1)
                            sems.w(e, f"gd{'ab'[dc % 2]}",
                                   g0[dc % 2] + 16 * ((grp * 8 + dc) // 2))
                            bk = 4 if (dc % 2 == 0) else 6
                            for wi in range(2):
                                if is_sh:
                                    li = e.tensor_copy(eo[dc % 2][:, wi * 512:(wi + 1) * 512],
                                                       pW[bk + wi][:])
                                else:
                                    li = e.scalar_tensor_tensor(eo[dc % 2][:, wi * 512:(wi + 1) * 512],
                                                                pW[bk + wi][:], 0.0,
                                                                gate_sb[:, wi * 512:(wi + 1) * 512],
                                                                ALU.add, ALU.mult)
                            li.then_inc(sems.h["cv9"], 1)
                    sems.n["cv9"] = v0 + 4 * per_grp_cv

                @b.gpsimd
                def _(e):
                    for grp in range(4):
                        for dc in range(8):
                            sems.w(e, "cv9", v0 + grp * per_grp_cv + (0 if is_sh else 1) + dc + 1)
                            e.dma_start(out=S["acc_hbm"][dc][:, grp * 1024:(grp + 1) * 1024],
                                        in_=eo[dc % 2][:],
                                        accum_op=(ALU.bypass if is_sh else ALU.add)).then_inc(
                                            sems.h[f"gd{'ab'[dc % 2]}"], 16)
                    sems.n["gda"] = g0[0] + 16 * 16
                    sems.n["gdb"] = g0[1] + 16 * 16

            if qb is not None:
                quant_commit(qb)


def emit_tail(S):
    nc, sems = S["nc"], S["sems"]
    NTT = NT // 128
    RS = AF.Abs_reciprocal_sqrt
    with ExitStack() as pst:
        sb = lambda *a, **k: pst.enter_context(nc.sbuf_tensor(*a, **k))
        ps_ = lambda *a, **k: pst.enter_context(nc.psum_tensor(*a, **k))
        ac8 = [sb(f"ac8_{i}", [128, 1024], F32) for i in range(2)]
        r1b = [sb(f"r1b_{i}", [128, D], F32) for i in range(2)]
        ffn_t = sb("ffn_t", [128, D], F32)
        res2 = sb("res2", [128, D], F32)
        sq2 = sb("sq2", [128, D], F32)
        yt = [sb(f"yt{i}", [128, D], F32) for i in range(2)]
        st2 = sb("st2", [128, 8], F32)
        pT = [ps_(f"pT{i}", [128, 512], F32) for i in range(4)]
        a0p = [sems.n["sac0"], sems.n["sac1"]]
        y0p = [sems.n["sy0"], sems.n["sy1"]]
        m0, v0, a0 = sems.n["mm9"], sems.n["cv9"], sems.n["ca9"]
        gall = [sems.n["gda"], sems.n["gdb"]]
        with nc.Block() as b:
            @b.sync
            def _(e):
                sems.w(e, "gda", gall[0])
                sems.w(e, "gdb", gall[1])
                for tt in range(NTT):
                    if tt >= 2:
                        sems.w(e, "cv9", v0 + 4 * (tt - 1))
                    for dc in range(8):
                        e.dma_start(out=ac8[tt % 2][:, dc * 128:(dc + 1) * 128],
                                    in_=S["acc_hbm"][dc][:, tt * 128:(tt + 1) * 128]).then_inc(sems.h[f"sac{tt % 2}"], 16)
                    e.dma_start(out=r1b[tt % 2][:], in_=S["r1n_hbm"][tt]).then_inc(sems.h[f"sac{tt % 2}"], 16)
                sems.n["sac0"] = a0p[0] + 16 * 9 * (NTT // 2)
                sems.n["sac1"] = a0p[1] + 16 * 9 * (NTT // 2)

            @b.tensor
            def _(e):
                for tt in range(NTT):
                    sems.w(e, f"sac{tt % 2}", a0p[tt % 2] + 16 * 9 * (tt // 2 + 1))
                    if tt >= 1:
                        sems.w(e, "cv9", v0 + 4 * (tt - 1) + 1)
                    for dc in range(8):
                        li = e.transpose(pT[dc // 2][:, (dc % 2) * 128:(dc % 2) * 128 + 128],
                                    ac8[tt % 2][:, dc * 128:(dc + 1) * 128],
                                    S["id_f32"][:, :])
                    li.then_inc(sems.h["mm9"], 1)
                sems.n["mm9"] = m0 + NTT

            @b.vector
            def _(e):
                for tt in range(NTT):
                    sems.w(e, "mm9", m0 + tt + 1)
                    for dc in range(8):
                        li = e.tensor_copy(ffn_t[:, dc * 128:(dc + 1) * 128],
                                      pT[dc // 2][:, (dc % 2) * 128:(dc % 2) * 128 + 128])
                    li.then_inc(sems.h["cv9"], 1)
                    sems.w(e, f"sac{tt % 2}", a0p[tt % 2] + 16 * 9 * (tt // 2 + 1))
                    if tt >= 1:
                        sems.w(e, "ca9", a0 + 2 * (tt - 1) + 1)
                    e.drain()
                    e.tensor_add(res2[:], ffn_t[:], r1b[tt % 2][:])
                    e.drain()
                    e.reduce_sum(st2[:, 0:1], res2[:], axis=AX.X).then_inc(sems.h["cv9"], 1)
                    sems.w(e, "ca9", a0 + 2 * tt + 1)
                    e.drain()
                    e.tensor_scalar(st2[:, 2:3], st2[:, 0:1], 1.0 / D, None, ALU.mult)
                    e.tensor_scalar(st2[:, 3:4], st2[:, 1:2], 1.0 / D, None, ALU.mult)
                    e.drain()
                    e.tensor_mul(st2[:, 6:7], st2[:, 2:3], st2[:, 2:3])
                    e.drain()
                    e.tensor_sub(st2[:, 4:5], st2[:, 3:4], st2[:, 6:7])
                    e.drain()
                    e.tensor_scalar(st2[:, 4:5], st2[:, 4:5], EPS, None, ALU.add).then_inc(sems.h["cv9"], 1)
                    sems.w(e, "ca9", a0 + 2 * tt + 2)
                    if tt >= 2:
                        sems.w(e, f"sy{tt % 2}", y0p[tt % 2] + 16 * (tt // 2))
                    e.reciprocal(st2[:, 7:8], st2[:, 5:6])
                    e.drain()
                    e.tensor_scalar(st2[:, 6:7], st2[:, 2:3], st2[:, 7:8], -1.0,
                                    ALU.mult, ALU.mult)
                    e.drain()
                    e.tensor_scalar(yt[tt % 2][:], res2[:], st2[:, 7:8], st2[:, 6:7],
                                    ALU.mult, ALU.add).then_inc(sems.h["cv9"], 1)
                sems.n["cv9"] = v0 + 4 * NTT

            @b.scalar
            def _(e):
                for tt in range(NTT):
                    sems.w(e, "cv9", v0 + 4 * tt + 2)
                    e.activation(sq2[:], res2[:], AF.Square,
                                 accum_out=st2[:, 1:2]).then_inc(sems.h["ca9"], 1)
                    sems.w(e, "cv9", v0 + 4 * tt + 3)
                    e.activation(st2[:, 5:6], st2[:, 4:5], AF.Sqrt).then_inc(sems.h["ca9"], 1)
                    sems.w(e, "cv9", v0 + 4 * tt + 4)
                    e.dma_start(out=S["y"][tt], in_=yt[tt % 2][:]).then_inc(sems.h[f"sy{tt % 2}"], 16)
                sems.n["ca9"] = a0 + 2 * NTT
                sems.n["sy0"] = y0p[0] + 16 * (NTT // 2)
                sems.n["sy1"] = y0p[1] + 16 * (NTT // 2)

        with nc.Block() as b:
            @b.sync
            def _(e):
                sems.w(e, "sy0", sems.n["sy0"])
                sems.w(e, "sy1", sems.n["sy1"])


def emit_dummy_tail(S):
    nc, sems = S["nc"], S["sems"]
    with nc.Block() as b:
        @b.sync
        def _(e):
            sems.w(e, "sr1", sems.n["sr1"])
            for tt in range(NT // 128):
                e.dma_start(out=S["y"][tt], in_=S["r1n_hbm"][tt]).then_inc(sems.h["sy0"], 16)
            sems.n["sy0"] += 16 * (NT // 128)
            sems.w(e, "sy0", sems.n["sy0"])


def host_in_maps(inputs):
    x = np.asarray(inputs["L0_attn_x"])
    w_qkv = np.asarray(inputs["L0_attn_w_qkv"])
    w_conv = np.asarray(inputs["L0_attn_w_conv"])
    w_fc = np.asarray(inputs["L0_attn_w_fc"])
    b_fc = np.asarray(inputs["L0_attn_b_fc"])
    sw1 = np.asarray(inputs["L0_ffn_shared_w1"])
    sw2 = np.asarray(inputs["L0_ffn_shared_w2"])
    ew1 = np.asarray(inputs["L0_ffn_expert_w1"])
    ew2 = np.asarray(inputs["L0_ffn_expert_w2"])
    rw = np.asarray(inputs["L0_ffn_router_w"])
    lng = np.asarray(inputs["L0_ffn_ln_gamma"])
    lnb = np.asarray(inputs["L0_ffn_ln_beta"])

    idbf = np.eye(128, dtype=BF)
    idf32 = np.eye(128, dtype=np.float32)
    onesbf = np.ones((128, 128), dtype=BF)
    identc = np.zeros((128, 6 * M), dtype=np.float32)
    for ci, coef in enumerate((7.0, 15.0, 3.25)):
        for pc in range(2):
            blk = np.zeros((128, M), dtype=np.float32)
            blk[np.arange(128), pc * 128 + np.arange(128)] = coef
            identc[:, ci * 2 * M + pc * M:(ci * 2 + pc + 1) * M] = blk
    identc = identc.astype(BF)

    maps = []
    for c in range(8):
        b, j = c // 2, c % 2
        hoff = 8 * j
        poff = 8 * (1 - j)
        xT = np.ascontiguousarray(x[b].T).reshape(8, 128, N)
        x_own = np.ascontiguousarray(x[b, j * NT:(j + 1) * NT]).reshape(NT // 128, 128, D)
        wq = np.concatenate([w_qkv[:, hoff * DH:hoff * DH + 512],
                             w_qkv[:, D + hoff * DH:D + hoff * DH + 512],
                             w_qkv[:, 2 * D + hoff * DH:2 * D + hoff * DH + 512]],
                            axis=1)
        wq = np.ascontiguousarray(wq).reshape(8, 128, 3 * NH * DH)
        wf = np.concatenate([w_fc[hoff * DH:hoff * DH + 512],
                             w_fc[poff * DH:poff * DH + 512]], axis=0)
        wf = np.ascontiguousarray(wf).reshape(8, 128, D)
        convd = np.zeros((4, CK, 128, 128), dtype=np.float32)
        for p in range(4):
            for jj in range(CK):
                dv = np.concatenate([np.full(64, w_conv[hoff + 2 * p, 0, jj, 0], np.float32),
                                     np.full(64, w_conv[hoff + 2 * p + 1, 0, jj, 0], np.float32)])
                convd[p, jj, np.arange(128), np.arange(128)] = dv
        meta = np.array([[(1 - j) * NT, 1 - j, j * NT, 0, 0, 0, 0, 0]], dtype=np.int32)
        maps.append({
            "xT": xT, "x_own": x_own, "wqkv": wq, "wfc": wf,
            "bfc_rep": np.ascontiguousarray(np.broadcast_to(b_fc, (128, D))),
            "lng_rep": np.ascontiguousarray(np.broadcast_to(lng, (128, D))),
            "lnb_rep": np.ascontiguousarray(np.broadcast_to(lnb, (128, D))),
            "convd": convd, "identc": identc, "idbf": idbf, "idf32": idf32,
            "onesbf": onesbf, "onesf": np.ones((128, 128), dtype=np.float32), "meta": meta,
            "rwT": np.ascontiguousarray(rw.T).reshape(8, 128, NEXP),
            "sw1": sw1.reshape(FD // 128, 128, D),
            "sw2": sw2.reshape(D // 128, 128, FD),
            "ew1": ew1.reshape(NEXP, FD // 128, 128, D),
            "ew2": ew2.reshape(NEXP, D // 128, 128, FD),
        })
    return maps


def kernel(**inputs):
    from concourse.bass_utils import run_bass_kernel_spmd
    nc = build(n_pass=4, do_ffn=True)
    nc.finalize()
    maps = host_in_maps(inputs)
    res = run_bass_kernel_spmd(nc, maps, core_ids=list(range(8)))
    out = np.zeros((4, N, D), dtype=np.float32)
    for c in range(8):
        b, j = c // 2, c % 2
        out[b, j * NT:(j + 1) * NT] = res.results[c]["y"].reshape(NT, D)
    return out


# version 04
# speedup vs baseline: 1.5211x; 1.4038x over previous
import math
from contextlib import ExitStack

import numpy as np
import ml_dtypes

import concourse.bass as bass
import concourse.bacc as bacc
import concourse.mybir as mybir

F32 = mybir.dt.float32
BF16 = mybir.dt.bfloat16
I32 = mybir.dt.int32
AF = mybir.ActivationFunctionType
ALU = mybir.AluOpType
AX = mybir.AxisListType
DS = bass.ds

N, D, NH, DH, M = 8192, 1024, 8, 64, 256
MP_ITERS, CK = 6, 13
SCL = DH ** -0.5
NT = 4096
FD, NEXP, EPS = 2048, 8, 1e-5
BF = ml_dtypes.bfloat16


class Sems:
    def __init__(self, nc, st, names):
        self.h = {nm: st.enter_context(nc.semaphore(nm)) for nm in names}
        self.n = {nm: 0 for nm in names}

    def inc(self, instr, nm, k=1):
        instr.then_inc(self.h[nm], k)
        self.n[nm] += k

    def w(self, eng, nm, v):
        eng.wait_ge(self.h[nm], v)


SEM_NAMES = ["sx", "cc",
             "cvw", "cvx", "cvl", "cvq", "mmq",
             "mm1", "ca1", "cve", "sgn", "sgo",
             "mm2", "ca2", "mm3", "ca3", "cv3",
             "mm4", "cv4", "mm5", "cv5", "mm6", "cv6",
             "mm7", "cv7",
             "mm8", "cv8", "ca8", "sx8",
             "cv9", "mm9", "ca9", "sw9",
             "swq0", "swq1", "swl0", "swl1", "sxx0", "sxx1", "scv",
             "sxv", "sl3", "sst0", "sst1", "sci",
             "sfc0", "sfc1", "sr1", "shf",
             "rwl0", "rwl1", "scw0", "scw1", "sxw", "shh0", "shh1", "sgr",
             "gda", "gdb", "sac0", "sac1", "sy0", "sy1", "cvz", "caz",
             "sgi0", "sgi1", "sgh0", "sgh1", "sgg0", "sgg1",
             "sxh0", "sxh1", "sw10", "sw11", "sw20", "sw21", "sgx"]
IOA = bass.IndirectOffsetOnAxis
PADIDX = NT + 4


def build(n_pass=4, do_ffn=True, dumps=(), dumpreq=()):
    nc = bacc.Bacc("TRN2", target_bir_lowering=False)
    st = ExitStack()

    S = {"nc": nc, "st": st}
    S["xT"] = nc.dram_tensor("xT", [8, 128, N], F32, kind="ExternalInput").ap()
    S["x_own"] = nc.dram_tensor("x_own", [NT // 128, 128, D], F32, kind="ExternalInput").ap()
    S["wqkv"] = nc.dram_tensor("wqkv", [8, 128, 3 * NH * DH], F32, kind="ExternalInput").ap()
    S["wfc"] = nc.dram_tensor("wfc", [8, 128, D], F32, kind="ExternalInput").ap()
    S["bfc_rep"] = nc.dram_tensor("bfc_rep", [128, D], F32, kind="ExternalInput").ap()
    S["lng_rep"] = nc.dram_tensor("lng_rep", [128, D], F32, kind="ExternalInput").ap()
    S["lnb_rep"] = nc.dram_tensor("lnb_rep", [128, D], F32, kind="ExternalInput").ap()
    S["convd"] = nc.dram_tensor("convd", [4, CK, 128, 128], F32, kind="ExternalInput").ap()
    S["identc"] = nc.dram_tensor("identc", [128, 6 * M], BF16, kind="ExternalInput").ap()
    S["idbf"] = nc.dram_tensor("idbf", [128, 128], BF16, kind="ExternalInput").ap()
    S["idf32"] = nc.dram_tensor("idf32", [128, 128], F32, kind="ExternalInput").ap()
    S["onesbf"] = nc.dram_tensor("onesbf", [128, 128], BF16, kind="ExternalInput").ap()
    S["onesf"] = nc.dram_tensor("onesf", [128, 128], F32, kind="ExternalInput").ap()
    S["meta"] = nc.dram_tensor("meta", [1, 8], I32, kind="ExternalInput").ap()
    S["rwT"] = nc.dram_tensor("rwT", [8, 128, NEXP], F32, kind="ExternalInput").ap()
    S["sw1"] = nc.dram_tensor("sw1", [FD // 128, 128, D], F32, kind="ExternalInput").ap()
    S["sw2"] = nc.dram_tensor("sw2", [D // 128, 128, FD], F32, kind="ExternalInput").ap()
    S["ew1"] = nc.dram_tensor("ew1", [NEXP, FD // 128, 128, D], F32, kind="ExternalInput").ap()
    S["ew2"] = nc.dram_tensor("ew2", [NEXP, D // 128, 128, FD], F32, kind="ExternalInput").ap()
    S["y"] = nc.dram_tensor("y", [NT // 128, 128, D], F32, kind="ExternalOutput").ap()

    S["cum128"] = nc.dram_tensor("cum128", [128, 128], F32, kind="ExternalInput").ap()
    S["cum32s"] = nc.dram_tensor("cum32s", [32, 32], F32, kind="ExternalInput").ap()
    S["cum8s"] = nc.dram_tensor("cum8s", [8, 8], F32, kind="ExternalInput").ap()
    S["k512"] = nc.dram_tensor("k512", [8, 8], F32, kind="ExternalInput").ap()
    S["iota16"] = nc.dram_tensor("iota16", [8, 16], F32, kind="ExternalInput").ap()
    S["iota32"] = nc.dram_tensor("iota32", [128, 32], I32, kind="ExternalInput").ap()
    S["gpad"] = nc.dram_tensor("gpad", [16 * 512, 1], I32, kind="ExternalInput").ap()
    S["zb16"] = nc.dram_tensor("zb16", [128, D], BF16, kind="ExternalInput").ap()
    S["zf32"] = nc.dram_tensor("zf32", [128, 1], F32, kind="ExternalInput").ap()

    S["outall"] = nc.dram_tensor("outall", [4, 128, N], BF16, kind="Internal").ap()
    S["cin"] = nc.dram_tensor("cin", [4, 128, NT], BF16, kind="Internal").ap()
    S["cout"] = nc.dram_tensor("cout", [2, 4, 128, NT], BF16, kind="Internal").ap()
    S["gin"] = nc.dram_tensor("gin", [1, 2], F32, kind="Internal").ap()
    S["gout"] = nc.dram_tensor("gout", [1, 2], F32, kind="Internal").ap()
    S["r1n_hbm"] = nc.dram_tensor("r1n_hbm", [NT // 128, 128, D], F32, kind="Internal").ap()
    S["w1scr"] = nc.dram_tensor("w1scr", [9, FD // 128, 128, D], BF16, kind="Internal").ap()
    S["w2scr"] = nc.dram_tensor("w2scr", [9, D // 128, 128, FD], BF16, kind="Internal").ap()
    S["h_tm"] = nc.dram_tensor("h_tm", [NT + 256, D], BF16, kind="Internal").ap()
    S["gsel_hbm"] = nc.dram_tensor("gsel_hbm", [NT + 256, 1], F32, kind="Internal").ap()
    S["gidx_hbm"] = nc.dram_tensor("gidx_hbm", [16 * 512, 1], I32, kind="Internal").ap()
    S["acc_tm"] = nc.dram_tensor("acc_tm", [NT + 256, D], F32, kind="Internal").ap()

    S["dump"] = {}
    S["dumpreq"] = dumpreq

    sb = lambda *a, **k: st.enter_context(nc.sbuf_tensor(*a, **k))
    S["mt"] = sb("mt", [1, 8], I32)
    S["id_bf"] = sb("id_bf", [128, 128], BF16)
    S["id_f32"] = sb("id_f32", [128, 128], F32)
    S["ones_bf"] = sb("ones_bf", [128, 128], BF16)
    S["ones_f32"] = sb("ones_f32", [128, 128], F32)
    S["identc_sb"] = sb("identc_sb", [128, 6 * M], BF16)
    ast = ExitStack()
    S["ql_all"] = ast.enter_context(nc.sbuf_tensor("ql_all", [128, NH * M], BF16, side="right"))
    S["kl_all"] = ast.enter_context(nc.sbuf_tensor("kl_all", [128, NH * M], BF16, side="right"))
    S["a2T_all"] = ast.enter_context(nc.sbuf_tensor("a2T_all", [128, NH * 2 * M], BF16, side="right"))
    S["junk"] = sb("junk", [1, 8], BF16)
    S["statv"] = sb("statv", [128, 8], F32)
    S["gsc"] = sb("gsc", [1, 8], F32)
    S["rden_rep"] = sb("rden_rep", [128, 1], F32)
    S["cum128_sb"] = sb("cum128_sb", [128, 128], F32)
    S["cum32s_sb"] = sb("cum32s_sb", [32, 32], F32)
    S["cum8s_sb"] = sb("cum8s_sb", [8, 8], F32)
    S["k512_sb"] = sb("k512_sb", [8, 8], F32)
    S["iota16_sb"] = sb("iota16_sb", [8, 16], F32)
    S["iot_sb"] = sb("iot_sb", [128, 32], I32)
    S["wofs"] = sb("wofs", [1, 16], I32)

    sems = Sems(nc, st, SEM_NAMES)
    S["sems"] = sems

    with nc.Block() as b:
        @b.sync
        def _(e):
            for dst, src in ((S["mt"], S["meta"]), (S["id_bf"], S["idbf"]),
                             (S["id_f32"], S["idf32"]), (S["ones_bf"], S["onesbf"]),
                             (S["ones_f32"], S["onesf"]),
                             (S["identc_sb"], S["identc"]),
                             (S["cum128_sb"], S["cum128"]), (S["cum32s_sb"], S["cum32s"]),
                             (S["cum8s_sb"], S["cum8s"]), (S["k512_sb"], S["k512"]),
                             (S["iota16_sb"], S["iota16"]), (S["iot_sb"], S["iota32"])):
                e.dma_start(out=dst[:], in_=src[:, :]).then_inc(sems.h["sx"], 16)
            sems.n["sx"] += 16 * 12
            e.dma_start(out=S["gidx_hbm"][:, :], in_=S["gpad"][:, :]).then_inc(sems.h["sx"], 16)
            for half in range(2):
                e.dma_start(out=S["h_tm"][NT + half * 128:NT + (half + 1) * 128, :],
                            in_=S["zb16"][:, :]).then_inc(sems.h["sx"], 16)
                e.dma_start(out=S["gsel_hbm"][NT + half * 128:NT + (half + 1) * 128, :],
                            in_=S["zf32"][:, :]).then_inc(sems.h["sx"], 16)
            sems.n["sx"] += 16 * 5

        @b.vector
        def _(e):
            sems.w(e, "sx", sems.n["sx"])
            e.memset(S["junk"][0:1, 0:4], 1.0)
            e.memset(S["junk"][0:1, 4:8], 0.0)
            e.memset(S["statv"][:], 0.0)
            e.memset(S["gsc"][0:1, 0:4], 1.0)
            e.memset(S["gsc"][0:1, 4:8], 0.0)

    S["v_oth"] = nc.values_load(S["mt"][0:1, 0:1], engines=[mybir.EngineType.SP, mybir.EngineType.Activation])
    S["v_slot"] = nc.values_load(S["mt"][0:1, 1:2], engines=[mybir.EngineType.SP])
    S["v_own"] = nc.values_load(S["mt"][0:1, 2:3], engines=[mybir.EngineType.SP])
    S["sx_total"] = sems.n["sx"]

    if do_ffn:
        assert n_pass == 4
        qsched = {0: (8, 0), 1: (1, 2), 2: (3, 4), 3: (5, 6)}
    else:
        qsched = {}
    for p in range(n_pass):
        emit_pass(S, p, qslots=qsched.get(p, ()))
    ast.close()
    emit_exchange(S, qslots=((7,) if do_ffn else ()))
    emit_fc_ln(S, do_idx=do_ffn)
    if do_ffn:
        S["v_we"] = [nc.values_load(S["wofs"][0:1, w_:w_ + 1], engines=[mybir.EngineType.SP])
                     for w_ in range(16)]
        emit_windows(S)
        emit_tail(S)
    else:
        emit_dummy_tail(S)

    if dumps:
        dt_ = {}
        for key in dumps:
            t = S[key].tensor
            dt_[key] = nc.dram_tensor("d_" + key, list(t.shape), t.dtype,
                                      kind="ExternalOutput").ap()
        with nc.Block() as b:
            @b.scalar
            def _(e):
                for key in dumps:
                    e.dma_start(out=dt_[key][...], in_=S[key][...]).then_inc(sems.h["scv"], 16)
                    sems.n["scv"] += 16
                sems.w(e, "scv", sems.n["scv"])

    st.close()
    return nc


def emit_pass(S, p, qslots=()):
    nc, sems = S["nc"], S["sems"]
    with ExitStack() as pst:
        sb = lambda *a, **k: pst.enter_context(nc.sbuf_tensor(*a, **k))
        ps_ = lambda *a, **k: pst.enter_context(nc.psum_tensor(*a, **k))
        qkv = [sb(f"qkv{p}_{i}", [128, N], BF16) for i in range(3)]
        cvb = sb(f"cvb{p}", [128, CK * 128], BF16)
        outp = sb(f"outp{p}", [128, N + 2 * (CK // 2)], BF16)
        ostg = [sb(f"ostg{p}_{i}", [128, 2048], BF16) for i in range(2)]
        pA = [ps_(f"pA{p}_{i}", [128, 512], F32) for i in range(8)]
        if p == 0:
            xlb = [sb(f"xlb_{i}", [128, M], BF16) for i in range(8)]
        inner = ExitStack()
        sbi = lambda *a, **k: inner.enter_context(nc.sbuf_tensor(*a, **k))
        xw = [sbi(f"xw{p}_{i}", [128, 2048], F32) for i in range(2)]
        xbg = [sbi(f"xbg{p}_{i}", [128, 2048], BF16) for i in range(8)]
        wqf = [sbi(f"wqf{p}_{i}", [128, 384], F32) for i in range(2)]
        wqb = [sbi(f"wqb{p}_{i}", [128, 384], BF16) for i in range(8)]
        cvf = sbi(f"cvf{p}", [128, CK * 128], F32)
        if p == 0:
            xl = [sbi(f"xl_{i}", [128, M], F32) for i in range(8)]
            wlf = [sbi(f"wlf_{i}", [128, 1024], F32) for i in range(2)]
            wlb = [sbi(f"wlb_{i}", [128, 1024], BF16) for i in range(8)]

        qc, kc, vc = 2 * p * DH, NH * DH + 2 * p * DH, 2 * NH * DH + 2 * p * DH
        cw0, cx0, cl0 = sems.n["cvw"], sems.n["cvx"], sems.n["cvl"]
        mq0, cq0 = sems.n["mmq"], sems.n["cvq"]
        q0 = [sems.n["swq0"], sems.n["swq1"]]
        x0 = [sems.n["sxx0"], sems.n["sxx1"]]
        v0s = sems.n["scv"]
        nwl = 8 if p == 0 else 0
        nwq_cast = 8 + nwl + 1

        with nc.Block() as b:
            @b.sync
            def _(e):
                for dc in range(8):
                    if dc >= 2:
                        sems.w(e, "cvw", cw0 + dc - 1)
                    for ci, c0 in enumerate((qc, kc, vc)):
                        e.dma_start(out=wqf[dc % 2][:, ci * 128:(ci + 1) * 128],
                                    in_=S["wqkv"][dc][:, c0:c0 + 128]).then_inc(sems.h[f"swq{dc % 2}"], 16)
                if p == 0:
                    for dc in range(8):
                        if dc >= 2:
                            sems.w(e, "cvw", cw0 + 8 + dc - 1)
                        e.dma_start(out=wlf[dc % 2][:],
                                    in_=S["wqkv"][dc][:, 0:1024]).then_inc(sems.h[f"swl{dc % 2}"], 16)
                for jj in range(CK):
                    e.dma_start(out=cvf[:, jj * 128:(jj + 1) * 128],
                                in_=S["convd"][p, jj]).then_inc(sems.h["scv"], 16)
                for g in range(4):
                    for dc in range(8):
                        i = g * 8 + dc
                        if i >= 2:
                            sems.w(e, "cvx", cx0 + i - 1)
                            if p == 0:
                                sems.w(e, "cvl", cl0 + i - 1)
                        e.dma_start(out=xw[i % 2][:],
                                    in_=S["xT"][dc][:, g * 2048:(g + 1) * 2048]).then_inc(sems.h[f"sxx{i % 2}"], 16)
                sems.n["swq0"] = q0[0] + 16 * 12
                sems.n["swq1"] = q0[1] + 16 * 12
                if p == 0:
                    sems.n["swl0"] += 16 * 4
                    sems.n["swl1"] += 16 * 4
                sems.n["scv"] = v0s + 16 * CK
                sems.n["sxx0"] = x0[0] + 16 * 16
                sems.n["sxx1"] = x0[1] + 16 * 16

            @b.vector
            def _(e):
                for dc in range(8):
                    sems.w(e, f"swq{dc % 2}", q0[dc % 2] + 16 * 3 * (dc // 2 + 1))
                    e.tensor_copy(wqb[dc][:], wqf[dc % 2][:]).then_inc(sems.h["cvw"], 1)
                if p == 0:
                    for dc in range(8):
                        sems.w(e, f"swl{dc % 2}", sems.n[f"swl{dc % 2}"] - 16 * 4 + 16 * (dc // 2 + 1))
                        e.tensor_copy(wlb[dc][:], wlf[dc % 2][:]).then_inc(sems.h["cvw"], 1)
                sems.w(e, "scv", v0s + 16 * CK)
                e.tensor_copy(cvb[:], cvf[:]).then_inc(sems.h["cvw"], 1)
                sems.n["cvw"] = cw0 + nwq_cast
                for g in range(4):
                    for dc in range(8):
                        i = g * 8 + dc
                        sems.w(e, f"sxx{i % 2}", x0[i % 2] + 16 * (i // 2 + 1))
                        if g > 0:
                            sems.w(e, "mmq", mq0 + 3 * g)
                        e.tensor_copy(xbg[dc][:], xw[i % 2][:]).then_inc(sems.h["cvx"], 1)
                        if p == 0:
                            e.reduce_sum(xl[dc][:, g * 64:(g + 1) * 64],
                                         xw[i % 2][:].rearrange("q (a w) -> q a w", w=32),
                                         axis=AX.X).then_inc(sems.h["cvl"], 1)
                    for cc in range(3):
                        k = g * 3 + cc
                        sems.w(e, "mmq", mq0 + k + 1)
                        for wi in range(4):
                            bank = (k % 2) * 4 + wi
                            li = e.tensor_copy(qkv[cc][:, g * 2048 + wi * 512:g * 2048 + (wi + 1) * 512],
                                               pA[bank][:])
                        li.then_inc(sems.h["cvq"], 1)
                sems.n["cvx"] = cx0 + 32
                sems.n["cvq"] = cq0 + 12
                if p == 0:
                    sems.n["cvl"] = cl0 + 32

            @b.tensor
            def _(e):
                sems.w(e, "cvw", cw0 + 8)
                for g in range(4):
                    sems.w(e, "cvx", cx0 + (g + 1) * 8)
                    for cc in range(3):
                        k = g * 3 + cc
                        if k >= 2:
                            sems.w(e, "cvq", cq0 + k - 1)
                        for dc in range(8):
                            e.ldweights(wqb[dc][:, cc * 128:(cc + 1) * 128])
                            for wi in range(4):
                                bank = (k % 2) * 4 + wi
                                li = e.matmul(pA[bank][:], wqb[dc][:, cc * 128:(cc + 1) * 128],
                                              xbg[dc][:, wi * 512:(wi + 1) * 512],
                                              start=(dc == 0), stop=(dc == 7))
                        li.then_inc(sems.h["mmq"], 1)
                sems.n["mmq"] = mq0 + 12

        if p == 0:
            for ci in range(3):
                dump_sb(S, f"qkv{p}_{ci}", qkv[ci][:])
            emit_attn2prep(S, xl, xlb, wlb, pA)
            dump_sb(S, "ql", S["ql_all"][:])
            dump_sb(S, "kl", S["kl_all"][:])
            dump_sb(S, "a2T", S["a2T_all"][:])
        inner.close()

        for hl in range(2):
            emit_head(S, p, hl, qkv, outp, pA)
        dump_sb(S, f"outp{p}", outp[:])
        emit_conv_store(S, p, outp, cvb, ostg, pA, qslots=qslots)


def emit_attn2prep(S, xl, xlb, wlb, pA):
    nc, sems = S["nc"], S["sems"]
    ql, kl, a2T = S["ql_all"], S["kl_all"], S["a2T_all"]
    ones = S["ones_bf"]
    with ExitStack() as pst:
        sb = lambda *a, **k: pst.enter_context(nc.sbuf_tensor(*a, **k))
        e2b = sb("e2b", [128, 2 * M], BF16)
        rl2 = sb("rl2", [128, M], F32)
        rsum = sb("rsum", [128, 1], F32)
        m0, a0, v0, x0 = sems.n["mm1"], sems.n["ca1"], sems.n["cve"], sems.n["cvx"]

        with nc.Block() as b:
            @b.vector
            def _(e):
                e.memset(ql[:], 0.0)
                e.memset(kl[:], 0.0)
                for dc in range(8):
                    e.tensor_scalar_mul(xlb[dc][:], xl[dc][:], 1.0 / 32.0).then_inc(sems.h["cvx"], 1)
                for h in range(NH):
                    ob = (h % 2) * 64
                    sems.w(e, "mm1", m0 + 6 * h + 2)
                    e.tensor_copy(ql[ob:ob + 64, h * M:(h + 1) * M], pA[0][ob:ob + 64, 0:M])
                    e.tensor_copy(kl[ob:ob + 64, h * M:(h + 1) * M],
                                  pA[1][ob:ob + 64, 0:M]).then_inc(sems.h["cve"], 1)
                    sems.w(e, "mm1", m0 + 6 * h + 5)
                    e.reciprocal(rl2[:], pA[4][:, 0:M])
                    e.drain()
                    e.tensor_mul(a2T[:, h * 2 * M:h * 2 * M + M], e2b[:, 0:M], rl2[:])
                    e.tensor_mul(a2T[:, h * 2 * M + M:(h + 1) * 2 * M], e2b[:, M:2 * M], rl2[:]).then_inc(sems.h["cve"], 1)
                    e.drain()
                    e.reduce_sum(rsum[:], a2T[:, h * 2 * M:h * 2 * M + M], axis=AX.X)
                    e.reduce_sum(S["statv"][:, 1:2], a2T[:, h * 2 * M + M:(h + 1) * 2 * M], axis=AX.X)
                    e.drain()
                    e.tensor_max(rsum[:], rsum[:], S["statv"][:, 1:2])
                    e.drain()
                    if h == 0:
                        e.tensor_copy(S["statv"][:, 0:1], rsum[:])
                    else:
                        e.tensor_max(S["statv"][:, 0:1], S["statv"][:, 0:1], rsum[:])
                    sems.w(e, "mm1", m0 + 6 * h + 6)
                    e.reduce_max(S["gsc"][0:1, 0:1] if h == 0 else S["gsc"][0:1, 2:3],
                                 pA[5][0:1, 0:M], axis=AX.X)
                    e.drain()
                    if h > 0:
                        li = e.tensor_max(S["gsc"][0:1, 0:1], S["gsc"][0:1, 0:1], S["gsc"][0:1, 2:3])
                    else:
                        li = e.engine_nop()
                    li.then_inc(sems.h["cve"], 1)
                sems.w(e, "mm1", m0 + 6 * NH + 1)
                e.reduce_max(S["gsc"][0:1, 1:2], pA[6][0:1, 0:128], axis=AX.X).then_inc(sems.h["cve"], 1)
                sems.n["cvx"] = x0 + 8
                sems.n["cve"] = v0 + 3 * NH + 1

            @b.tensor
            def _(e):
                sems.w(e, "cvx", x0 + 8)
                for h in range(NH):
                    ob = (h % 2) * 64
                    if h > 0:
                        sems.w(e, "cve", v0 + 3 * (h - 1) + 3)
                    for side, dst in ((0, 0), (1, 1)):
                        for dc in range(8):
                            wslice = wlb[dc][:, side * 512 + h * DH:side * 512 + (h + 1) * DH]
                            e.ldweights(wslice)
                            li = e.matmul(pA[dst][ob:ob + 64, 0:M], wslice, xlb[dc][:],
                                          start=(dc == 0), stop=(dc == 7))
                        li.then_inc(sems.h["mm1"], 1)
                    sems.w(e, "cve", v0 + 3 * h + 1)
                    for mc in range(2):
                        lw = kl[:, h * M + mc * 128:h * M + (mc + 1) * 128]
                        e.ldweights(lw)
                        e.matmul(pA[2 + mc][:, 0:M], lw, ql[:, h * M:(h + 1) * M],
                                 start=True, stop=True).then_inc(sems.h["mm1"], 1)
                    sems.w(e, "ca1", a0 + 2 * (h + 1))
                    e.ldweights(ones[:, :])
                    e.matmul(pA[4][:, 0:M], ones[:, :], e2b[:, 0:M], start=True, stop=False)
                    e.matmul(pA[4][:, 0:M], ones[:, :], e2b[:, M:2 * M],
                             start=False, stop=True).then_inc(sems.h["mm1"], 1)
                    sems.w(e, "cve", v0 + 3 * h + 2)
                    e.ldweights(ones[:, 0:64])
                    e.matmul(pA[5][0:64, 0:M], ones[:, 0:64], a2T[:, h * 2 * M:h * 2 * M + M],
                             start=True, stop=False)
                    e.matmul(pA[5][0:64, 0:M], ones[:, 0:64],
                             a2T[:, h * 2 * M + M:(h + 1) * 2 * M],
                             start=False, stop=True).then_inc(sems.h["mm1"], 1)
                sems.w(e, "cve", v0 + 3 * NH)
                e.transpose(pA[6][0:1, 0:128], S["statv"][:, 0:1],
                            S["id_f32"][:, :]).then_inc(sems.h["mm1"], 1)
                sems.n["mm1"] = m0 + 6 * NH + 1

            @b.scalar
            def _(e):
                for h in range(NH):
                    for mc in range(2):
                        sems.w(e, "mm1", m0 + 6 * h + 3 + mc)
                        if h > 0 and mc == 0:
                            sems.w(e, "cve", v0 + 3 * (h - 1) + 2)
                        e.activation(e2b[:, mc * M:(mc + 1) * M], pA[2 + mc][:, 0:M],
                                     AF.Exp, scale=SCL).then_inc(sems.h["ca1"], 1)
                sems.w(e, "cve", v0 + 3 * NH + 1)
                e.dma_start(out=S["gin"][:, :], in_=S["gsc"][0:1, 0:2]).then_inc(sems.h["sgn"], 16)
                sems.n["ca1"] = a0 + 2 * NH
                sems.n["sgn"] += 16

        with nc.Block() as b:
            @b.gpsimd
            def _(e):
                sems.w(e, "sgn", sems.n["sgn"])
                e.collective_compute("AllReduce", ALU.max,
                                     ins=[S["gin"][:, :]], outs=[S["gout"][:, :]],
                                     replica_groups=[[0, 1, 2, 3, 4, 5, 6, 7]]).then_inc(sems.h["cc"], 1)
                sems.n["cc"] += 1
                sems.w(e, "cc", sems.n["cc"])
                e.dma_start(out=S["gsc"][0:1, 4:6], in_=S["gout"][:, :]).then_inc(sems.h["sgo"], 16)
                sems.n["sgo"] += 16

        with nc.Block() as b:
            m1 = sems.n["mm1"]

            @b.vector
            def _(e):
                sems.w(e, "sgo", sems.n["sgo"])
                e.tensor_mul(S["gsc"][0:1, 6:7], S["gsc"][0:1, 4:5], S["gsc"][0:1, 5:6])
                e.drain()
                e.reciprocal(S["gsc"][0:1, 7:8], S["gsc"][0:1, 6:7]).then_inc(sems.h["cve"], 1)
                sems.w(e, "mm1", m1 + 1)
                e.tensor_copy(S["rden_rep"][:], pA[7][:, 0:1])
                sems.n["cve"] += 1

            @b.tensor
            def _(e):
                sems.w(e, "cve", sems.n["cve"])
                e.matmul(pA[7][:, 0:1], S["ones_f32"][0:1, :], S["gsc"][0:1, 7:8],
                         start=True, stop=True).then_inc(sems.h["mm1"], 1)
                sems.n["mm1"] = m1 + 1


def emit_head(S, p, hl, qkv, outp, pA):
    nc, sems = S["nc"], S["sems"]
    h = 2 * p + hl
    r0, r1 = hl * 64, hl * 64 + 64
    ql, kl, a2T = S["ql_all"], S["kl_all"], S["a2T_all"]
    ones, idc = S["ones_bf"], S["identc_sb"]

    with ExitStack() as hst:
        sb = lambda *a, **k: hst.enter_context(nc.sbuf_tensor(*a, **k))
        vaug = sb(f"vaug{h}", [128, 128 * 64], BF16, is_transpose_dest=True)
        exp1 = [sb(f"exp1_{h}_{i}", [128, N], BF16) for i in range(2)]
        e3b = [sb(f"e3b{h}_{i}", [128, M], BF16) for i in range(2)]
        t1f = sb(f"t1f{h}", [65, M], F32)
        t1id = [sb(f"t1id{h}_{i}", [128, 65], BF16) for i in range(2)]
        l3r = sb(f"l3r{h}", [1, M], F32)
        rl3r = sb(f"rl3r{h}", [1, M], F32)
        rl3pp = sb(f"rl3pp{h}", [128, 2], F32)
        a2f = sb(f"a2f{h}", [128, 512], F32)
        a2b = sb(f"a2b{h}", [128, 512], BF16)
        zb = sb(f"zb{h}", [128, 512], BF16)
        ztb = sb(f"ztb{h}", [128, 512], BF16)
        aztb = sb(f"aztb{h}", [128, 512], BF16)
        mpb = sb(f"mpb{h}", [128, 512], BF16)
        zstb = sb(f"zstb{h}", [128, 512], BF16)
        t2b = [sb(f"t2b{h}_{i}", [128, 64], BF16) for i in range(2)]
        rl1 = sb(f"rl1{h}", [128, 512], F32)

        a2s = a2T[:, h * 2 * M:(h + 1) * 2 * M]

        sx0, m20, a20 = sems.n["sxv"], sems.n["mm2"], sems.n["ca2"]
        with nc.Block() as b:
            @b.vector
            def _(e):
                if hl == 0:
                    e.memset(outp[:, 0:CK // 2], 0.0)
                    e.memset(outp[:, CK // 2 + N:CK // 2 + N + CK // 2], 0.0)
                e.memset(vaug[:], 1.0).then_inc(sems.h["cv3"], 1)

            @b.sync
            def _(e):
                sems.w(e, "cv3", sems.n["cv3"] + 1)
                for c in range(64):
                    e.dma_start_transpose(out=vaug[:, c * 128:c * 128 + 64],
                                          in_=qkv[2][r0:r1, c * 128:(c + 1) * 128]).then_inc(sems.h["sxv"], 16)
                sems.n["sxv"] = sx0 + 64 * 16
            sems.n["cv3"] += 1

            @b.tensor
            def _(e):
                for wg in range(4):
                    for mc in range(2):
                        j = 2 * wg + mc
                        if j >= 2:
                            sems.w(e, "ca2", a20 + j - 1)
                        lw = kl[:, h * M + mc * 128:h * M + (mc + 1) * 128]
                        e.ldweights(lw)
                        for wi in range(4):
                            w = wg * 4 + wi
                            li = e.matmul(pA[mc * 4 + wi][:], lw,
                                          qkv[0][:, w * 512:(w + 1) * 512],
                                          start=True, stop=True)
                        li.then_inc(sems.h["mm2"], 1)
                sems.n["mm2"] = m20 + 8

            @b.scalar
            def _(e):
                for wg in range(4):
                    for mc in range(2):
                        j = 2 * wg + mc
                        sems.w(e, "mm2", m20 + j + 1)
                        for wi in range(4):
                            w = wg * 4 + wi
                            li = e.activation(exp1[mc][:, w * 512:(w + 1) * 512],
                                              pA[mc * 4 + wi][:], AF.Exp, scale=SCL)
                        li.then_inc(sems.h["ca2"], 1)
                sems.n["ca2"] = a20 + 8

        dump_sb(S, f"vaug{h}", vaug[:])
        dump_sb(S, f"exp1a{h}", exp1[0][:])
        m30, a30 = sems.n["mm3"], sems.n["ca3"]
        with nc.Block() as b:
            @b.tensor
            def _(e):
                sems.w(e, "sxv", sems.n["sxv"])
                for c in range(64):
                    if c >= 2:
                        sems.w(e, "ca3", a30 + c - 1)
                    lw = qkv[1][:, c * 128:(c + 1) * 128]
                    e.ldweights(lw)
                    e.matmul(pA[c % 2][:, 0:M], lw, ql[:, h * M:(h + 1) * M],
                             start=True, stop=True).then_inc(sems.h["mm3"], 1)
                    sems.w(e, "ca3", a30 + c + 1)
                    e.ldweights(vaug[:, c * 128:c * 128 + 65])
                    e.matmul(pA[2][0:65, 0:M], vaug[:, c * 128:c * 128 + 65], e3b[c % 2][:],
                             start=(c == 0), stop=(c == 63)).then_inc(sems.h["mm3"], 1)
                sems.n["mm3"] = m30 + 128

            @b.scalar
            def _(e):
                for c in range(64):
                    sems.w(e, "mm3", m30 + 2 * c + 1)
                    e.activation(e3b[c % 2][:], pA[c % 2][:, 0:M],
                                 AF.Exp, scale=SCL).then_inc(sems.h["ca3"], 1)
                sems.n["ca3"] = a30 + 64

        m40, v40, w40 = sems.n["mm4"], sems.n["cv4"], sems.n["sl3"]
        with nc.Block() as b:
            @b.vector
            def _(e):
                e.tensor_copy(t1f[:], pA[2][0:65, 0:M]).then_inc(sems.h["cv4"], 1)
                e.tensor_copy(a2f[:], a2s).then_inc(sems.h["cv4"], 1)
                sems.w(e, "sl3", w40 + 16)
                e.reciprocal(rl3r[:], l3r[:]).then_inc(sems.h["cv4"], 1)
                sems.w(e, "mm4", m40 + 2)
                e.tensor_copy(rl3pp[:, 0:1], pA[3][:, 0:1])
                e.tensor_copy(rl3pp[:, 1:2], pA[3][:, 1:2]).then_inc(sems.h["cv4"], 1)
                sems.w(e, "mm4", m40 + 4)
                e.tensor_copy(t1id[0][:], pA[4][:, 0:65])
                e.tensor_copy(t1id[1][:], pA[4][:, 65:130]).then_inc(sems.h["cv4"], 1)
                sems.w(e, "mm4", m40 + 8)
                for pc in range(2):
                    for rc in range(2):
                        li = e.tensor_copy(a2b[:, pc * 256 + rc * 128:pc * 256 + (rc + 1) * 128],
                                           pA[5 + pc][:, rc * 128:(rc + 1) * 128])
                li.then_inc(sems.h["cv4"], 1)
                sems.n["cv4"] = v40 + 6

            @b.scalar
            def _(e):
                sems.w(e, "cv4", v40 + 1)
                e.dma_start(out=l3r[0:1, :], in_=t1f[64:65, :]).then_inc(sems.h["sl3"], 16)
                sems.n["sl3"] = w40 + 16

            @b.tensor
            def _(e):
                sems.w(e, "cv4", v40 + 3)
                e.transpose(pA[3][:, 0:1], rl3r[0:1, 0:128], S["id_f32"][0:1, 0:1]).then_inc(sems.h["mm4"], 1)
                e.transpose(pA[3][:, 1:2], rl3r[0:1, 128:256], S["id_f32"][0:1, 0:1]).then_inc(sems.h["mm4"], 1)
                e.transpose(pA[4][:, 0:65], t1f[0:65, 0:128], S["id_f32"][0:65, 0:65]).then_inc(sems.h["mm4"], 1)
                e.transpose(pA[4][:, 65:130], t1f[0:65, 128:256], S["id_f32"][0:65, 0:65]).then_inc(sems.h["mm4"], 1)
                sems.w(e, "cv4", v40 + 2)
                for pc in range(2):
                    for rc in range(2):
                        e.transpose(pA[5 + pc][:, rc * 128:(rc + 1) * 128],
                                    a2f[:, rc * 256 + pc * 128:rc * 256 + (pc + 1) * 128],
                                    S["id_f32"][:, :]).then_inc(sems.h["mm4"], 1)
                sems.n["mm4"] = m40 + 8

        dump_sb(S, f"t1f{h}", t1f[:])
        dump_sb(S, f"rl3pp{h}", rl3pp[:])
        dump_sb(S, f"a2b{h}", a2b[:])
        m50, v50 = sems.n["mm5"], sems.n["cv5"]
        with nc.Block() as b:
            @b.vector
            def _(e):
                e.tensor_scalar_mul(zb[:], a2s, S["rden_rep"][:, 0:1])
                e.tensor_scalar_mul(ztb[:], a2b[:], S["rden_rep"][:, 0:1]).then_inc(sems.h["cv5"], 1)
                for i in range(MP_ITERS):
                    last = i == MP_ITERS - 1
                    base = m50 + 12 * i
                    vb = v50 + 1 + 6 * i
                    sems.w(e, "mm5", base + 2)
                    for pc in range(2):
                        li = e.scalar_tensor_tensor(mpb[:, pc * 256:(pc + 1) * 256],
                                               pA[pc][:, 0:M], -1.0,
                                               idc[:, 0 * 512 + pc * 256:0 * 512 + (pc + 1) * 256],
                                               ALU.mult, ALU.add)
                    li.then_inc(sems.h["cv5"], 1)
                    sems.w(e, "mm5", base + 4)
                    for pc in range(2):
                        li = e.tensor_copy(aztb[:, pc * 256:(pc + 1) * 256], pA[2 + pc][:, 0:M])
                    li.then_inc(sems.h["cv5"], 1)
                    sems.w(e, "mm5", base + 6)
                    for pc in range(2):
                        li = e.scalar_tensor_tensor(mpb[:, pc * 256:(pc + 1) * 256],
                                               pA[4 + pc][:, 0:M], -1.0,
                                               idc[:, 1 * 512 + pc * 256:1 * 512 + (pc + 1) * 256],
                                               ALU.mult, ALU.add)
                    li.then_inc(sems.h["cv5"], 1)
                    sems.w(e, "mm5", base + 8)
                    for pc in range(2):
                        li = e.scalar_tensor_tensor(mpb[:, pc * 256:(pc + 1) * 256],
                                               pA[4 + pc][:, 0:M], -0.25,
                                               idc[:, 2 * 512 + pc * 256:2 * 512 + (pc + 1) * 256],
                                               ALU.mult, ALU.add)
                    li.then_inc(sems.h["cv5"], 1)
                    sems.w(e, "mm5", base + 10)
                    li = None
                    if not last:
                        for pc in range(2):
                            li = e.tensor_copy(zb[:, pc * 256:(pc + 1) * 256], pA[6 + pc][:, 0:M])
                    (li if li is not None else e.engine_nop()).then_inc(sems.h["cv5"], 1)
                    sems.w(e, "mm5", base + 12)
                    for pc in range(2):
                        if last:
                            li = e.tensor_scalar_mul(zstb[:, pc * 256:(pc + 1) * 256],
                                                     pA[0 + pc][:, 0:M], rl3pp[:, pc:pc + 1])
                        else:
                            li = e.tensor_copy(ztb[:, pc * 256:(pc + 1) * 256], pA[0 + pc][:, 0:M])
                    li.then_inc(sems.h["cv5"], 1)
                sems.w(e, "mm5", m50 + 12 * MP_ITERS + 2)
                for mc in range(2):
                    li = e.tensor_copy(t2b[mc][:], pA[2 + mc][:, 0:64])
                li.then_inc(sems.h["cv5"], 1)
                sems.n["cv5"] = v50 + 2 + 6 * MP_ITERS

            @b.tensor
            def _(e):
                sems.w(e, "cv5", v50 + 1)
                for i in range(MP_ITERS):
                    base = m50 + 12 * i
                    vb = v50 + 1 + 6 * i
                    if i > 0:
                        sems.w(e, "cv5", vb)
                    for rc in range(2):
                        for mc in range(2):
                            lw = a2s[:, mc * 256 + rc * 128:mc * 256 + rc * 128 + 128]
                            e.ldweights(lw)
                            li = e.matmul(pA[rc][:, 0:M], lw, zb[:, mc * 256:(mc + 1) * 256],
                                     start=(mc == 0), stop=(mc == 1))
                        li.then_inc(sems.h["mm5"], 1)
                    for ic in range(2):
                        for mc in range(2):
                            lw = zb[:, mc * 256 + ic * 128:mc * 256 + ic * 128 + 128]
                            e.ldweights(lw)
                            li = e.matmul(pA[2 + ic][:, 0:M], lw, a2s[:, mc * 256:(mc + 1) * 256],
                                     start=(mc == 0), stop=(mc == 1))
                        li.then_inc(sems.h["mm5"], 1)
                    sems.w(e, "cv5", vb + 2)
                    for rc in range(2):
                        for ic in range(2):
                            lw = aztb[:, ic * 256 + rc * 128:ic * 256 + rc * 128 + 128]
                            e.ldweights(lw)
                            li = e.matmul(pA[4 + rc][:, 0:M], lw, mpb[:, ic * 256:(ic + 1) * 256],
                                     start=(ic == 0), stop=(ic == 1))
                        li.then_inc(sems.h["mm5"], 1)
                    sems.w(e, "cv5", vb + 3)
                    for rc in range(2):
                        for ic in range(2):
                            lw = aztb[:, ic * 256 + rc * 128:ic * 256 + rc * 128 + 128]
                            e.ldweights(lw)
                            li = e.matmul(pA[4 + rc][:, 0:M], lw, mpb[:, ic * 256:(ic + 1) * 256],
                                     start=(ic == 0), stop=(ic == 1))
                        li.then_inc(sems.h["mm5"], 1)
                    sems.w(e, "cv5", vb + 4)
                    for rc in range(2):
                        for ic in range(2):
                            lw = ztb[:, ic * 256 + rc * 128:ic * 256 + rc * 128 + 128]
                            e.ldweights(lw)
                            li = e.matmul(pA[6 + rc][:, 0:M], lw, mpb[:, ic * 256:(ic + 1) * 256],
                                     start=(ic == 0), stop=(ic == 1))
                        li.then_inc(sems.h["mm5"], 1)
                    for jc in range(2):
                        for ic in range(2):
                            lw = mpb[:, ic * 256 + jc * 128:ic * 256 + jc * 128 + 128]
                            e.ldweights(lw)
                            li = e.matmul(pA[0 + jc][:, 0:M], lw, ztb[:, ic * 256:(ic + 1) * 256],
                                     start=(ic == 0), stop=(ic == 1))
                        li.then_inc(sems.h["mm5"], 1)
                sems.w(e, "cv5", v50 + 1 + 6 * MP_ITERS)
                for mc in range(2):
                    for ic in range(2):
                        lw = zstb[:, ic * 256 + mc * 128:ic * 256 + mc * 128 + 128]
                        e.ldweights(lw)
                        li = e.matmul(pA[2 + mc][:, 0:64], lw, t1id[ic][:, 0:64],
                                 start=(ic == 0), stop=(ic == 1))
                    li.then_inc(sems.h["mm5"], 1)
                sems.n["mm5"] = m50 + 12 * MP_ITERS + 2

        dump_sb(S, f"zstb{h}", zstb[:])
        dump_sb(S, f"t2b{h}", t2b[0][:])
        m60, v60 = sems.n["mm6"], sems.n["cv6"]
        ob = hl * 64
        with nc.Block() as b:
            @b.tensor
            def _(e):
                for wg in range(4):
                    if wg >= 1:
                        sems.w(e, "cv6", v60 + wg)
                    for mc in range(2):
                        e.ldweights(t2b[mc][:])
                        for wi in range(4):
                            w = wg * 4 + wi
                            e.matmul(pA[wi][ob:ob + 64, :], t2b[mc][:],
                                     exp1[mc][:, w * 512:(w + 1) * 512],
                                     start=(mc == 0), stop=(mc == 1))
                    e.ldweights(ones[:, 0:64])
                    for wi in range(4):
                        w = wg * 4 + wi
                        e.matmul(pA[4 + wi][ob:ob + 64, :], ones[:, 0:64],
                                 exp1[0][:, w * 512:(w + 1) * 512], start=True, stop=False)
                        li = e.matmul(pA[4 + wi][ob:ob + 64, :], ones[:, 0:64],
                                 exp1[1][:, w * 512:(w + 1) * 512], start=False, stop=True)
                    li.then_inc(sems.h["mm6"], 1)
                sems.n["mm6"] = m60 + 4

            @b.vector
            def _(e):
                for wg in range(4):
                    sems.w(e, "mm6", m60 + wg + 1)
                    for wi in range(4):
                        w = wg * 4 + wi
                        e.reciprocal(rl1[ob:ob + 64, :], pA[4 + wi][ob:ob + 64, :])
                        e.drain()
                        li = e.tensor_mul(outp[ob:ob + 64, CK // 2 + w * 512:CK // 2 + (w + 1) * 512],
                                     pA[wi][ob:ob + 64, :], rl1[ob:ob + 64, :])
                        e.drain()
                    li.then_inc(sems.h["cv6"], 1)
                sems.n["cv6"] = v60 + 4


def dump_sb(S, name, ap):
    if name not in S.get("dumpreq", ()):
        return
    nc, sems = S["nc"], S["sems"]
    t = nc.dram_tensor("d_" + name, list(ap.shape), ap.dtype, kind="ExternalOutput").ap()
    with nc.Block() as b:
        @b.scalar
        def _(e):
            e.dma_start(out=t[...], in_=ap).then_inc(sems.h["scv"], 16)
            sems.n["scv"] += 16
            sems.w(e, "scv", sems.n["scv"])


def maybe_dump(S, name, src_ap):
    if name not in S["dump"]:
        return
    nc, sems = S["nc"], S["sems"]
    with nc.Block() as b:
        @b.scalar
        def _(e):
            e.dma_start(out=S["dump"][name][...], in_=src_ap).then_inc(sems.h["scv"], 16)
            sems.n["scv"] += 16
            sems.w(e, "scv", sems.n["scv"])


def quant_tiles_of(S, slots):
    tiles = []
    for s_ in slots:
        w1src = S["sw1"] if s_ == 8 else S["ew1"][s_]
        w2src = S["sw2"] if s_ == 8 else S["ew2"][s_]
        tiles += [("w1", s_, r, w1src[r], 1024, 1.0 / 1024) for r in range(16)]
        tiles += [("w2", s_, r, w2src[r], 2048, 1.0 / 2048) for r in range(8)]
    return tiles


def quant_bases(S):
    sems = S["sems"]
    return ([sems.n["rwl0"], sems.n["rwl1"]], sems.n["caz"], sems.n["cvz"],
            [sems.n["scw0"], sems.n["scw1"]])


def quant_commit(S, ntiles):
    sems = S["sems"]
    sems.n["rwl0"] += 16 * (ntiles // 2)
    sems.n["rwl1"] += 16 * (ntiles - ntiles // 2)
    sems.n["scw0"] += 16 * (ntiles // 2)
    sems.n["scw1"] += 16 * (ntiles - ntiles // 2)
    sems.n["caz"] += ntiles
    sems.n["cvz"] += ntiles


def alloc_qbuf(sb, tag):
    return {
        "rawb": [sb(f"rawb{tag}_{i}", [128, FD], F32) for i in range(2)],
        "wqtb": [sb(f"wqtb{tag}_{i}", [128, FD], BF16) for i in range(2)],
        "q2b": sb(f"q2b{tag}", [128, FD], BF16),
        "sums": sb(f"sums{tag}", [128, 4], F32),
    }


def emit_quant_loads(S, e, tiles, rng, bases, qb2):
    sems = S["sems"]
    rawb = qb2["rawb"]
    r0p, c0, v0, s0p = bases
    for t in rng:
        kind, s_, r, src_, width, inv = tiles[t]
        q = t % 2
        sems.w(e, "cvz", max(v0, v0 + t - 1))
        e.dma_start(out=rawb[q][:, 0:width], in_=src_).then_inc(sems.h[f"rwl{q}"], 16)


def emit_quant_scalar(S, e, tiles, rng, bases, qb2):
    sems = S["sems"]
    rawb, wqtb, q2b, sums = qb2["rawb"], qb2["wqtb"], qb2["q2b"], qb2["sums"]
    r0p, c0, v0, s0p = bases
    for t in rng:
        kind, s_, r, src_, width, inv = tiles[t]
        q = t % 2
        sems.w(e, f"rwl{q}", r0p[q] + 16 * (t // 2 + 1))
        sems.w(e, "cvz", v0 + t)
        e.activation(q2b[:, 0:width], rawb[q][:, 0:width], AF.Abs,
                     accum_out=sums[:, 0:1]).then_inc(sems.h["caz"], 1)
        sems.w(e, "cvz", v0 + t + 1)
        dst = S["w1scr"][s_, r] if kind == "w1" else S["w2scr"][s_, r]
        e.dma_start(out=dst, in_=wqtb[q][:, 0:width]).then_inc(sems.h[f"scw{q}"], 16)


def emit_quant_vector(S, e, tiles, rng, bases, qb2):
    sems = S["sems"]
    rawb, wqtb, q2b, sums = qb2["rawb"], qb2["wqtb"], qb2["q2b"], qb2["sums"]
    r0p, c0, v0, s0p = bases
    for t in rng:
        kind, s_, r, src_, width, inv = tiles[t]
        q = t % 2
        sems.w(e, "caz", c0 + t + 1)
        sems.w(e, f"scw{q}", max(s0p[q], s0p[q] + 16 * (t // 2)))
        e.tensor_scalar(sums[:, 1:2], sums[:, 0:1], inv, 1e-5, ALU.mult, ALU.max)
        e.drain()
        e.tensor_scalar_mul(sums[:, 2:3], sums[:, 1:2], 0.5)
        e.tensor_scalar_mul(sums[:, 3:4], sums[:, 1:2], -0.5)
        e.drain()
        e.tensor_scalar(wqtb[q][:, 0:width], rawb[q][:, 0:width],
                        sums[:, 2:3], sums[:, 1:2], ALU.is_ge, ALU.mult)
        e.tensor_scalar(q2b[:, 0:width], rawb[q][:, 0:width],
                        sums[:, 3:4], sums[:, 1:2], ALU.is_le, ALU.mult)
        e.drain()
        e.tensor_sub(wqtb[q][:, 0:width], wqtb[q][:, 0:width],
                     q2b[:, 0:width]).then_inc(sems.h["cvz"], 1)


def emit_conv_store(S, p, outp, cvb, ostg, pA, qslots=()):
    nc, sems = S["nc"], S["sems"]
    m70, v70 = sems.n["mm7"], sems.n["cv7"]
    s70 = [sems.n["sst0"], sems.n["sst1"]]
    qt = quant_tiles_of(S, qslots) if qslots else []
    qb = quant_bases(S)
    nqt = len(qt)
    per_wg = nqt // 4
    qst = ExitStack()
    qb2 = alloc_qbuf(lambda *a, **k: qst.enter_context(nc.sbuf_tensor(*a, **k)),
                     f"c{p}") if qt else None
    with nc.Block() as b:
        if qt:
            @b.sync
            def _(e):
                emit_quant_loads(S, e, qt, range(nqt), qb, qb2)

        @b.tensor
        def _(e):
            for wg in range(4):
                if wg >= 2:
                    sems.w(e, "cv7", v70 + wg - 1)
                for j in range(CK):
                    lw = cvb[:, j * 128:(j + 1) * 128]
                    e.ldweights(lw)
                    for wi in range(4):
                        w = wg * 4 + wi
                        li = e.matmul(pA[(wg % 2) * 4 + wi][:],
                                 lw, outp[:, w * 512 + j:w * 512 + j + 512],
                                 start=(j == 0), stop=(j == CK - 1))
                li.then_inc(sems.h["mm7"], 1)
            sems.n["mm7"] = m70 + 4

        @b.vector
        def _(e):
            for wg in range(4):
                sems.w(e, "mm7", m70 + wg + 1)
                if wg >= 2:
                    sems.w(e, f"sst{wg % 2}", s70[wg % 2] + 16)
                for wi in range(4):
                    w = wg * 4 + wi
                    li = e.scalar_tensor_tensor(ostg[wg % 2][:, wi * 512:(wi + 1) * 512],
                                           pA[(wg % 2) * 4 + wi][:], 0.0,
                                           outp[:, CK // 2 + w * 512:CK // 2 + (w + 1) * 512],
                                           ALU.add, ALU.add)
                li.then_inc(sems.h["cv7"], 1)
                if qt:
                    emit_quant_vector(S, e, qt, range(per_wg * wg, per_wg * (wg + 1)), qb, qb2)
            sems.n["cv7"] = v70 + 4

        @b.scalar
        def _(e):
            for wg in range(4):
                sems.w(e, "cv7", v70 + wg + 1)
                e.dma_start(out=S["outall"][p][:, wg * 2048:(wg + 1) * 2048],
                            in_=ostg[wg % 2][:]).then_inc(sems.h[f"sst{wg % 2}"], 16)
                if qt:
                    emit_quant_scalar(S, e, qt, range(per_wg * wg, per_wg * (wg + 1)), qb, qb2)
            sems.w(e, "sst0", s70[0] + 32)
            sems.w(e, "sst1", s70[1] + 32)
            if qt:
                sems.w(e, "scw0", qb[3][0] + 16 * (nqt // 2))
                sems.w(e, "scw1", qb[3][1] + 16 * (nqt - nqt // 2))
            e.dma_start(out=S["cin"][p][:, :],
                        in_=S["outall"][p][:, DS(S["v_oth"], NT)]).then_inc(sems.h["sci"], 16)
            sems.n["sst0"] = s70[0] + 32
            sems.n["sst1"] = s70[1] + 32
            sems.n["sci"] += 16
    if qt:
        quant_commit(S, nqt)
    qst.close()


def emit_exchange(S, qslots=()):
    nc, sems = S["nc"], S["sems"]
    qt = quant_tiles_of(S, qslots) if qslots else []
    qb = quant_bases(S)
    qst = ExitStack()
    qb2 = alloc_qbuf(lambda *a, **k: qst.enter_context(nc.sbuf_tensor(*a, **k)),
                     "x") if qt else None
    with nc.Block() as b:
        @b.gpsimd
        def _(e):
            sems.w(e, "sci", sems.n["sci"])
            e.collective_compute("AllGather", ALU.bypass,
                                 ins=[S["cin"][:, :, :]], outs=[S["cout"][:, :, :, :]],
                                 replica_groups=[[0, 1], [2, 3], [4, 5], [6, 7]]).then_inc(sems.h["cc"], 1)
            sems.n["cc"] += 1

        if qt:
            @b.sync
            def _(e):
                emit_quant_loads(S, e, qt, range(len(qt)), qb, qb2)

            @b.scalar
            def _(e):
                emit_quant_scalar(S, e, qt, range(len(qt)), qb, qb2)
                sems.w(e, "scw0", qb[3][0] + 16 * (len(qt) // 2))
                sems.w(e, "scw1", qb[3][1] + 16 * (len(qt) - len(qt) // 2))

            @b.vector
            def _(e):
                emit_quant_vector(S, e, qt, range(len(qt)), qb, qb2)
    if qt:
        quant_commit(S, len(qt))
    qst.close()


def emit_fc_ln(S, do_idx=True):
    nc, sems = S["nc"], S["sems"]
    NTT = NT // 128
    RS = AF.Abs_reciprocal_sqrt
    with ExitStack() as pst:
        sb = lambda *a, **k: pst.enter_context(nc.sbuf_tensor(*a, **k))
        ps_ = lambda *a, **k: pst.enter_context(nc.psum_tensor(*a, **k))
        wff = [sb(f"wff{i}", [128, D], F32) for i in range(2)]
        wfb = [sb(f"wfb{i}", [128, D], BF16) for i in range(8)]
        rwf = sb("rwf", [128, NEXP], F32)
        rwb = [sb(f"rwb{i}", [128, NEXP], BF16) for i in range(8)]
        bfc_sb = sb("bfc_sb", [128, D], F32)
        lng_sb = sb("lng_sb", [128, D], F32)
        lnb_sb = sb("lnb_sb", [128, D], F32)
        lt = [sb(f"lt{i}", [128, 128], BF16) for i in range(16)]
        xo = [sb(f"xo{i}", [128, D], F32) for i in range(2)]
        res_t = sb("res_t", [128, D], F32)
        sq_t = sb("sq_t", [128, D], F32)
        r1t = sb("r1t", [128, D], F32)
        hh = sb("hh", [128, D], F32)
        hb = sb("hb", [128, D], BF16)
        hstg = sb("hstg", [128, 8192], BF16, is_transpose_dest=True)
        lsb = sb("lsb", [8, 128], F32)
        logall = sb("logall", [128, NEXP * NTT], F32)
        el = sb("el", [128, NEXP], F32)
        stt_ = sb("stt_", [128, 8], F32)
        gt = sb("gt", [128, NEXP], F32)
        gates8 = sb("gates8", [8, NT], F32)
        gsel_sb = sb("gsel_sb", [128, NTT], F32)
        gselT = sb("gselT", [32, 128], F32)
        pF = [ps_(f"pF{i}", [128, 512], F32) for i in range(8)]

        v0a = sems.n["cv8"]
        q0f = [sems.n["swq0"], sems.n["swq1"]]
        g0f, c0f = sems.n["sgr"], sems.n["scv"]
        with nc.Block() as b:
            @b.sync
            def _(e):
                for dc in range(8):
                    if dc >= 2:
                        sems.w(e, "cv8", v0a + dc - 1)
                    e.dma_start(out=wff[dc % 2][:], in_=S["wfc"][dc]).then_inc(sems.h[f"swq{dc % 2}"], 16)
                e.dma_start(out=bfc_sb[:], in_=S["bfc_rep"][:, :]).then_inc(sems.h["scv"], 16)
                e.dma_start(out=lng_sb[:], in_=S["lng_rep"][:, :]).then_inc(sems.h["scv"], 16)
                e.dma_start(out=lnb_sb[:], in_=S["lnb_rep"][:, :]).then_inc(sems.h["scv"], 16)
                for dc in range(8):
                    sems.w(e, "cv8", v0a + 8 + dc)
                    e.dma_start(out=rwf[:], in_=S["rwT"][dc]).then_inc(sems.h["sgr"], 16)
                sems.n["swq0"] = q0f[0] + 16 * 4
                sems.n["swq1"] = q0f[1] + 16 * 4
                sems.n["scv"] = c0f + 16 * 3
                sems.n["sgr"] = g0f + 16 * 8
                sems.w(e, "scv", sems.n["scv"])

            @b.vector
            def _(e):
                for dc in range(8):
                    sems.w(e, f"swq{dc % 2}", q0f[dc % 2] + 16 * (dc // 2 + 1))
                    e.tensor_copy(wfb[dc][:], wff[dc % 2][:]).then_inc(sems.h["cv8"], 1)
                for dc in range(8):
                    sems.w(e, "sgr", g0f + 16 * (dc + 1))
                    e.tensor_copy(rwb[dc][:], rwf[:]).then_inc(sems.h["cv8"], 1)
                sems.n["cv8"] = v0a + 16

        f0 = [sems.n["sfc0"], sems.n["sfc1"]]
        r10, hf0 = sems.n["sr1"], sems.n["shf"]
        m0, v0, a0, t0 = sems.n["mm8"], sems.n["cv8"], sems.n["ca8"], sems.n["sx8"]
        with nc.Block() as b:
            @b.sync
            def _(e):
                sems.w(e, "cc", sems.n["cc"])
                for tt in range(NTT):
                    if tt >= 2:
                        sems.w(e, "mm8", m0 + 3 * (tt - 1))
                        sems.w(e, "cv8", v0 + 7 * (tt - 1))
                    for kc in range(8):
                        if kc < 4:
                            src = S["outall"][kc][:, DS(S["v_own"] + tt * 128, 128)]
                        else:
                            src = S["cout"][DS(S["v_slot"], 1), kc - 4].rearrange(
                                "s p t -> p (s t)")[:, tt * 128:(tt + 1) * 128]
                        e.dma_start(out=lt[(tt % 2) * 8 + kc][:], in_=src).then_inc(sems.h[f"sfc{tt % 2}"], 16)
                    e.dma_start(out=xo[tt % 2][:], in_=S["x_own"][tt]).then_inc(sems.h[f"sfc{tt % 2}"], 16)
                sems.n["sfc0"] = f0[0] + 16 * 9 * (NTT // 2)
                sems.n["sfc1"] = f0[1] + 16 * 9 * (NTT // 2)

            @b.tensor
            def _(e):
                for tt in range(NTT):
                    sems.w(e, f"sfc{tt % 2}", f0[tt % 2] + 16 * 9 * (tt // 2 + 1))
                    if tt >= 2:
                        sems.w(e, "cv8", v0 + 7 * (tt - 1))
                    for kc in range(8):
                        e.ldweights(lt[(tt % 2) * 8 + kc][:])
                        for half in range(2):
                            li = e.matmul(pF[(tt % 2) * 2 + half][:], lt[(tt % 2) * 8 + kc][:],
                                     wfb[kc][:, half * 512:(half + 1) * 512],
                                     start=(kc == 0), stop=(kc == 7))
                    li.then_inc(sems.h["mm8"], 1)
                    if tt >= 1:
                        sems.w(e, "cv8", v0 + 7 * (tt - 1) + 6)
                    sems.w(e, "sx8", t0 + 16 * 8 * (tt + 1))
                    for dc in range(8):
                        e.ldweights(rwb[dc][:])
                        li = e.matmul(pF[4][0:8, 0:128], rwb[dc][:],
                                 hstg[:, dc * 1024 + (tt % 8) * 128:dc * 1024 + (tt % 8) * 128 + 128],
                                 start=(dc == 0), stop=(dc == 7))
                    li.then_inc(sems.h["mm8"], 1)
                    sems.w(e, "cv8", v0 + 7 * tt + 6)
                    e.transpose(pF[5][:, 0:8], lsb[0:8, :], S["id_f32"][0:8, 0:8]).then_inc(sems.h["mm8"], 1)
                sems.n["mm8"] = m0 + 3 * NTT

            @b.vector
            def _(e):
                for tt in range(NTT):
                    sems.w(e, "mm8", m0 + 3 * tt + 1)
                    if tt >= 1:
                        sems.w(e, "ca8", a0 + 4 * (tt - 1) + 1)
                    e.scalar_tensor_tensor(res_t[:, 0:512], pF[(tt % 2) * 2][:], 0.0,
                                           bfc_sb[:, 0:512], ALU.add, ALU.add)
                    e.scalar_tensor_tensor(res_t[:, 512:1024], pF[(tt % 2) * 2 + 1][:], 0.0,
                                           bfc_sb[:, 512:1024], ALU.add, ALU.add)
                    sems.w(e, f"sfc{tt % 2}", f0[tt % 2] + 16 * 9 * (tt // 2 + 1))
                    e.drain()
                    e.tensor_add(res_t[:], res_t[:], xo[tt % 2][:])
                    e.drain()
                    e.reduce_sum(stt_[:, 0:1], res_t[:], axis=AX.X).then_inc(sems.h["cv8"], 1)
                    sems.w(e, "ca8", a0 + 4 * tt + 1)
                    e.drain()
                    e.tensor_scalar(stt_[:, 2:3], stt_[:, 0:1], 1.0 / D, None, ALU.mult)
                    e.tensor_scalar(stt_[:, 3:4], stt_[:, 1:2], 1.0 / D, None, ALU.mult)
                    e.drain()
                    e.tensor_mul(stt_[:, 6:7], stt_[:, 2:3], stt_[:, 2:3])
                    e.drain()
                    e.tensor_sub(stt_[:, 4:5], stt_[:, 3:4], stt_[:, 6:7])
                    e.drain()
                    e.tensor_scalar(stt_[:, 4:5], stt_[:, 4:5], EPS, None, ALU.add).then_inc(sems.h["cv8"], 1)
                    sems.w(e, "ca8", a0 + 4 * tt + 2)
                    if tt >= 1:
                        sems.w(e, "sr1", r10 + 16 * tt)
                    e.reciprocal(stt_[:, 7:8], stt_[:, 5:6])
                    e.drain()
                    e.tensor_scalar(stt_[:, 6:7], stt_[:, 2:3], stt_[:, 7:8], -1.0,
                                    ALU.mult, ALU.mult)
                    e.drain()
                    e.tensor_scalar(r1t[:], res_t[:], stt_[:, 7:8], stt_[:, 6:7],
                                    ALU.mult, ALU.add).then_inc(sems.h["cv8"], 1)
                    sems.w(e, "ca8", a0 + 4 * tt + 3)
                    e.drain()
                    e.reduce_sum(stt_[:, 0:1], r1t[:], axis=AX.X)
                    e.drain()
                    e.tensor_scalar(stt_[:, 2:3], stt_[:, 0:1], 1.0 / D, None, ALU.mult)
                    e.tensor_scalar(stt_[:, 3:4], stt_[:, 1:2], 1.0 / D, None, ALU.mult)
                    e.drain()
                    e.tensor_mul(stt_[:, 6:7], stt_[:, 2:3], stt_[:, 2:3])
                    e.drain()
                    e.tensor_sub(stt_[:, 4:5], stt_[:, 3:4], stt_[:, 6:7])
                    e.drain()
                    e.tensor_scalar(stt_[:, 4:5], stt_[:, 4:5], EPS, None, ALU.add).then_inc(sems.h["cv8"], 1)
                    sems.w(e, "ca8", a0 + 4 * tt + 4)
                    if tt >= 1:
                        sems.w(e, "sx8", t0 + 16 * 8 * tt)
                        sems.w(e, "shf", hf0 + 16 * tt)
                    e.reciprocal(stt_[:, 7:8], stt_[:, 5:6])
                    e.drain()
                    e.tensor_scalar(stt_[:, 6:7], stt_[:, 2:3], stt_[:, 7:8], -1.0,
                                    ALU.mult, ALU.mult)
                    e.drain()
                    e.tensor_scalar(hh[:], r1t[:], stt_[:, 7:8], stt_[:, 6:7],
                                    ALU.mult, ALU.add)
                    e.drain()
                    e.tensor_mul(hh[:], hh[:], lng_sb[:])
                    e.drain()
                    e.tensor_add(hb[:], hh[:], lnb_sb[:]).then_inc(sems.h["cv8"], 1)
                    sems.w(e, "mm8", m0 + 3 * tt + 2)
                    e.tensor_copy(lsb[0:8, :], pF[4][0:8, 0:128]).then_inc(sems.h["cv8"], 1)
                    sems.w(e, "mm8", m0 + 3 * tt + 3)
                    e.tensor_copy(logall[:, tt * NEXP:(tt + 1) * NEXP],
                                  pF[5][:, 0:8]).then_inc(sems.h["cv8"], 1)
                sems.n["cv8"] = v0 + 7 * NTT

            @b.scalar
            def _(e):
                for tt in range(NTT):
                    sems.w(e, "cv8", v0 + 7 * tt + 1)
                    e.activation(sq_t[:], res_t[:], AF.Square,
                                 accum_out=stt_[:, 1:2]).then_inc(sems.h["ca8"], 1)
                    sems.w(e, "cv8", v0 + 7 * tt + 2)
                    e.activation(stt_[:, 5:6], stt_[:, 4:5], AF.Sqrt).then_inc(sems.h["ca8"], 1)
                    sems.w(e, "cv8", v0 + 7 * tt + 3)
                    e.dma_start(out=S["r1n_hbm"][tt], in_=r1t[:]).then_inc(sems.h["sr1"], 16)
                    e.drain()
                    e.activation(sq_t[:], r1t[:], AF.Square,
                                 accum_out=stt_[:, 1:2]).then_inc(sems.h["ca8"], 1)
                    sems.w(e, "cv8", v0 + 7 * tt + 4)
                    e.activation(stt_[:, 5:6], stt_[:, 4:5], AF.Sqrt).then_inc(sems.h["ca8"], 1)
                    sems.w(e, "cv8", v0 + 7 * tt + 5)
                    e.dma_start(out=S["h_tm"][tt * 128:(tt + 1) * 128, :],
                                in_=hb[:]).then_inc(sems.h["shf"], 16)
                    if tt >= 8:
                        sems.w(e, "mm8", m0 + 3 * (tt - 8) + 2)
                    for dc in range(8):
                        e.dma_start_transpose(
                            out=hstg[:, dc * 1024 + (tt % 8) * 128:dc * 1024 + (tt % 8) * 128 + 128],
                            in_=hb[:, dc * 128:(dc + 1) * 128]).then_inc(sems.h["sx8"], 16)
                sems.n["ca8"] = a0 + 4 * NTT
                sems.n["sx8"] = t0 + 16 * 8 * NTT
                sems.n["sr1"] = r10 + 16 * NTT
                sems.n["shf"] = hf0 + 16 * NTT
                sems.w(e, "sr1", sems.n["sr1"])

        g0, gm0, gv0, gw0 = sems.n["ca9"], sems.n["mm9"], sems.n["cv9"], sems.n["sw9"]
        with nc.Block() as b:
            @b.vector
            def _(e):
                for tt in range(NTT):
                    lg = logall[:, tt * NEXP:(tt + 1) * NEXP]
                    e.reduce_max(stt_[:, 0:1], lg, axis=AX.X)
                    e.drain()
                    e.tensor_scalar_mul(stt_[:, 1:2], stt_[:, 0:1], -1.0).then_inc(sems.h["cv9"], 1)
                    sems.w(e, "ca9", g0 + tt + 1)
                    e.reduce_sum(stt_[:, 2:3], el[:], axis=AX.X)
                    e.drain()
                    e.reciprocal(stt_[:, 3:4], stt_[:, 2:3])
                    e.tensor_scalar(gt[:], lg, stt_[:, 0:1], None, ALU.is_ge)
                    e.drain()
                    e.tensor_copy(gsel_sb[:, tt:tt + 1], stt_[:, 3:4])
                    e.tensor_scalar_mul(gt[:], gt[:], stt_[:, 3:4]).then_inc(sems.h["cv9"], 1)
                    sems.w(e, "mm9", gm0 + tt + 1)
                    e.tensor_copy(gates8[0:8, tt * 128:(tt + 1) * 128],
                                  pF[6][0:8, 0:128]).then_inc(sems.h["cv9"], 1)
                sems.w(e, "mm9", gm0 + NTT + 1)
                e.tensor_copy(gselT[:], pF[7][0:32, 256:384]).then_inc(sems.h["cv9"], 1)
                sems.n["cv9"] = gv0 + 3 * NTT + 1

            @b.scalar
            def _(e):
                for tt in range(NTT):
                    sems.w(e, "cv9", gv0 + 3 * tt + 1)
                    e.activation(el[:], logall[:, tt * NEXP:(tt + 1) * NEXP],
                                 AF.Exp, bias=stt_[:, 1:2]).then_inc(sems.h["ca9"], 1)
                sems.n["ca9"] = g0 + NTT

            @b.tensor
            def _(e):
                for tt in range(NTT):
                    sems.w(e, "cv9", gv0 + 3 * tt + 2)
                    e.transpose(pF[6][0:8, 0:128], gt[:, 0:NEXP],
                                S["id_f32"][:, :]).then_inc(sems.h["mm9"], 1)
                sems.w(e, "cv9", gv0 + 3 * NTT)
                e.transpose(pF[7][0:32, 256:384], gsel_sb[:],
                            S["id_f32"][:, :]).then_inc(sems.h["mm9"], 1)
                sems.n["mm9"] = gm0 + NTT + 1

        with nc.Block() as b:
            @b.scalar
            def _(e):
                e.dma_start(out=S["gsel_hbm"][0:NT].rearrange("(c p) o -> c (p o)", p=128),
                            in_=gselT[:]).then_inc(sems.h["sw9"], 16)
                sems.n["sw9"] = gw0 + 16
                sems.w(e, "sw9", sems.n["sw9"])

        if do_idx:
            emit_index(S, sb, gates8, pF)


def emit_index(S, sb, gates8, pF):
    nc, sems = S["nc"], S["sems"]
    ones = S["ones_f32"]
    idf = S["id_f32"]
    gtT = sb("gtT", [128, 256], F32)
    mskall = sb("mskall", [128, 256], F32)
    poss = sb("poss", [128, 256], F32)
    wc_sb = sb("wc_sb", [128, 32], F32)
    totT = sb("totT", [32, 1], F32)
    offsT = sb("offsT", [32, 1], F32)
    offs_row = sb("offs_row", [1, 32], F32)
    ncol = sb("ncol", [1, 8], F32)
    n8 = sb("n8", [8, 1], F32)
    kcmp = sb("kcmp", [8, 8], F32)
    wcnt8 = sb("wcnt8", [8, 1], F32)
    baseW8 = sb("baseW8", [8, 1], F32)
    baseT8 = sb("baseT8", [8, 1], F32)
    base_row = sb("base_row", [1, 8], F32)
    baseb = sb("baseb", [128, 8], F32)
    cmp816 = sb("cmp816", [8, 16], F32)
    weT = sb("weT", [16, 8], F32)
    weS = sb("weS", [16, 1], F32)
    wofs_f = sb("wofs_f", [16, 1], F32)
    wofs_rowf = sb("wofs_rowf", [1, 16], F32)
    tmpA = sb("tmpA", [128, 32], F32)
    s_all = sb("s_all", [128, 32], F32)
    s_i32 = sb("s_i32", [128, 32], I32)

    steps = []
    T = lambda fn: steps.append(("T", fn))
    V = lambda fn: steps.append(("V", fn))

    for r in range(4):
        def t0(e, r=r):
            for b8 in range(8):
                c = r * 8 + b8
                li = e.transpose(pF[b8][:, 0:8], gates8[0:8, c * 128:(c + 1) * 128],
                                 idf[0:8, 0:8])
            return li
        T(t0)
        def v0(e, r=r):
            for b8 in range(8):
                c = r * 8 + b8
                li = e.tensor_copy(gtT[:, c * 8:(c + 1) * 8], pF[b8][:, 0:8])
            return li
        V(v0)

    gtT3 = gtT[:].rearrange("p (c e) -> p e c", e=8)
    for ex in range(NEXP):
        V(lambda e, ex=ex: e.tensor_scalar(
            mskall[:, ex * 32:(ex + 1) * 32],
            gtT3[:, ex:ex + 1].rearrange("p e c -> p (e c)"), 1e-20, None, ALU.is_ge))
        T(lambda e, ex=ex: e.matmul(pF[6][:, 0:32], S["cum128_sb"][:],
                                    mskall[:, ex * 32:(ex + 1) * 32], start=True, stop=True))
        V(lambda e: e.tensor_copy(wc_sb[:], pF[6][:, 0:32]))
        T(lambda e: e.transpose(pF[7][0:32, 0:128], wc_sb[:], idf[:, :]))
        V(lambda e: e.tensor_copy(totT[:], pF[7][0:32, 127:128]))
        T(lambda e: e.matmul(pF[6][0:32, 32:33], S["cum32s_sb"][:], totT[:],
                             start=True, stop=True))
        V(lambda e: e.tensor_copy(offsT[:], pF[6][0:32, 32:33]))
        T(lambda e: e.matmul(pF[7][0:1, 256:257], totT[:], ones[0:32, 0:1],
                             start=True, stop=True))
        V(lambda e, ex=ex: e.tensor_copy(ncol[0:1, ex:ex + 1], pF[7][0:1, 256:257]))
        T(lambda e: e.transpose(pF[7][0:1, 128:160], offsT[:], idf[0:32, 0:32]))
        V(lambda e: e.tensor_copy(offs_row[:], pF[7][0:1, 128:160]))
        T(lambda e: e.matmul(pF[6][:, 64:96], ones[0:1, :], offs_row[:],
                             start=True, stop=True))
        V(lambda e, ex=ex: e.tensor_add(poss[:, ex * 32:(ex + 1) * 32],
                                        wc_sb[:], pF[6][:, 64:96]))

    T(lambda e: e.transpose(pF[7][0:8, 160:161], ncol[0:1, 0:8], idf[0:1, 0:1]))
    V(lambda e: e.tensor_copy(n8[:], pF[7][0:8, 160:161]))
    V(lambda e: e.tensor_scalar(kcmp[:], S["k512_sb"][:], n8[:, 0:1], None, ALU.is_lt))
    V(lambda e: e.reduce_sum(wcnt8[:], kcmp[:], axis=AX.X))
    T(lambda e: e.matmul(pF[6][0:8, 96:97], S["cum8s_sb"][:], wcnt8[:],
                         start=True, stop=True))
    V(lambda e: e.tensor_copy(baseW8[:], pF[6][0:8, 96:97]))
    V(lambda e: e.tensor_scalar_mul(baseT8[:], baseW8[:], 512.0))
    T(lambda e: e.transpose(pF[7][0:1, 168:176], baseT8[:], idf[0:8, 0:8]))
    V(lambda e: e.tensor_copy(base_row[:], pF[7][0:1, 168:176]))
    T(lambda e: e.matmul(pF[6][:, 104:112], ones[0:1, :], base_row[:],
                         start=True, stop=True))
    V(lambda e: e.tensor_copy(baseb[:], pF[6][:, 104:112]))
    V(lambda e: e.tensor_scalar(cmp816[:], S["iota16_sb"][:], baseW8[:, 0:1],
                                None, ALU.is_ge))
    T(lambda e: e.transpose(pF[7][0:16, 176:184], cmp816[:], idf[0:8, 0:8]))
    V(lambda e: e.tensor_copy(weT[:], pF[7][0:16, 176:184]))
    V(lambda e: e.reduce_sum(weS[:], weT[:], axis=AX.X))
    V(lambda e: e.tensor_scalar(wofs_f[:], weS[:], -1.0, None, ALU.add))
    T(lambda e: e.transpose(pF[6][0:1, 112:128], wofs_f[:], idf[0:16, 0:16]))
    V(lambda e: e.tensor_copy(wofs_rowf[:], pF[6][0:1, 112:128]))
    V(lambda e: e.tensor_copy(S["wofs"][:], wofs_rowf[:]))

    for ex in range(NEXP):
        V(lambda e, ex=ex: e.tensor_scalar(tmpA[:], poss[:, ex * 32:(ex + 1) * 32],
                                           baseb[:, ex:ex + 1], -1.0, ALU.add, ALU.add))
        V(lambda e, ex=ex: e.tensor_mul(tmpA[:], tmpA[:], mskall[:, ex * 32:(ex + 1) * 32]))
        if ex == 0:
            V(lambda e: e.tensor_copy(s_all[:], tmpA[:]))
        else:
            V(lambda e: e.tensor_add(s_all[:], s_all[:], tmpA[:]))
    V(lambda e: e.tensor_copy(s_i32[:], s_all[:]))

    m0, v0c = sems.n["mm9"], sems.n["cv9"]
    nT = sum(1 for k, _ in steps if k == "T")
    nV = len(steps) - nT
    with nc.Block() as b:
        @b.tensor
        def _(e):
            mi, vi = 0, 0
            for kind, fn in steps:
                if kind == "T":
                    sems.w(e, "cv9", v0c + vi)
                    fn(e).then_inc(sems.h["mm9"], 1)
                    mi += 1
                else:
                    vi += 1
            sems.n["mm9"] = m0 + nT

        @b.vector
        def _(e):
            mi, vi = 0, 0
            for kind, fn in steps:
                if kind == "T":
                    mi += 1
                else:
                    sems.w(e, "mm9", m0 + mi)
                    e.drain()
                    fn(e).then_inc(sems.h["cv9"], 1)
                    vi += 1
            sems.n["cv9"] = v0c + nV

        @b.gpsimd
        def _(e):
            sems.w(e, "cv9", v0c + nV)
            sems.w(e, "sx", S["sx_total"])
            for c in range(32):
                e.indirect_dma_start(out=S["gidx_hbm"][:, :],
                                     out_offset=IOA(s_i32[:, c:c + 1], 0),
                                     in_=S["iot_sb"][:, c:c + 1],
                                     in_offset=None).then_inc(sems.h["sgx"], 16)
            sems.n["sgx"] += 16 * 32


def emit_windows(S):
    nc, sems = S["nc"], S["sems"]
    NW, NDYN = 24, 16
    with ExitStack() as pst:
        sb = lambda *a, **k: pst.enter_context(nc.sbuf_tensor(*a, **k))
        ps_ = lambda *a, **k: pst.enter_context(nc.psum_tensor(*a, **k))
        w1b = [sb(f"w1b{i}", [128, 8 * FD], BF16, is_transpose_dest=True) for i in range(2)]
        w2b = [sb(f"w2b{i}", [128, 16 * D], BF16, is_transpose_dest=True) for i in range(2)]
        hgt = [sb(f"hgt{i}", [128, D], BF16) for i in range(2)]
        hgf = [sb(f"hgf{i}", [128, 8 * 512], BF16, is_transpose_dest=True) for i in range(2)]
        h1g = sb("h1g", [128, 16 * 512], BF16)
        eo = [sb(f"eo{i}", [128, D], F32) for i in range(2)]
        g_col = sb("g_col", [128, 8], F32)
        gidxb = [sb(f"gidxb{i}", [128, 4], I32) for i in range(2)]
        pW = [ps_(f"pW{i}", [128, 512], F32) for i in range(8)]

        m0, a0, v0 = sems.n["mm9"], sems.n["ca9"], sems.n["cv9"]
        s10 = [sems.n["sw10"], sems.n["sw11"]]
        s20 = [sems.n["sw20"], sems.n["sw21"]]
        gi0 = [sems.n["sgi0"], sems.n["sgi1"]]
        gh0 = [sems.n["sgh0"], sems.n["sgh1"]]
        gg0 = [sems.n["sgg0"], sems.n["sgg1"]]
        xh0 = [sems.n["sxh0"], sems.n["sxh1"]]
        g0p = [sems.n["gda"], sems.n["gdb"]]
        shf_t, sw9_t, sgx_t = sems.n["shf"], sems.n["sw9"], sems.n["sgx"]
        scw_t = [sems.n["scw0"], sems.n["scw1"]]

        def wloads(e, w):
            par = w % 2
            if w >= 2:
                sems.w(e, "mm9", m0 + 20 * (w - 1))
            if w >= 8:
                src1 = S["w1scr"][DS(S["v_we"][w - 8], 1)].rearrange("s c p d -> (s c p) d")
                src2 = S["w2scr"][DS(S["v_we"][w - 8], 1)].rearrange("s c p f -> (s c p) f")
            else:
                src1 = S["w1scr"][8].rearrange("c p d -> (c p) d")
                src2 = S["w2scr"][8].rearrange("c p f -> (c p) f")
            for dc in range(8):
                e.dma_start_transpose(out=w1b[par][:, dc * 2048:(dc + 1) * 2048],
                                      in_=src1[:, dc * 128:(dc + 1) * 128]).then_inc(
                                          sems.h[f"sw1{par}"], 16)
            for fc in range(16):
                e.dma_start_transpose(out=w2b[par][:, fc * 1024:(fc + 1) * 1024],
                                      in_=src2[:, fc * 128:(fc + 1) * 128]).then_inc(
                                          sems.h[f"sw2{par}"], 16)

        def gload(e, w):
            par = w % 2
            if w >= 10:
                sems.w(e, "gda", g0p[0] + 16 * 2 * (w - 1))
                sems.w(e, "gdb", g0p[1] + 16 * 2 * (w - 1))
            e.dma_start(out=gidxb[par][:],
                        in_=S["gidx_hbm"][(w - 8) * 512:(w - 7) * 512].rearrange(
                            "(p c) o -> p (c o)", c=4)).then_inc(sems.h[f"sgi{par}"], 16)

        with nc.Block() as b:
            @b.sync
            def _(e):
                sems.w(e, "shf", shf_t)
                sems.w(e, "sgx", sgx_t)
                sems.w(e, "scw0", scw_t[0])
                sems.w(e, "scw1", scw_t[1])
                wloads(e, 0)
                for w in range(NW):
                    if w >= 2:
                        sems.w(e, "mm9", m0 + 20 * (w - 2) + 16)
                    for c in range(4):
                        g = 4 * w + c
                        sems.w(e, f"sgh{g % 2}", gh0[g % 2] + 16 * (g // 2 + 1))
                        for dc in range(8):
                            e.dma_start_transpose(
                                out=hgf[w % 2][:, dc * 512 + c * 128:dc * 512 + (c + 1) * 128],
                                in_=hgt[g % 2][:, dc * 128:(dc + 1) * 128]).then_inc(
                                    sems.h[f"sxh{g % 2}"], 16)
                    if w + 1 < NW:
                        if w + 1 >= 8:
                            gload(e, w + 1)
                        wloads(e, w + 1)
                sems.n["sw10"] = s10[0] + 16 * 8 * 12
                sems.n["sw11"] = s10[1] + 16 * 8 * 12
                sems.n["sw20"] = s20[0] + 16 * 16 * 12
                sems.n["sw21"] = s20[1] + 16 * 16 * 12
                sems.n["sgi0"] = gi0[0] + 16 * 8
                sems.n["sgi1"] = gi0[1] + 16 * 8
                sems.n["sxh0"] = xh0[0] + 128 * 48
                sems.n["sxh1"] = xh0[1] + 128 * 48

            @b.gpsimd
            def _(e):
                sems.w(e, "sw9", sw9_t)
                sems.w(e, "shf", shf_t)
                def tload(g):
                    if g >= 2:
                        sems.w(e, f"sxh{g % 2}", xh0[g % 2] + 128 * (g // 2))
                    e.dma_start(out=hgt[g % 2][:],
                                in_=S["h_tm"][g * 128:(g + 1) * 128, :]).then_inc(
                                    sems.h[f"sgh{g % 2}"], 16)
                for c in range(4):
                    tload(c)
                first_add = True
                for w in range(NW):
                    nw = w + 1
                    if nw < 8:
                        for c in range(4):
                            tload(4 * nw + c)
                    elif 8 <= nw < NW:
                        par = nw % 2
                        sems.w(e, f"sgi{par}", gi0[par] + 16 * ((nw - 8) // 2 + 1))
                        if nw >= 10:
                            sems.w(e, "cv9", v0 + 4 * (nw - 1))
                        for c in range(4):
                            g = 4 * nw + c
                            sems.w(e, f"sxh{g % 2}", xh0[g % 2] + 128 * (g // 2))
                            e.indirect_dma_start(
                                out=hgt[g % 2][:], out_offset=None, in_=S["h_tm"][:, :],
                                in_offset=IOA(gidxb[par][:, c:c + 1], 0)).then_inc(
                                    sems.h[f"sgh{g % 2}"], 16)
                            e.indirect_dma_start(
                                out=g_col[:, par * 4 + c:par * 4 + c + 1], out_offset=None,
                                in_=S["gsel_hbm"][:, :],
                                in_offset=IOA(gidxb[par][:, c:c + 1], 0)).then_inc(
                                    sems.h[f"sgg{par}"], 16)
                    for tt in range(4):
                        T = 4 * w + tt
                        sems.w(e, "cv9", v0 + T + 1)
                        gsem = f"gd{'ab'[T % 2]}"
                        if w >= 8:
                            if first_add:
                                sems.w(e, "gda", g0p[0] + 16 * 16)
                                sems.w(e, "gdb", g0p[1] + 16 * 16)
                                first_add = False
                            e.indirect_dma_start(
                                out=S["acc_tm"][:, :],
                                out_offset=IOA(gidxb[w % 2][:, tt:tt + 1], 0),
                                in_=eo[tt % 2][:], in_offset=None,
                                compute_op=ALU.add).then_inc(sems.h[gsem], 16)
                        else:
                            e.dma_start(out=S["acc_tm"][w * 512 + tt * 128:
                                                        w * 512 + (tt + 1) * 128, :],
                                        in_=eo[tt % 2][:]).then_inc(sems.h[gsem], 16)
                sems.n["gda"] = g0p[0] + 16 * 2 * NW
                sems.n["gdb"] = g0p[1] + 16 * 2 * NW
                sems.n["sgh0"] = gh0[0] + 16 * 48
                sems.n["sgh1"] = gh0[1] + 16 * 48
                sems.n["sgg0"] = gg0[0] + 16 * 4 * 8
                sems.n["sgg1"] = gg0[1] + 16 * 4 * 8

            @b.tensor
            def _(e):
                for w in range(NW):
                    par = w % 2
                    sems.w(e, "sxh0", xh0[0] + 256 * (w + 1))
                    sems.w(e, "sxh1", xh0[1] + 256 * (w + 1))
                    sems.w(e, f"sw1{par}", s10[par] + 16 * 8 * (w // 2 + 1))
                    for fc in range(16):
                        F = 16 * w + fc
                        if F >= 4:
                            sems.w(e, "ca9", a0 + F - 3)
                        for dc in range(8):
                            lw = w1b[par][:, dc * 2048 + fc * 128:dc * 2048 + (fc + 1) * 128]
                            e.ldweights(lw)
                            li = e.matmul(pW[fc % 4][:], lw, hgf[par][:, dc * 512:(dc + 1) * 512],
                                          start=(dc == 0), stop=(dc == 7))
                        li.then_inc(sems.h["mm9"], 1)
                    sems.w(e, f"sw2{par}", s20[par] + 16 * 16 * (w // 2 + 1))
                    sems.w(e, "ca9", a0 + 16 * (w + 1))
                    for tt in range(4):
                        T = 4 * w + tt
                        if T >= 2:
                            sems.w(e, "cv9", v0 + T - 1)
                        for fc in range(16):
                            lw = h1g[:, fc * 512 + tt * 128:fc * 512 + (tt + 1) * 128]
                            e.ldweights(lw)
                            for dh in range(2):
                                li = e.matmul(pW[4 + (tt % 2) * 2 + dh][:], lw,
                                              w2b[par][:, fc * 1024 + dh * 512:
                                                       fc * 1024 + (dh + 1) * 512],
                                              start=(fc == 0), stop=(fc == 15))
                        li.then_inc(sems.h["mm9"], 1)
                sems.n["mm9"] = m0 + 20 * NW

            @b.scalar
            def _(e):
                for w in range(NW):
                    for fc in range(16):
                        sems.w(e, "mm9", m0 + 20 * w + fc + 1)
                        e.activation(h1g[:, fc * 512:(fc + 1) * 512],
                                     pW[fc % 4][:], AF.Silu).then_inc(sems.h["ca9"], 1)
                sems.n["ca9"] = a0 + 16 * NW

            @b.vector
            def _(e):
                for w in range(NW):
                    par = w % 2
                    if w >= 8:
                        sems.w(e, f"sgg{par}", gg0[par] + 64 * ((w - 8) // 2 + 1))
                    for tt in range(4):
                        T = 4 * w + tt
                        sems.w(e, "mm9", m0 + 20 * w + 16 + tt + 1)
                        if T >= 2:
                            sems.w(e, f"gd{'ab'[T % 2]}", g0p[T % 2] + 16 * (T // 2))
                        for dh in range(2):
                            bk = 4 + (tt % 2) * 2 + dh
                            if w >= 8:
                                li = e.tensor_scalar(eo[tt % 2][:, dh * 512:(dh + 1) * 512],
                                                     pW[bk][:],
                                                     g_col[:, par * 4 + tt:par * 4 + tt + 1],
                                                     None, ALU.mult)
                            else:
                                li = e.tensor_copy(eo[tt % 2][:, dh * 512:(dh + 1) * 512],
                                                   pW[bk][:])
                        li.then_inc(sems.h["cv9"], 1)
                sems.n["cv9"] = v0 + 4 * NW
def emit_tail(S):
    nc, sems = S["nc"], S["sems"]
    NTT = NT // 128
    RS = AF.Abs_reciprocal_sqrt
    with ExitStack() as pst:
        sb = lambda *a, **k: pst.enter_context(nc.sbuf_tensor(*a, **k))
        ps_ = lambda *a, **k: pst.enter_context(nc.psum_tensor(*a, **k))
        ac8 = [sb(f"ac8_{i}", [128, 1024], F32) for i in range(2)]
        r1b = [sb(f"r1b_{i}", [128, D], F32) for i in range(2)]
        res2 = sb("res2", [128, D], F32)
        sq2 = sb("sq2", [128, D], F32)
        yt = [sb(f"yt{i}", [128, D], F32) for i in range(2)]
        st2 = sb("st2", [128, 8], F32)
        a0p = [sems.n["sac0"], sems.n["sac1"]]
        y0p = [sems.n["sy0"], sems.n["sy1"]]
        v0, a0 = sems.n["cv9"], sems.n["ca9"]
        gall = [sems.n["gda"], sems.n["gdb"]]
        with nc.Block() as b:
            @b.sync
            def _(e):
                sems.w(e, "gda", gall[0])
                sems.w(e, "gdb", gall[1])
                for tt in range(NTT):
                    if tt >= 2:
                        sems.w(e, "cv9", v0 + 3 * (tt - 1))
                    e.dma_start(out=ac8[tt % 2][:],
                                in_=S["acc_tm"][tt * 128:(tt + 1) * 128, :]).then_inc(sems.h[f"sac{tt % 2}"], 16)
                    e.dma_start(out=r1b[tt % 2][:], in_=S["r1n_hbm"][tt]).then_inc(sems.h[f"sac{tt % 2}"], 16)
                sems.n["sac0"] = a0p[0] + 16 * 2 * (NTT // 2)
                sems.n["sac1"] = a0p[1] + 16 * 2 * (NTT // 2)

            @b.vector
            def _(e):
                for tt in range(NTT):
                    sems.w(e, f"sac{tt % 2}", a0p[tt % 2] + 16 * 2 * (tt // 2 + 1))
                    if tt >= 1:
                        sems.w(e, "ca9", a0 + 2 * (tt - 1) + 1)
                    e.drain()
                    e.tensor_add(res2[:], ac8[tt % 2][:], r1b[tt % 2][:])
                    e.drain()
                    e.reduce_sum(st2[:, 0:1], res2[:], axis=AX.X).then_inc(sems.h["cv9"], 1)
                    sems.w(e, "ca9", a0 + 2 * tt + 1)
                    e.drain()
                    e.tensor_scalar(st2[:, 2:3], st2[:, 0:1], 1.0 / D, None, ALU.mult)
                    e.tensor_scalar(st2[:, 3:4], st2[:, 1:2], 1.0 / D, None, ALU.mult)
                    e.drain()
                    e.tensor_mul(st2[:, 6:7], st2[:, 2:3], st2[:, 2:3])
                    e.drain()
                    e.tensor_sub(st2[:, 4:5], st2[:, 3:4], st2[:, 6:7])
                    e.drain()
                    e.tensor_scalar(st2[:, 4:5], st2[:, 4:5], EPS, None, ALU.add).then_inc(sems.h["cv9"], 1)
                    sems.w(e, "ca9", a0 + 2 * tt + 2)
                    if tt >= 2:
                        sems.w(e, f"sy{tt % 2}", y0p[tt % 2] + 16 * (tt // 2))
                    e.reciprocal(st2[:, 7:8], st2[:, 5:6])
                    e.drain()
                    e.tensor_scalar(st2[:, 6:7], st2[:, 2:3], st2[:, 7:8], -1.0,
                                    ALU.mult, ALU.mult)
                    e.drain()
                    e.tensor_scalar(yt[tt % 2][:], res2[:], st2[:, 7:8], st2[:, 6:7],
                                    ALU.mult, ALU.add).then_inc(sems.h["cv9"], 1)
                sems.n["cv9"] = v0 + 3 * NTT

            @b.scalar
            def _(e):
                for tt in range(NTT):
                    sems.w(e, "cv9", v0 + 3 * tt + 1)
                    e.activation(sq2[:], res2[:], AF.Square,
                                 accum_out=st2[:, 1:2]).then_inc(sems.h["ca9"], 1)
                    sems.w(e, "cv9", v0 + 3 * tt + 2)
                    e.activation(st2[:, 5:6], st2[:, 4:5], AF.Sqrt).then_inc(sems.h["ca9"], 1)
                    sems.w(e, "cv9", v0 + 3 * tt + 3)
                    e.dma_start(out=S["y"][tt], in_=yt[tt % 2][:]).then_inc(sems.h[f"sy{tt % 2}"], 16)
                sems.n["ca9"] = a0 + 2 * NTT
                sems.n["sy0"] = y0p[0] + 16 * (NTT // 2)
                sems.n["sy1"] = y0p[1] + 16 * (NTT // 2)

        with nc.Block() as b:
            @b.sync
            def _(e):
                sems.w(e, "sy0", sems.n["sy0"])
                sems.w(e, "sy1", sems.n["sy1"])


def emit_dummy_tail(S):
    nc, sems = S["nc"], S["sems"]
    with nc.Block() as b:
        @b.sync
        def _(e):
            sems.w(e, "sr1", sems.n["sr1"])
            for tt in range(NT // 128):
                e.dma_start(out=S["y"][tt], in_=S["r1n_hbm"][tt]).then_inc(sems.h["sy0"], 16)
            sems.n["sy0"] += 16 * (NT // 128)
            sems.w(e, "sy0", sems.n["sy0"])


def host_in_maps(inputs):
    x = np.asarray(inputs["L0_attn_x"])
    w_qkv = np.asarray(inputs["L0_attn_w_qkv"])
    w_conv = np.asarray(inputs["L0_attn_w_conv"])
    w_fc = np.asarray(inputs["L0_attn_w_fc"])
    b_fc = np.asarray(inputs["L0_attn_b_fc"])
    sw1 = np.asarray(inputs["L0_ffn_shared_w1"])
    sw2 = np.asarray(inputs["L0_ffn_shared_w2"])
    ew1 = np.asarray(inputs["L0_ffn_expert_w1"])
    ew2 = np.asarray(inputs["L0_ffn_expert_w2"])
    rw = np.asarray(inputs["L0_ffn_router_w"])
    lng = np.asarray(inputs["L0_ffn_ln_gamma"])
    lnb = np.asarray(inputs["L0_ffn_ln_beta"])

    idbf = np.eye(128, dtype=BF)
    idf32 = np.eye(128, dtype=np.float32)
    onesbf = np.ones((128, 128), dtype=BF)
    cum128 = np.triu(np.ones((128, 128), np.float32))
    cum32s = np.triu(np.ones((32, 32), np.float32), 1)
    cum8s = np.triu(np.ones((8, 8), np.float32), 1)
    k512 = np.ascontiguousarray(np.broadcast_to(
        512.0 * np.arange(8, dtype=np.float32), (8, 8)))
    iota16 = np.ascontiguousarray(np.broadcast_to(
        np.arange(16, dtype=np.float32), (8, 16)))
    iota32 = (np.arange(128, dtype=np.int32)[:, None]
              + 128 * np.arange(32, dtype=np.int32)[None, :])
    ii = np.arange(16 * 512, dtype=np.int32)
    gpad = (NT + ((ii % 4) % 2) * 128 + (ii % 512) // 4).astype(np.int32)[:, None]
    zb16 = np.zeros((128, D), BF)
    zf32 = np.zeros((128, 1), np.float32)
    identc = np.zeros((128, 6 * M), dtype=np.float32)
    for ci, coef in enumerate((7.0, 15.0, 3.25)):
        for pc in range(2):
            blk = np.zeros((128, M), dtype=np.float32)
            blk[np.arange(128), pc * 128 + np.arange(128)] = coef
            identc[:, ci * 2 * M + pc * M:(ci * 2 + pc + 1) * M] = blk
    identc = identc.astype(BF)

    maps = []
    for c in range(8):
        b, j = c // 2, c % 2
        hoff = 8 * j
        poff = 8 * (1 - j)
        xT = np.ascontiguousarray(x[b].T).reshape(8, 128, N)
        x_own = np.ascontiguousarray(x[b, j * NT:(j + 1) * NT]).reshape(NT // 128, 128, D)
        wq = np.concatenate([w_qkv[:, hoff * DH:hoff * DH + 512],
                             w_qkv[:, D + hoff * DH:D + hoff * DH + 512],
                             w_qkv[:, 2 * D + hoff * DH:2 * D + hoff * DH + 512]],
                            axis=1)
        wq = np.ascontiguousarray(wq).reshape(8, 128, 3 * NH * DH)
        wf = np.concatenate([w_fc[hoff * DH:hoff * DH + 512],
                             w_fc[poff * DH:poff * DH + 512]], axis=0)
        wf = np.ascontiguousarray(wf).reshape(8, 128, D)
        convd = np.zeros((4, CK, 128, 128), dtype=np.float32)
        for p in range(4):
            for jj in range(CK):
                dv = np.concatenate([np.full(64, w_conv[hoff + 2 * p, 0, jj, 0], np.float32),
                                     np.full(64, w_conv[hoff + 2 * p + 1, 0, jj, 0], np.float32)])
                convd[p, jj, np.arange(128), np.arange(128)] = dv
        meta = np.array([[(1 - j) * NT, 1 - j, j * NT, 0, 0, 0, 0, 0]], dtype=np.int32)
        maps.append({
            "xT": xT, "x_own": x_own, "wqkv": wq, "wfc": wf,
            "bfc_rep": np.ascontiguousarray(np.broadcast_to(b_fc, (128, D))),
            "lng_rep": np.ascontiguousarray(np.broadcast_to(lng, (128, D))),
            "lnb_rep": np.ascontiguousarray(np.broadcast_to(lnb, (128, D))),
            "convd": convd, "identc": identc, "idbf": idbf, "idf32": idf32,
            "onesbf": onesbf, "onesf": np.ones((128, 128), dtype=np.float32), "meta": meta,
            "cum128": cum128, "cum32s": cum32s, "cum8s": cum8s, "k512": k512,
            "iota16": iota16, "iota32": iota32, "gpad": gpad, "zb16": np.asarray(zb16),
            "zf32": zf32,
            "rwT": np.ascontiguousarray(rw.T).reshape(8, 128, NEXP),
            "sw1": sw1.reshape(FD // 128, 128, D),
            "sw2": sw2.reshape(D // 128, 128, FD),
            "ew1": ew1.reshape(NEXP, FD // 128, 128, D),
            "ew2": ew2.reshape(NEXP, D // 128, 128, FD),
        })
    return maps


def kernel(**inputs):
    from concourse.bass_utils import run_bass_kernel_spmd
    nc = build(n_pass=4, do_ffn=True)
    nc.finalize()
    maps = host_in_maps(inputs)
    res = run_bass_kernel_spmd(nc, maps, core_ids=list(range(8)))
    out = np.zeros((4, N, D), dtype=np.float32)
    for c in range(8):
        b, j = c // 2, c % 2
        out[b, j * NT:(j + 1) * NT] = res.results[c]["y"].reshape(NT, D)
    return out
